# version 06
# baseline (speedup 1.0000x reference)
import numpy as np
from contextlib import ExitStack
import concourse.bass as bass
import concourse.mybir as mybir
from concourse.bass_utils import run_bass_kernel_spmd

F32 = mybir.dt.float32
BF16 = mybir.dt.bfloat16
AF = mybir.ActivationFunctionType
ALU = mybir.AluOpType
AX = mybir.AxisListType.X

NCORES = 8
NB = 2
S = 2048
D = 2048
NH, DKH = 16, 128
NE0, F0 = 16, 768
NE1, I1 = 8, 1024
NSEG, SEG = 4, 512
QT = 1024
ISQ = 0.08838834764831845
NEGBIG = -1e30
EPS = 1e-5
DBG_SKIP_EXPERTS = False
DBG_SKIP_ATTNOUT = False
DC = D // 128
TC = S // 128
FC0 = F0 // 128
IC1 = I1 // 128

WNAMES = [
    "L0_attn_Wq", "L0_attn_bq", "L0_attn_Wk", "L0_attn_bk", "L0_attn_Wv",
    "L0_attn_bv", "L0_attn_Wo", "L0_attn_bo", "L0_ffn_router_w",
    "L0_ffn_gate_w", "L0_ffn_up_w", "L0_ffn_down_w",
    "L1_attn_Wq", "L1_attn_Wk", "L1_attn_Wv", "L1_attn_Wo", "L1_attn_betas",
    "L1_ffn_gate_w", "L1_ffn_gate_b", "L1_ffn_W1", "L1_ffn_W2", "L1_ffn_W3",
    "L1_ffn_Ws1", "L1_ffn_Ws2", "L1_ffn_Ws3",
]
L0_WNAMES = [n for n in WNAMES if n.startswith("L0")]
L1_WNAMES = [n for n in WNAMES if n.startswith("L1")]

INPUT_SHAPES = {
    "L0_attn_Wq": [D, D], "L0_attn_bq": [D], "L0_attn_Wk": [D, D],
    "L0_attn_bk": [D], "L0_attn_Wv": [D, D], "L0_attn_bv": [D],
    "L0_attn_Wo": [D, D], "L0_attn_bo": [D],
    "L0_ffn_router_w": [D, NE0],
    "L0_ffn_gate_w": [NE0, D, F0], "L0_ffn_up_w": [NE0, D, F0],
    "L0_ffn_down_w": [NE0, F0, D],
    "L1_attn_Wq": [D, D], "L1_attn_Wk": [D, D], "L1_attn_Wv": [D, D],
    "L1_attn_Wo": [D, D], "L1_attn_betas": [1, NH, 1, DKH],
    "L1_ffn_gate_w": [NE1, D], "L1_ffn_gate_b": [NE1],
    "L1_ffn_W1": [NE1, I1, D], "L1_ffn_W2": [NE1, D, I1],
    "L1_ffn_W3": [NE1, I1, D],
    "L1_ffn_Ws1": [I1, D], "L1_ffn_Ws2": [D, I1], "L1_ffn_Ws3": [I1, D],
}



_NAMEC = [0]


def nm(base):
    _NAMEC[0] += 1
    return f"{base}_{_NAMEC[0]}"

class Sem:
    __slots__ = ("h", "n")

    def __init__(self, es, nc, name):
        self.h = es.enter_context(nc.semaphore(name))
        self.n = 0

    def inc(self, ins, k=1):
        ins.then_inc(self.h, k)
        self.n += k

    def bump(self, eng, k=1):
        eng.sem_inc(self.h, k)
        self.n += k

    def flush(self, eng, ins):
        self.inc(ins)
        eng.wait_ge(self.h, self.n)
        return ins

    def wait(self, eng, val=None):
        eng.wait_ge(self.h, self.n if val is None else val)


class Ctx:
    pass


def chunk_view(ap, width):
    return ap.rearrange("p (c j) -> p c j", j=width)


def _mk_ctx(nc, es, layers):
    ctx = Ctx()
    ctx.nc = nc
    ctx.es = es
    ctx.layers = layers
    for s in ("ld", "st", "mm", "ve", "se", "gp"):
        setattr(ctx, s, Sem(es, nc, "sem_" + s))
    return ctx


def _consts(ctx):
    nc, es = ctx.nc, ctx.es
    ctx.ident_b = es.enter_context(nc.sbuf_tensor(nm("ident_b"), [128, 128], BF16))
    ctx.ident_f = es.enter_context(nc.sbuf_tensor(nm("ident_f"), [128, 128], F32))
    ctx.mask = es.enter_context(nc.sbuf_tensor(nm("maskstrip"), [128, 1280], F32))
    ctx.sel = es.enter_context(nc.sbuf_tensor(nm("sel"), [16, 2048], F32))
    ctx.ones_col_b = es.enter_context(nc.sbuf_tensor(nm("ones_col_b"), [128, 1], BF16))
    ctx.ones_row_b = es.enter_context(nc.sbuf_tensor(nm("ones_row_b"), [1, 128], BF16))
    ctx.ones_row_f = es.enter_context(nc.sbuf_tensor(nm("ones_row_f"), [1, 128], F32))
    ctx.one11_f = es.enter_context(nc.sbuf_tensor(nm("one11_f"), [1, 1], F32))
    ctx.eps_col = es.enter_context(nc.sbuf_tensor(nm("eps_col"), [128, 1], F32))
    with ctx.nc.Block() as block:
        @block.gpsimd
        def _(g):
            for t in (ctx.ident_b, ctx.ident_f):
                g.memset(t[:], 0.0)
                g.affine_select(out=t[:], in_=t[:],
                                compare_op=ALU.not_equal, fill=1.0, base=0,
                                pattern=[[-1, 128]], channel_multiplier=1)
            g.memset(ctx.mask[:], 0.0)
            g.affine_select(out=ctx.mask[:], in_=ctx.mask[:],
                            compare_op=ALU.is_ge, fill=NEGBIG, base=-384,
                            pattern=[[1, 1280]], channel_multiplier=-1)
            g.memset(ctx.sel[:], 0.0)
            g.affine_select(
                out=chunk_view(ctx.sel[:], 128),
                in_=chunk_view(ctx.sel[:], 128),
                compare_op=ALU.not_equal, fill=1.0, base=0,
                pattern=[[-1, 16], [0, 128]], channel_multiplier=1)
            g.memset(ctx.ones_col_b[:], 1.0)
            g.memset(ctx.ones_row_b[:], 1.0)
            g.memset(ctx.ones_row_f[:], 1.0)
            g.memset(ctx.one11_f[:], 1.0)
            ctx.gp.inc(g.memset(ctx.eps_col[:], EPS))



def _prep_cast(ctx, src2d, dst, R, C):
    nc = ctx.nc
    ld, ve, st = ctx.ld, ctx.ve, ctx.st
    ph = min(128, R)
    nch = max(1, R // 128)
    kmax = max(1, min(nch, 16384 // C))
    groups = [(g, min(kmax, nch - g)) for g in range(0, nch, kmax)]
    src = src2d.rearrange("(c p) n -> p c n", p=ph)
    dstv = dst[:, :].rearrange("(c p) n -> p c n", p=ph)
    with (
        nc.sbuf_tensor(nm("pc_f0"), [ph, kmax * C], F32) as sf0,
        nc.sbuf_tensor(nm("pc_f1"), [ph, kmax * C], F32) as sf1,
        nc.sbuf_tensor(nm("pc_b0"), [ph, kmax * C], BF16) as sb0,
        nc.sbuf_tensor(nm("pc_b1"), [ph, kmax * C], BF16) as sb1,
        nc.Block() as block,
    ):
        sfs, sbs = [sf0, sf1], [sb0, sb1]
        st_marks = {}
        ld_marks = {}
        ve_marks = {}
        for gi, (g0, k) in enumerate(groups):
            @block.sync
            def _(sync, gi=gi, g0=g0, k=k):
                if gi >= 2:
                    sync.wait_ge(st.h, st_marks[gi - 2])
                sync.dma_start(out=chunk_view(sfs[gi % 2][:, :k * C], C),
                               in_=src[:, g0:g0 + k, :]).then_inc(ld.h, 16)
                ld.n += 16
                ld_marks[gi] = ld.n

            @block.vector
            def _(v, gi=gi, g0=g0, k=k):
                v.wait_ge(ld.h, ld_marks[gi])
                ve.inc(v.tensor_copy(sbs[gi % 2][:, :k * C],
                                     sfs[gi % 2][:, :k * C]))
                ve_marks[gi] = ve.n

            @block.sync
            def _(sync, gi=gi, g0=g0, k=k):
                sync.wait_ge(ve.h, ve_marks[gi])
                sync.dma_start(out=dstv[:, g0:g0 + k, :],
                               in_=chunk_view(sbs[gi % 2][:, :k * C], C)
                               ).then_inc(st.h, 16)
                st.n += 16
                st_marks[gi] = st.n

        @block.sync
        def _(sync):
            st.wait(sync)


def _prep_transpose(ctx, src2d, dst, R, C):
    nc = ctx.nc
    ld, ve, se, mm, st = ctx.ld, ctx.ve, ctx.se, ctx.mm, ctx.st
    ph = min(128, R)
    nch = max(1, R // 128)
    ncc = C // 128
    kmax = max(1, min(nch, 16384 // C))
    groups = [(g, min(kmax, nch - g)) for g in range(0, nch, kmax)]
    src = src2d.rearrange("(c p) n -> p c n", p=ph)
    with (
        nc.sbuf_tensor(nm("pt_f"), [ph, kmax * C], F32) as sf,
        nc.sbuf_tensor(nm("pt_b"), [ph, kmax * C], BF16) as sb,
        nc.sbuf_tensor(nm("pt_o"), [128, ncc * R], BF16) as so,
        nc.psum_tensor(nm("pt_p0"), [128, 1024], BF16) as p0,
        nc.psum_tensor(nm("pt_p1"), [128, 1024], BF16) as p1,
        nc.Block() as block,
    ):
        pb = [p0, p1]
        fill = [0]
        evac_done = [0, 0]
        ov = so[:].rearrange("p (c r) -> p c r", r=R)
        for gi, (g0, k) in enumerate(groups):
            @block.sync
            def _(sync, g0=g0, k=k):
                sync.dma_start(out=chunk_view(sf[:, :k * C], C),
                               in_=src[:, g0:g0 + k, :]).then_inc(ld.h, 16)
                ld.n += 16

            @block.vector
            def _(v, g0=g0, k=k):
                ld.wait(v)
                if gi > 0:
                    v.wait_ge(mm.h, mm.n)
                ve.inc(v.tensor_copy(sb[:, :k * C], sf[:, :k * C]))

            @block.sync
            def _(sync):
                ve.wait(sync)

            for rr in range(g0, g0 + k):
                for cc0 in range(0, ncc, 8):
                    w = min(8, ncc - cc0)
                    fi = fill[0] % 2

                    @block.tensor
                    def _(t, rr=rr, cc0=cc0, w=w, fi=fi, g0=g0):
                        ve.wait(t)
                        t.wait_ge(se.h, evac_done[fi])
                        ins = None
                        for j in range(w):
                            cc = cc0 + j
                            ins = t.transpose(
                                pb[fi][:, j * 128:(j + 1) * 128],
                                sb[:, (rr - g0) * C + cc * 128:
                                       (rr - g0) * C + (cc + 1) * 128],
                                ctx.ident_b[:ph, :])
                        mm.inc(ins)

                    @block.scalar
                    def _(s, rr=rr, cc0=cc0, w=w, fi=fi):
                        mm.wait(s)
                        se.inc(s.copy(
                            ov[:, cc0:cc0 + w, rr * ph:(rr + 1) * ph],
                            chunk_view(pb[fi][:, :w * 128], 128)[:, :, :ph]))
                        evac_done[fi] = se.n
                    fill[0] += 1

        @block.sync
        def _(sync):
            se.wait(sync)
            sync.dma_start(out=dst[:, :].rearrange("(c p) r -> p c r", p=128),
                           in_=ov).then_inc(st.h, 16)
            st.n += 16
            st.wait(sync)


def _prep_weights(ctx, inp):
    nc = ctx.nc
    w = {}

    def scr(name, shape):
        w[name] = nc.dram_tensor("scr_" + name, shape, BF16, kind="Internal")
        return w[name]

    if 0 in ctx.layers:
        _prep_transpose(ctx, inp["L0_attn_Wq"][:, :], scr("wqT0", [D, D]), D, D)
        _prep_transpose(ctx, inp["L0_attn_Wk"][:, :], scr("wkT0", [D, D]), D, D)
        _prep_transpose(ctx, inp["L0_attn_Wv"][:, :], scr("wvT0", [D, D]), D, D)
        _prep_transpose(ctx, inp["L0_attn_Wo"][:, :], scr("woT0", [D, D]), D, D)
        _prep_cast(ctx, inp["L0_ffn_router_w"][:, :], scr("rt0", [D, NE0]), D, NE0)
        scr("gw0", [NE0, D, F0]); scr("uw0", [NE0, D, F0]); scr("dw0", [NE0, F0, D])
        for e in range(NE0):
            _prep_cast(ctx, inp["L0_ffn_gate_w"][e, :, :], w["gw0"][e, :, :], D, F0)
            _prep_cast(ctx, inp["L0_ffn_up_w"][e, :, :], w["uw0"][e, :, :], D, F0)
            _prep_cast(ctx, inp["L0_ffn_down_w"][e, :, :], w["dw0"][e, :, :], F0, D)
    if 1 in ctx.layers:
        _prep_cast(ctx, inp["L1_attn_Wq"][:, :], scr("wq1", [D, D]), D, D)
        _prep_cast(ctx, inp["L1_attn_Wk"][:, :], scr("wk1", [D, D]), D, D)
        _prep_cast(ctx, inp["L1_attn_Wv"][:, :], scr("wv1", [D, D]), D, D)
        _prep_cast(ctx, inp["L1_attn_Wo"][:, :], scr("wo1", [D, D]), D, D)
        _prep_transpose(ctx, inp["L1_ffn_gate_w"][:, :], scr("gT1", [D, NE1]), NE1, D)
        scr("w1T", [NE1, D, I1]); scr("w3T", [NE1, D, I1]); scr("w2T", [NE1, I1, D])
        for e in range(NE1):
            _prep_transpose(ctx, inp["L1_ffn_W1"][e, :, :], w["w1T"][e, :, :], I1, D)
            _prep_transpose(ctx, inp["L1_ffn_W3"][e, :, :], w["w3T"][e, :, :], I1, D)
            _prep_transpose(ctx, inp["L1_ffn_W2"][e, :, :], w["w2T"][e, :, :], D, I1)
        _prep_transpose(ctx, inp["L1_ffn_Ws1"][:, :], scr("ws1T", [D, I1]), I1, D)
        _prep_transpose(ctx, inp["L1_ffn_Ws3"][:, :], scr("ws3T", [D, I1]), I1, D)
        _prep_transpose(ctx, inp["L1_ffn_Ws2"][:, :], scr("ws2T", [I1, D]), D, I1)
    ctx.w = w



def _ln(ctx, b, x_src):
    nc = ctx.nc
    ld, ve, se, mm, st = ctx.ld, ctx.ve, ctx.se, ctx.mm, ctx.st
    with (
        nc.sbuf_tensor(nm("ln_x0"), [128, D], F32) as sx0,
        nc.sbuf_tensor(nm("ln_x1"), [128, D], F32) as sx1,
        nc.sbuf_tensor(nm("ln_sq"), [128, D], F32) as ssq,
        nc.sbuf_tensor(nm("ln_st"), [128, 16], F32) as sst,
        nc.sbuf_tensor(nm("ln_xn"), [128, D], BF16) as sxn,
        nc.psum_tensor(nm("ln_p0"), [128, 1024], BF16) as p0,
        nc.psum_tensor(nm("ln_p1"), [128, 1024], BF16) as p1,
        nc.Block() as block,
    ):
        sx = [sx0, sx1]
        pb = [p0, p1]
        xn_done = [0, 0]
        bank_free = [0, 0]
        xnv = chunk_view(ctx.xnT[:], S)
        for tc in range(TC):
            @block.sync
            def _(sync, tc=tc):
                sync.wait_ge(ve.h, xn_done[tc % 2])
                sync.dma_start(out=sx[tc % 2][:],
                               in_=x_src[b, tc * 128:(tc + 1) * 128, :]
                               ).then_inc(ld.h, 16)
                ld.n += 16

            o = 8 * (tc % 2)

            @block.scalar
            def _(s, tc=tc, o=o):
                ld.wait(s)
                se.inc(s.activation(ssq[:], sx[tc % 2][:], AF.Square,
                                    accum_out=sst[:, o + 1:o + 2]))

            @block.vector
            def _(v, tc=tc, o=o):
                ld.wait(v)
                ve.flush(v, v.reduce_sum(sst[:, o:o + 1], sx[tc % 2][:], axis=AX))
                se.wait(v)
                ve.flush(v, v.tensor_scalar_mul(sst[:, o + 2:o + 3],
                                                sst[:, o:o + 1], 1.0 / D))
                v.tensor_scalar_mul(sst[:, o + 3:o + 4], sst[:, o + 1:o + 2],
                                    1.0 / D)
                ve.flush(v, v.tensor_mul(sst[:, o + 4:o + 5], sst[:, o + 2:o + 3],
                                         sst[:, o + 2:o + 3]))
                ve.inc(v.tensor_sub(sst[:, o + 5:o + 6], sst[:, o + 3:o + 4],
                                    sst[:, o + 4:o + 5]))

            @block.scalar
            def _(s, tc=tc, o=o):
                ve.wait(s)
                se.inc(s.activation(sst[:, o + 6:o + 7], sst[:, o + 5:o + 6],
                                    AF.Sqrt, bias=ctx.eps_col[:]))

            @block.vector
            def _(v, tc=tc, o=o):
                se.wait(v)
                ve.flush(v, v.reciprocal(sst[:, o + 7:o + 8], sst[:, o + 6:o + 7]))
                ve.inc(v.tensor_scalar(sxn[:], sx[tc % 2][:], sst[:, o + 2:o + 3],
                                       sst[:, o + 7:o + 8], op0=ALU.subtract,
                                       op1=ALU.mult))
                xn_done[tc % 2] = ve.n

            for fi in range(2):
                @block.tensor
                def _(t, tc=tc, fi=fi):
                    ve.wait(t)
                    t.wait_ge(ve.h, bank_free[fi])
                    ins = None
                    for j in range(8):
                        cc = fi * 8 + j
                        ins = t.transpose(pb[fi][:, j * 128:(j + 1) * 128],
                                          sxn[:, cc * 128:(cc + 1) * 128],
                                          ctx.ident_b[:])
                    mm.inc(ins)

                @block.vector
                def _(v, tc=tc, fi=fi):
                    mm.wait(v)
                    ve.inc(v.tensor_copy(
                        xnv[:, fi * 8:fi * 8 + 8, tc * 128:(tc + 1) * 128],
                        chunk_view(pb[fi][:], 128)))
                    bank_free[fi] = ve.n

        @block.sync
        def _(sync):
            ve.wait(sync)
            sync.dma_start(
                out=ctx.xnt_scr[b, :, :].rearrange("(c p) t -> p c t", p=128),
                in_=xnv).then_inc(st.h, 16)
            st.n += 16
            st.wait(sync)



def _router_l0(ctx, b):
    nc = ctx.nc
    ld, ve, se, mm, st = ctx.ld, ctx.ve, ctx.se, ctx.mm, ctx.st
    E = NE0
    with (
        nc.sbuf_tensor(nm("rt_w"), [128, DC * E], BF16) as srt,
        nc.sbuf_tensor(nm("rt_lg"), [128, E], F32) as slg,
        nc.sbuf_tensor(nm("rt_t"), [128, 5 * E], F32) as stp,
        nc.sbuf_tensor(nm("rt_s"), [128, 4], F32) as sst,
        nc.sbuf_tensor(nm("rt_cT"), [16, S], F32) as scT,
        nc.psum_tensor(nm("rt_pl"), [128, E], F32) as plg,
        nc.psum_tensor(nm("rt_pt"), [16, 128], F32) as ptr,
        nc.Block() as block,
    ):
        @block.sync
        def _(sync):
            sync.dma_start(out=chunk_view(srt[:], E),
                           in_=ctx.w["rt0"][:, :].rearrange("(c p) n -> p c n", p=128)
                           ).then_inc(ld.h, 16)
            ld.n += 16

        plg_free = [0]
        ptr_free = [0]
        for tc in range(TC):
            @block.tensor
            def _(t, tc=tc):
                ld.wait(t)
                t.wait_ge(ve.h, plg_free[0])
                ins = None
                for kc in range(DC):
                    ins = t.matmul(plg[:],
                                   ctx.xnT[:, kc * S + tc * 128:
                                           kc * S + (tc + 1) * 128],
                                   srt[:, kc * E:(kc + 1) * E],
                                   start=(kc == 0), stop=(kc == DC - 1))
                mm.inc(ins)

            @block.vector
            def _(v, tc=tc):
                mm.wait(v)
                ve.flush(v, v.tensor_copy(slg[:], plg[:]))
                plg_free[0] = ve.n
                ve.flush(v, v.reduce_max(sst[:, 0:1], slg[:], axis=AX,
                                         negate=True))
                ve.flush(v, v.tensor_scalar(stp[:, 3 * E:4 * E], slg[:],
                                            sst[:, 0:1], 0.0,
                                            op0=ALU.add, op1=ALU.is_ge))
                ve.flush(v, v.scalar_tensor_tensor(stp[:, 4 * E:5 * E],
                                                   stp[:, 3 * E:4 * E], NEGBIG,
                                                   slg[:],
                                                   op0=ALU.mult, op1=ALU.add))
                ve.inc(v.reduce_max(sst[:, 1:2], stp[:, 4 * E:5 * E], axis=AX))

            @block.scalar
            def _(s, tc=tc):
                ve.wait(s)
                se.inc(s.activation(stp[:, 0:E], slg[:], AF.Exp, bias=sst[:, 0:1]))

            @block.vector
            def _(v, tc=tc):
                ve.wait(v)
                se.wait(v)
                ve.flush(v, v.tensor_scalar(stp[:, E:2 * E], slg[:], sst[:, 1:2],
                                            None, op0=ALU.is_ge))
                ve.flush(v, v.tensor_mul(stp[:, 2 * E:3 * E], stp[:, 0:E],
                                         stp[:, E:2 * E]))
                ve.flush(v, v.reduce_sum(sst[:, 2:3], stp[:, 2 * E:3 * E],
                                         axis=AX))
                ve.flush(v, v.reciprocal(sst[:, 3:4], sst[:, 2:3]))
                ve.inc(v.tensor_scalar_mul(stp[:, 2 * E:3 * E],
                                           stp[:, 2 * E:3 * E], sst[:, 3:4]))

            @block.tensor
            def _(t, tc=tc):
                ve.wait(t)
                t.wait_ge(ve.h, ptr_free[0])
                mm.inc(t.transpose(ptr[:], stp[:, 2 * E:3 * E], ctx.ident_f[:]))

            @block.vector
            def _(v, tc=tc):
                mm.wait(v)
                ve.inc(v.tensor_copy(scT[:, tc * 128:(tc + 1) * 128], ptr[:]))
                ptr_free[0] = ve.n

        @block.sync
        def _(sync):
            ve.wait(sync)
            sync.dma_start(out=ctx.cmb_scr[:, b * S:(b + 1) * S], in_=scT[:]
                           ).then_inc(st.h, 16)
            st.n += 16
            st.wait(sync)


def _gate_l1(ctx, b, inp):
    nc = ctx.nc
    ld, ve, se, mm, st = ctx.ld, ctx.ve, ctx.se, ctx.mm, ctx.st
    E = NE1
    with (
        nc.sbuf_tensor(nm("gt_w"), [128, DC * E], BF16) as sgw,
        nc.sbuf_tensor(nm("gt_br"), [1, E], F32) as sgbr,
        nc.sbuf_tensor(nm("gt_b"), [128, E], F32) as sgb,
        nc.sbuf_tensor(nm("gt_lg"), [128, E], F32) as slg,
        nc.sbuf_tensor(nm("gt_t"), [128, 6 * E], F32) as stp,
        nc.sbuf_tensor(nm("gt_s"), [128, 4], F32) as sst,
        nc.sbuf_tensor(nm("gt_cT"), [E, S], F32) as scT,
        nc.psum_tensor(nm("gt_pl"), [128, E], F32) as plg,
        nc.psum_tensor(nm("gt_pt"), [16, 128], F32) as ptr,
        nc.Block() as block,
    ):
        @block.sync
        def _(sync):
            sync.dma_start(out=chunk_view(sgw[:], E),
                           in_=ctx.w["gT1"][:, :].rearrange("(c p) n -> p c n", p=128)
                           ).then_inc(ld.h, 16)
            sync.dma_start(out=sgbr[:], in_=inp["L1_ffn_gate_b"][None, :]
                           ).then_inc(ld.h, 16)
            ld.n += 32

        @block.tensor
        def _(t):
            ld.wait(t)
            mm.inc(t.matmul(plg[:], ctx.ones_row_f[:], sgbr[:],
                            start=True, stop=True))

        @block.vector
        def _(v):
            mm.wait(v)
            ve.inc(v.tensor_copy(sgb[:], plg[:]))

        plg_free = [ve.n + 0]
        ptr_free = [0]
        plg_free[0] = ve.n
        for tc in range(TC):
            @block.tensor
            def _(t, tc=tc):
                t.wait_ge(ve.h, plg_free[0])
                ins = None
                for kc in range(DC):
                    ins = t.matmul(plg[:],
                                   ctx.xnT[:, kc * S + tc * 128:
                                           kc * S + (tc + 1) * 128],
                                   sgw[:, kc * E:(kc + 1) * E],
                                   start=(kc == 0), stop=(kc == DC - 1))
                mm.inc(ins)

            @block.vector
            def _(v, tc=tc):
                mm.wait(v)
                ve.flush(v, v.tensor_copy(slg[:], plg[:]))
                plg_free[0] = ve.n
                ve.inc(v.reduce_max(sst[:, 0:1], slg[:], axis=AX, negate=True))

            @block.scalar
            def _(s, tc=tc):
                ve.wait(s)
                se.inc(s.activation(stp[:, 0:E], slg[:], AF.Exp, bias=sst[:, 0:1],
                                    accum_out=sst[:, 1:2]))

            @block.vector
            def _(v, tc=tc):
                se.wait(v)
                ve.flush(v, v.reciprocal(sst[:, 2:3], sst[:, 1:2]))
                ve.flush(v, v.tensor_scalar_mul(stp[:, 0:E], stp[:, 0:E],
                                                sst[:, 2:3]))
                ve.flush(v, v.tensor_add(stp[:, E:2 * E], stp[:, 0:E], sgb[:]))
                ve.flush(v, v.reduce_max(sst[:, 3:4], stp[:, E:2 * E], axis=AX))
                ve.flush(v, v.tensor_scalar(stp[:, 3 * E:4 * E],
                                            stp[:, E:2 * E],
                                            sst[:, 3:4], None, op0=ALU.is_ge))
                ve.flush(v, v.scalar_tensor_tensor(stp[:, 4 * E:5 * E],
                                                   stp[:, 3 * E:4 * E], NEGBIG,
                                                   stp[:, E:2 * E],
                                                   op0=ALU.mult, op1=ALU.add))
                ve.flush(v, v.reduce_max(sst[:, 3:4], stp[:, 4 * E:5 * E],
                                         axis=AX))
                ve.flush(v, v.tensor_scalar(stp[:, 5 * E:6 * E],
                                            stp[:, E:2 * E],
                                            sst[:, 3:4], None, op0=ALU.is_ge))
                ve.inc(v.tensor_mul(stp[:, 2 * E:3 * E], stp[:, 0:E],
                                    stp[:, 5 * E:6 * E]))

            @block.tensor
            def _(t, tc=tc):
                ve.wait(t)
                t.wait_ge(ve.h, ptr_free[0])
                mm.inc(t.transpose(ptr[:E, :], stp[:, 2 * E:3 * E],
                                   ctx.ident_f[:]))

            @block.vector
            def _(v, tc=tc):
                mm.wait(v)
                ve.inc(v.tensor_copy(scT[:, tc * 128:(tc + 1) * 128], ptr[:E, :]))
                ptr_free[0] = ve.n

        @block.sync
        def _(sync):
            ve.wait(sync)
            sync.dma_start(out=ctx.dw1_scr[:, b * S:(b + 1) * S], in_=scT[:]
                           ).then_inc(st.h, 16)
            st.n += 16
            st.wait(sync)



def _bias_cols(ctx, block, rows_sb, cols_sb, p_b, n):
    mm, ve = ctx.mm, ctx.ve

    @block.tensor
    def _(t):
        ctx.ld.wait(t)
        ins = None
        for k in range(n):
            for c in range(DC):
                ins = t.matmul(p_b[:, k * DC + c:k * DC + c + 1],
                               rows_sb[0:1, k * D + c * 128:k * D + (c + 1) * 128],
                               ctx.one11_f[:], start=True, stop=True)
        mm.inc(ins)

    @block.vector
    def _(v):
        mm.wait(v)
        ve.inc(v.tensor_copy(cols_sb[:], p_b[:, :n * DC]))


def _attn_l0(ctx, b, inp):
    nc = ctx.nc
    ld, ve, se, mm, st = ctx.ld, ctx.ve, ctx.se, ctx.mm, ctx.st
    _es = ExitStack()
    with _es:
        sbrow = _es.enter_context(nc.sbuf_tensor(nm("a0_brow"), [1, 3 * D], F32))
        sbvr = _es.enter_context(nc.sbuf_tensor(nm("a0_bvrow"), [1, D], BF16))
        sbc = _es.enter_context(nc.sbuf_tensor(nm("a0_bcols"), [128, 3 * DC], F32))
        swq = _es.enter_context(nc.sbuf_tensor(nm("a0_wq"), [128, 2 * DC * 128], BF16))
        swk = _es.enter_context(nc.sbuf_tensor(nm("a0_wk"), [128, 2 * DC * 128], BF16))
        swv = _es.enter_context(nc.sbuf_tensor(nm("a0_wv"), [128, 2 * DC * 128], BF16))
        sqT = _es.enter_context(nc.sbuf_tensor(nm("a0_qT"), [128, S], BF16))
        skT = _es.enter_context(nc.sbuf_tensor(nm("a0_kT"), [128, S], BF16))
        svn = _es.enter_context(nc.sbuf_tensor(nm("a0_vn"), [128, S], BF16))
        sexp = _es.enter_context(nc.sbuf_tensor(nm("a0_exp"), [128, 16 * 512], BF16))
        srcp = _es.enter_context(nc.sbuf_tensor(nm("a0_rcp"), [1, S], F32))
        sbcn = _es.enter_context(nc.sbuf_tensor(nm("a0_bcn"), [128, 512], F32))
        sctxT = _es.enter_context(nc.sbuf_tensor(nm("a0_ctxT"), [128, S], BF16))
        pbig = _es.enter_context(nc.psum_tensor(nm("a0_pbig"), [128, 2048], F32))
        ps0 = _es.enter_context(nc.psum_tensor(nm("a0_ps0"), [128, 512], F32))
        ps1 = _es.enter_context(nc.psum_tensor(nm("a0_ps1"), [128, 512], F32))
        pdn = _es.enter_context(nc.psum_tensor(nm("a0_pdn"), [1, 512], F32))
        pctx = _es.enter_context(nc.psum_tensor(nm("a0_pctx"), [128, 512], F32))
        ps = [ps0, ps1]
        HW = DC * 128

        def load_head(sync, h):
            for sb_, wname in ((swq, "wqT0"), (swk, "wkT0"), (swv, "wvT0")):
                sync.dma_start(
                    out=chunk_view(sb_[:, (h % 2) * HW:(h % 2 + 1) * HW], 128),
                    in_=ctx.w[wname][:, h * 128:(h + 1) * 128]
                    .rearrange("(c p) j -> p c j", p=128)).then_inc(ld.h, 16)
                ld.n += 16
            return ld.n

        head_mark = {}
        with nc.Block() as block:
            @block.sync
            def _(sync):
                for k, bn in enumerate(["L0_attn_bq", "L0_attn_bk", "L0_attn_bv"]):
                    sync.dma_start(out=sbrow[0:1, k * D:(k + 1) * D],
                                   in_=inp[bn][None, :]).then_inc(ld.h, 16)
                    ld.n += 16
                head_mark[0] = load_head(sync, 0)
            _bias_cols(ctx, block, sbrow, sbc, pbig, 3)

            @block.scalar
            def _(s):
                ld.wait(s)
                se.inc(s.copy(sbvr[:], sbrow[0:1, 2 * D:3 * D]))

        ctxT_stored = [st.n]
        for h in range(NH):
            with nc.Block() as block:
                po = (h % 2) * HW

                @block.sync
                def _(sync, h=h):
                    if h + 1 < NH:
                        head_mark[h + 1] = load_head(sync, h + 1)

                for rnd, (wsb, dst, bcol) in enumerate(
                        ((swq, sqT, 0), (swk, skT, 1))):
                    @block.tensor
                    def _(t, rnd=rnd, wsb=wsb, h=h, po=po):
                        t.wait_ge(ld.h, head_mark[h])
                        ve.wait(t)
                        for nt in range(4):
                            ins = None
                            for kc in range(DC):
                                ins = t.matmul(
                                    pbig[:, nt * 512:(nt + 1) * 512],
                                    wsb[:, po + kc * 128:po + (kc + 1) * 128],
                                    ctx.xnT[:, kc * S + nt * 512:
                                            kc * S + (nt + 1) * 512],
                                    start=(kc == 0), stop=(kc == DC - 1))
                            mm.inc(ins)

                    @block.vector
                    def _(v, rnd=rnd, dst=dst, bcol=bcol, h=h):
                        for nt in range(4):
                            v.wait_ge(mm.h, mm.n - 3 + nt)
                            vi = v.tensor_scalar(
                                dst[:, nt * 512:(nt + 1) * 512],
                                pbig[:, nt * 512:(nt + 1) * 512],
                                sbc[:, bcol * DC + h:bcol * DC + h + 1], None,
                                op0=ALU.add)
                            ve.inc(vi)

                @block.tensor
                def _(t, h=h):
                    ve.wait(t)
                    se.wait(t)
                    for grp in range(4):
                        ins = None
                        for tc in range(grp * 4, grp * 4 + 4):
                            for kc in range(DC):
                                ins = t.matmul(
                                    pbig[:, tc * 128:(tc + 1) * 128],
                                    ctx.xnT[:, kc * S + tc * 128:
                                            kc * S + (tc + 1) * 128],
                                    swv[:, po + kc * 128:po + (kc + 1) * 128],
                                    start=(kc == 0), stop=False)
                            ins = t.matmul(
                                pbig[:, tc * 128:(tc + 1) * 128],
                                ctx.ones_row_b[:],
                                sbvr[0:1, h * 128:(h + 1) * 128],
                                start=False, stop=True)
                        mm.inc(ins)

                @block.vector
                def _(v, h=h):
                    for grp in range(4):
                        v.wait_ge(mm.h, mm.n - 3 + grp)
                        ve.inc(v.tensor_copy(svn[:, grp * 512:(grp + 1) * 512],
                                             pbig[:, grp * 512:(grp + 1) * 512]))

                exp_done = {}
                bank_free = [ve.n, ve.n]
                bank_free_sem = [None, None]
                pctx_free = [0]
                pdn_free = [0]
                for j in range(4):
                    imax = 4 * j + 4
                    se0 = se.n

                    @block.tensor
                    def _(t, j=j, imax=imax, se0=se0):
                        ve.wait(t)
                        for i in range(imax):
                            if i >= 2:
                                t.wait_ge(se.h, se0 + i - 1)
                            if i < 2 and j > 0:
                                t.wait_ge(se.h, exp_done[(j - 1, "bcn")])
                            ins = t.matmul(ps[i % 2][:],
                                           skT[:, i * 128:(i + 1) * 128],
                                           sqT[:, j * 512:(j + 1) * 512],
                                           start=True, stop=True)
                            mm.inc(ins)

                    @block.vector
                    def _(v, j=j, imax=imax):
                        for i in range(4 * j, imax):
                            v.wait_ge(mm.h, mm.n - (imax - 1 - i))
                            c = 512 * j - 128 * i
                            ve.inc(v.tensor_add(
                                ps[i % 2][:], ps[i % 2][:],
                                ctx.mask[:, c + 384:c + 384 + 512]))

                    @block.scalar
                    def _(s, j=j, imax=imax):
                        mmbase = mm.n - imax
                        vebase = ve.n - 4
                        for i in range(imax):
                            if i < 4 * j:
                                s.wait_ge(mm.h, mmbase + i + 1)
                            else:
                                s.wait_ge(ve.h, vebase + (i - 4 * j) + 1)
                            se.inc(s.activation(sexp[:, i * 512:(i + 1) * 512],
                                                ps[i % 2][:], AF.Exp, scale=ISQ))
                            exp_done[(j, i)] = se.n

                    @block.tensor
                    def _(t, j=j, imax=imax):
                        t.wait_ge(ve.h, pdn_free[0])
                        t.wait_ge(ve.h, pctx_free[0])
                        ins = None
                        for i in range(imax):
                            t.wait_ge(se.h, exp_done[(j, i)])
                            t.matmul(pdn[:], ctx.ones_col_b[:],
                                     sexp[:, i * 512:(i + 1) * 512],
                                     start=(i == 0), stop=(i == imax - 1))
                            ins = t.matmul(pctx[:],
                                           svn[:, i * 128:(i + 1) * 128],
                                           sexp[:, i * 512:(i + 1) * 512],
                                           start=(i == 0), stop=(i == imax - 1))
                        mm.inc(ins)

                    @block.vector
                    def _(v, j=j):
                        mm.wait(v)
                        ve.inc(v.reciprocal(srcp[0:1, j * 512:(j + 1) * 512],
                                            pdn[:]))
                        pdn_free[0] = ve.n

                    @block.tensor
                    def _(t, j=j):
                        ve.wait(t)
                        mm.inc(t.matmul(ps0[:], ctx.ones_row_f[:],
                                        srcp[0:1, j * 512:(j + 1) * 512],
                                        start=True, stop=True))

                    @block.scalar
                    def _(s, j=j):
                        mm.wait(s)
                        se.inc(s.copy(sbcn[:], ps0[:]))
                        exp_done[(j, "bcn")] = se.n

                    @block.vector
                    def _(v, j=j, h=h):
                        se.wait(v)
                        if j == 0:
                            v.wait_ge(st.h, ctxT_stored[0])
                        ve.inc(v.tensor_mul(sctxT[:, j * 512:(j + 1) * 512],
                                            pctx[:], sbcn[:]))
                        pctx_free[0] = ve.n

                @block.sync
                def _(sync, h=h):
                    ve.wait(sync)
                    sync.dma_start(out=ctx.ctxt_scr[b, h * 128:(h + 1) * 128, :],
                                   in_=sctxT[:]).then_inc(st.h, 16)
                    st.n += 16
                    ctxT_stored[0] = st.n


def _attn_l1(ctx, b, inp):
    nc = ctx.nc
    ld, ve, se, mm, st = ctx.ld, ctx.ve, ctx.se, ctx.mm, ctx.st
    if b == 0:
        with (
            nc.sbuf_tensor(nm("b_raw"), [16, 128], F32) as sraw,
            nc.sbuf_tensor(nm("b_sig"), [16, 128], F32) as ssig,
            nc.psum_tensor(nm("b_p"), [128, 128], F32) as pbt,
            nc.Block() as block,
        ):
            @block.sync
            def _(sync):
                sync.dma_start(out=sraw[:], in_=inp["L1_attn_betas"][0, :, 0, :]
                               ).then_inc(ld.h, 16)
                ld.n += 16

            @block.scalar
            def _(s):
                ld.wait(s)
                se.inc(s.activation(ssig[:], sraw[:], AF.Sigmoid))

            pbt_free = [0]
            for h in range(NH):
                @block.tensor
                def _(t, h=h):
                    se.wait(t)
                    t.wait_ge(ve.h, pbt_free[0])
                    mm.inc(t.matmul(pbt[:], ctx.sel[:, h * 128:(h + 1) * 128],
                                    ssig[:], start=True, stop=True))

                @block.vector
                def _(v, h=h):
                    mm.wait(v)
                    ve.inc(v.tensor_copy(ctx.beta[:, h * 128:(h + 1) * 128],
                                         pbt[:]))
                    pbt_free[0] = ve.n

    _es = ExitStack()
    with _es:
        sw = _es.enter_context(nc.sbuf_tensor(nm("a1_w"), [128, 2 * 3 * DC * 128], BF16))
        sxn = _es.enter_context(nc.sbuf_tensor(nm("a1_xn"), [128, 2 * DC * 512], BF16))
        s_qq = _es.enter_context(nc.sbuf_tensor(nm("a1_qq"), [128, 512], BF16))
        s_kkb = _es.enter_context(nc.sbuf_tensor(nm("a1_kkb"), [128, 512], BF16))
        s_kkf = _es.enter_context(nc.sbuf_tensor(nm("a1_kkf"), [128, 512], F32))
        s_qqf = _es.enter_context(nc.sbuf_tensor(nm("a1_qqf"), [128, 512], F32))
        s_vvf = _es.enter_context(nc.sbuf_tensor(nm("a1_vvf"), [128, 512], F32))
        s_t1 = _es.enter_context(nc.sbuf_tensor(nm("a1_t1"), [128, 512], F32))
        s_t2 = _es.enter_context(nc.sbuf_tensor(nm("a1_t2"), [128, 512], F32))
        s_sq = _es.enter_context(nc.sbuf_tensor(nm("a1_sq"), [128, 512], BF16))
        s_skf = _es.enter_context(nc.sbuf_tensor(nm("a1_skf"), [128, 512], F32))
        s_skb = _es.enter_context(nc.sbuf_tensor(nm("a1_skb"), [128, 512], BF16))
        s_skn = _es.enter_context(nc.sbuf_tensor(nm("a1_skn"), [128, 512], BF16))
        s_vn = _es.enter_context(nc.sbuf_tensor(nm("a1_vn"), [128, 512], BF16))
        sexp = _es.enter_context(nc.sbuf_tensor(nm("a1_exp"), [128, 4 * 512], BF16))
        srn = _es.enter_context(nc.sbuf_tensor(nm("a1_rn"), [128, 12], F32))
        sdlt = _es.enter_context(nc.sbuf_tensor(nm("a1_dlt"), [128, 512], BF16))
        sam = _es.enter_context(nc.sbuf_tensor(nm("a1_am"), [128, 128], F32))
        sad = _es.enter_context(nc.sbuf_tensor(nm("a1_ad"), [128, 128], F32))
        sdf = _es.enter_context(nc.sbuf_tensor(nm("a1_df"), [128, 128], F32))
        satt = _es.enter_context(nc.sbuf_tensor(nm("a1_att"), [128, 128], F32))
        sattT = _es.enter_context(nc.sbuf_tensor(nm("a1_attT"), [128, 2 * 512], BF16))
        smem = _es.enter_context(nc.sbuf_tensor(nm("a1_mem"), [128, NH * 128], F32))
        smemb = _es.enter_context(nc.sbuf_tensor(nm("a1_memb"), [128, NH * 128], BF16))
        sz = _es.enter_context(nc.sbuf_tensor(nm("a1_z"), [128, NH], F32))
        szb = _es.enter_context(nc.sbuf_tensor(nm("a1_zb"), [128, NH], BF16))
        PB = _es.enter_context(nc.psum_tensor(nm("a1_PB"), [128, 2048], F32))
        pA = _es.enter_context(nc.psum_tensor(nm("a1_pA"), [128, 512], F32))
        pB = _es.enter_context(nc.psum_tensor(nm("a1_pB"), [128, 512], F32))
        pC = _es.enter_context(nc.psum_tensor(nm("a1_pC"), [128, 512], F32))
        pD = _es.enter_context(nc.psum_tensor(nm("a1_pD"), [128, 512], F32))
        WSL = 3 * DC * 128

        def load_wslices(sync, sg, h, par):
            for wi, wname in enumerate(("wq1", "wk1", "wv1")):
                sync.dma_start(
                    out=chunk_view(sw[:, par * WSL + wi * DC * 128:
                                      par * WSL + (wi + 1) * DC * 128], 128),
                    in_=ctx.w[wname][:, h * 128:(h + 1) * 128]
                    .rearrange("(c p) j -> p c j", p=128)).then_inc(ld.h, 16)
                ld.n += 16
            return ld.n

        with nc.Block() as block:
            w_mark = {}
            xn_mark = {}
            evac_mark = [0]
            exp_mark = {}
            admn_read = [0]
            skm_read = [0]
            upd_read = [0]
            tr_evac = [0, 0]
            mem_cast = {}
            attst_done = [0, 0]

            @block.sync
            def _(sync):
                sync.dma_start(
                    out=chunk_view(sxn[:, 0:DC * 512], 512),
                    in_=ctx.xnt_scr[b, :, 0:512]
                    .rearrange("(c p) t -> p c t", p=128)).then_inc(ld.h, 16)
                ld.n += 16
                xn_mark[0] = ld.n
                w_mark[(0, 0)] = load_wslices(sync, 0, 0, 0)

            @block.vector
            def _(v):
                v.memset(smem[:], 0.0)
                v.memset(smemb[:], 0.0)
                v.memset(sz[:], 1.0 / DKH)
                ve.inc(v.memset(szb[:], 1.0 / DKH))
            mem_init = ve.n

            for sg in range(NSEG):
                for h in range(NH):
                    it = sg * NH + h
                    par = it % 2
                    po = par * WSL

                    @block.sync
                    def _(sync, sg=sg, h=h, it=it):
                        nsg, nh = (sg, h + 1) if h + 1 < NH else (sg + 1, 0)
                        if nsg < NSEG:
                            if nh == 0:
                                sync.dma_start(
                                    out=chunk_view(
                                        sxn[:, (nsg % 2) * DC * 512:
                                            (nsg % 2 + 1) * DC * 512], 512),
                                    in_=ctx.xnt_scr[b, :, nsg * 512:(nsg + 1) * 512]
                                    .rearrange("(c p) t -> p c t", p=128)
                                    ).then_inc(ld.h, 16)
                                ld.n += 16
                                xn_mark[nsg] = ld.n
                            w_mark[(nsg, nh)] = load_wslices(
                                sync, nsg, nh, (it + 1) % 2)

                    @block.tensor
                    def _(t, sg=sg, h=h, it=it, po=po):
                        t.wait_ge(ld.h, max(w_mark[(sg, h)], xn_mark[sg]))
                        t.wait_ge(ve.h, evac_mark[0])
                        if it > 0:
                            t.wait_ge(se.h, exp_mark[(it - 1)])
                        else:
                            t.wait_ge(ve.h, mem_init)
                        xb = (sg % 2) * DC * 512
                        for wi in range(3):
                            ins = None
                            for kc in range(DC):
                                ins = t.matmul(
                                    PB[:, wi * 512:(wi + 1) * 512],
                                    sw[:, po + wi * DC * 128 + kc * 128:
                                       po + wi * DC * 128 + (kc + 1) * 128],
                                    sxn[:, xb + kc * 512:xb + (kc + 1) * 512],
                                    start=(kc == 0), stop=(kc == DC - 1))
                            mm.inc(ins)

                    @block.vector
                    def _(v, sg=sg, h=h):
                        base = mm.n - 3
                        v.wait_ge(mm.h, base + 1)
                        v.tensor_copy(s_qq[:], PB[:, 0:512])
                        v.tensor_copy(s_qqf[:], PB[:, 0:512])
                        v.wait_ge(mm.h, base + 2)
                        v.tensor_copy(s_kkb[:], PB[:, 512:1024])
                        v.tensor_copy(s_kkf[:], PB[:, 512:1024])
                        v.wait_ge(mm.h, base + 3)
                        ve.flush(v, v.tensor_copy(s_vvf[:], PB[:, 1024:1536]))
                        evac_mark[0] = ve.n
                        v.tensor_scalar_min(s_t1[:], s_qqf[:], 0.0)
                        ve.inc(v.tensor_relu(s_t2[:], s_qqf[:]))

                    @block.scalar
                    def _(s):
                        ve.wait(s)
                        se.inc(s.activation(s_t1[:], s_t1[:], AF.Exp))

                    @block.vector
                    def _(v):
                        se.wait(v)
                        ve.inc(v.tensor_add(s_sq[:], s_t1[:], s_t2[:]))
                        v.tensor_scalar_min(s_t1[:], s_kkf[:], 0.0)
                        ve.inc(v.tensor_relu(s_t2[:], s_kkf[:]))

                    @block.scalar
                    def _(s):
                        ve.wait(s)
                        se.inc(s.activation(s_t1[:], s_t1[:], AF.Exp))

                    @block.vector
                    def _(v):
                        se.wait(v)
                        ve.flush(v, v.tensor_add(s_skf[:], s_t1[:], s_t2[:]))
                        ve.inc(v.tensor_copy(s_skb[:], s_skf[:]))

                    for ti, (src_t, dst_t) in enumerate(((s_skf, s_skn),
                                                         (s_vvf, s_vn))):
                        for s1c in range(4):
                            slot = (ti * 4 + s1c) % 2

                            @block.tensor
                            def _(t, src_t=src_t, s1c=s1c, slot=slot):
                                ve.wait(t)
                                t.wait_ge(se.h, tr_evac[slot])
                                mm.inc(t.transpose(
                                    pD[:, 256 + slot * 128:384 + slot * 128],
                                    src_t[:, s1c * 128:(s1c + 1) * 128],
                                    ctx.ident_f[:]))

                            @block.scalar
                            def _(s, dst_t=dst_t, s1c=s1c, slot=slot):
                                mm.wait(s)
                                se.inc(s.copy(
                                    dst_t[:, s1c * 128:(s1c + 1) * 128],
                                    pD[:, 256 + slot * 128:384 + slot * 128]))
                                tr_evac[slot] = se.n

                    @block.tensor
                    def _(t, sg=sg, h=h, it=it):
                        ve.wait(t)
                        for i in range(4):
                            ins = t.matmul(PB[:, i * 512:(i + 1) * 512],
                                           s_kkb[:, i * 128:(i + 1) * 128],
                                           s_qq[:], start=True, stop=True)
                            mm.inc(ins)

                    @block.vector
                    def _(v, it=it):
                        base = mm.n - 4
                        for i in range(4):
                            v.wait_ge(mm.h, base + i + 1)
                            c = -128 * i
                            ve.inc(v.tensor_add(
                                PB[:, i * 512:(i + 1) * 512],
                                PB[:, i * 512:(i + 1) * 512],
                                ctx.mask[:, c + 384:c + 384 + 512]))

                    @block.scalar
                    def _(s, it=it):
                        base = ve.n - 4
                        for i in range(4):
                            s.wait_ge(ve.h, base + i + 1)
                            se.inc(s.activation(sexp[:, i * 512:(i + 1) * 512],
                                                PB[:, i * 512:(i + 1) * 512],
                                                AF.Exp, scale=ISQ))
                        exp_mark[it] = se.n

                    @block.tensor
                    def _(t, sg=sg, h=h, it=it):
                        se.wait(t)
                        t.wait_ge(ve.h, admn_read[0])
                        t.wait_ge(ve.h, skm_read[0])
                        if it > 0:
                            t.wait_ge(se.h, mem_cast.get(h, 0) if sg > 0 else 0)
                        ins = None
                        for s1c in range(4):
                            t.matmul(pB[:, s1c * 128:(s1c + 1) * 128],
                                     s_sq[:, s1c * 128:(s1c + 1) * 128],
                                     smemb[:, h * 128:(h + 1) * 128],
                                     start=True, stop=True)
                            t.matmul(pC[:, s1c * 128:(s1c + 1) * 128],
                                     s_skb[:, s1c * 128:(s1c + 1) * 128],
                                     smemb[:, h * 128:(h + 1) * 128],
                                     start=True, stop=True)
                            for i in range(s1c + 1):
                                t.matmul(pD[:, s1c:s1c + 1],
                                         sexp[:, i * 512 + s1c * 128:
                                              i * 512 + (s1c + 1) * 128],
                                         ctx.ones_col_b[:],
                                         start=(i == 0), stop=(i == s1c))
                            for i in range(s1c + 1):
                                t.matmul(pA[:, s1c * 128:(s1c + 1) * 128],
                                         sexp[:, i * 512 + s1c * 128:
                                              i * 512 + (s1c + 1) * 128],
                                         s_vn[:, i * 128:(i + 1) * 128],
                                         start=(i == 0), stop=(i == s1c))
                            t.matmul(pD[:, 4 + s1c:5 + s1c],
                                     s_sq[:, s1c * 128:(s1c + 1) * 128],
                                     szb[:, h:h + 1], start=True, stop=True)
                            ins = t.matmul(pD[:, 8 + s1c:9 + s1c],
                                           s_skb[:, s1c * 128:(s1c + 1) * 128],
                                           szb[:, h:h + 1], start=True, stop=True)
                        mm.inc(ins)

                    @block.vector
                    def _(v, sg=sg, h=h):
                        mm.wait(v)
                        ve.flush(v, v.reciprocal(srn[:], pD[:, 0:12]))
                        for s1c in range(4):
                            ve.inc(v.scalar_tensor_tensor(
                                sdlt[:, s1c * 128:(s1c + 1) * 128],
                                pC[:, s1c * 128:(s1c + 1) * 128],
                                srn[:, 8 + s1c:9 + s1c],
                                s_vn[:, s1c * 128:(s1c + 1) * 128],
                                op0=ALU.mult, op1=ALU.subtract))
                        skm_read[0] = ve.n

                    @block.tensor
                    def _(t, sg=sg, h=h):
                        ve.wait(t)
                        t.wait_ge(ve.h, upd_read[0])
                        for s1c in range(4):
                            t.matmul(pD[:, 128:256],
                                     s_skn[:, s1c * 128:(s1c + 1) * 128],
                                     sdlt[:, s1c * 128:(s1c + 1) * 128],
                                     start=(s1c == 0), stop=(s1c == 3))
                        ins = None
                        for s1c in range(4):
                            ins = t.matmul(pD[:, 260:261],
                                           s_skn[:, s1c * 128:(s1c + 1) * 128],
                                           ctx.ones_col_b[:],
                                           start=(s1c == 0), stop=(s1c == 3))
                        mm.inc(ins)

                    @block.vector
                    def _(v, sg=sg, h=h):
                        mm.wait(v)
                        v.tensor_sub(smem[:, h * 128:(h + 1) * 128],
                                     smem[:, h * 128:(h + 1) * 128],
                                     pD[:, 128:256])
                        ve.inc(v.tensor_add(sz[:, h:h + 1], sz[:, h:h + 1],
                                            pD[:, 260:261]))
                        upd_read[0] = ve.n

                    @block.scalar
                    def _(s, sg=sg, h=h):
                        ve.wait(s)
                        s.copy(smemb[:, h * 128:(h + 1) * 128],
                               smem[:, h * 128:(h + 1) * 128])
                        se.inc(s.copy(szb[:, h:h + 1], sz[:, h:h + 1]))
                        mem_cast[h] = se.n

                    for s1c in range(4):
                        @block.vector
                        def _(v, sg=sg, h=h, s1c=s1c, par=par):
                            if s1c == 0:
                                v.wait_ge(st.h, attst_done[par])
                            else:
                                v.wait_ge(mm.h, mm.n)
                            v.tensor_scalar_mul(sam[:],
                                                pB[:, s1c * 128:(s1c + 1) * 128],
                                                srn[:, 4 + s1c:5 + s1c])
                            ve.flush(v, v.tensor_scalar_mul(
                                sad[:], pA[:, s1c * 128:(s1c + 1) * 128],
                                srn[:, s1c:s1c + 1]))
                            ve.flush(v, v.tensor_sub(sdf[:], sam[:], sad[:]))
                            ve.flush(v, v.tensor_mul(
                                sdf[:], sdf[:],
                                ctx.beta[:, h * 128:(h + 1) * 128]))
                            ve.inc(v.tensor_add(satt[:], sad[:], sdf[:]))
                            if s1c == 3:
                                admn_read[0] = ve.n

                        slot = s1c % 2

                        @block.tensor
                        def _(t, s1c=s1c, slot=slot):
                            ve.wait(t)
                            t.wait_ge(se.h, tr_evac[slot])
                            mm.inc(t.transpose(
                                pD[:, 256 + slot * 128:384 + slot * 128],
                                satt[:], ctx.ident_f[:]))

                        @block.scalar
                        def _(s, s1c=s1c, slot=slot, par=par):
                            mm.wait(s)
                            se.inc(s.copy(
                                sattT[:, par * 512 + s1c * 128:
                                      par * 512 + (s1c + 1) * 128],
                                pD[:, 256 + slot * 128:384 + slot * 128]))
                            tr_evac[slot] = se.n

                    @block.sync
                    def _(sync, sg=sg, h=h, par=par):
                        se.wait(sync)
                        sync.dma_start(
                            out=ctx.attt_scr[b, h * 128:(h + 1) * 128,
                                             sg * 512:(sg + 1) * 512],
                            in_=sattT[:, par * 512:(par + 1) * 512]
                            ).then_inc(st.h, 16)
                        st.n += 16
                        attst_done[par] = st.n

            @block.sync
            def _(sync):
                st.wait(sync)


def _quarters(ctx, L, x_src, x_dst, inp):
    nc = ctx.nc
    ld, ve, se, mm, st = ctx.ld, ctx.ve, ctx.se, ctx.mm, ctx.st
    if L == 0:
        qt = 512
        FC, FF = FC0, F0
        NEXP = NE0
        dw_scr = ctx.cmb_scr
        att_scr = ctx.ctxt_scr
        wo = ctx.w["woT0"]
        experts = [(ctx.w["gw0"][e, :, :], ctx.w["uw0"][e, :, :],
                    ctx.w["dw0"][e, :, :], True) for e in range(NE0)]
    else:
        qt = 512
        FC, FF = IC1, I1
        NEXP = NE1
        dw_scr = ctx.dw1_scr
        att_scr = ctx.attt_scr
        wo = ctx.w["wo1"]
        experts = [(ctx.w["w1T"][e, :, :], ctx.w["w3T"][e, :, :],
                    ctx.w["w2T"][e, :, :], True) for e in range(NE1)]
        experts.append((ctx.w["ws1T"][:, :], ctx.w["ws3T"][:, :],
                        ctx.w["ws2T"][:, :], False))
    NTT = qt // 512

    _es = ExitStack()
    with _es:
        acc = _es.enter_context(nc.sbuf_tensor(nm("q_acc"), [128, DC * qt], F32))
        sxn = _es.enter_context(nc.sbuf_tensor(nm("q_xn"), [128, DC * qt], BF16))
        sdwT = _es.enter_context(nc.sbuf_tensor(nm("q_dwT"), [16, qt], F32))
        sdwb = _es.enter_context(nc.sbuf_tensor(nm("q_dwb"), [128, qt], F32))
        sh = _es.enter_context(nc.sbuf_tensor(nm("q_h"), [128, FC * qt], BF16))
        sgt = _es.enter_context(nc.sbuf_tensor(nm("q_gt"), [128, 512], BF16))
        sut = _es.enter_context(nc.sbuf_tensor(nm("q_ut"), [128, 512], BF16))
        sbo = _es.enter_context(nc.sbuf_tensor(nm("q_bo"), [128, DC], F32))
        sw1 = _es.enter_context(nc.sbuf_tensor(nm("q_w1"), [128, DC * FF], BF16))
        sw3 = _es.enter_context(nc.sbuf_tensor(nm("q_w3"), [128, DC * FF], BF16))
        sw2 = _es.enter_context(nc.sbuf_tensor(nm("q_w2"), [128, FC * D], BF16))
        pa = _es.enter_context(nc.psum_tensor(nm("q_pa"), [128, 512], F32))
        pb_ = _es.enter_context(nc.psum_tensor(nm("q_pb"), [128, 512], F32))
        pc = _es.enter_context(nc.psum_tensor(nm("q_pc"), [128, 512], F32))
        pd = _es.enter_context(nc.psum_tensor(nm("q_pd"), [128, 512], F32))
        pe5 = _es.enter_context(nc.psum_tensor(nm("q_pe5"), [128, 512], F32))
        pe6 = _es.enter_context(nc.psum_tensor(nm("q_pe6"), [128, 512], F32))
        pu_rot = [pa, pb_, pe5]
        pg_rot = [pc, pd, pe6]
        pdown_rot = [pa, pb_, pe5, pe6]
        pout_rot = [pa, pb_, pc, pd]
        if L == 0:
            with (
                nc.sbuf_tensor(nm("q_borow"), [1, D], F32) as sborow,
                nc.Block() as block,
            ):
                @block.sync
                def _(sync):
                    sync.dma_start(out=sborow[:], in_=inp["L0_attn_bo"][None, :]
                                   ).then_inc(ld.h, 16)
                    ld.n += 16
                _bias_cols(ctx, block, sborow, sbo, pa, 1)

        def load_w13(sync, e):
            w13a, w13b, _, _ = experts[e]
            sync.dma_start(out=chunk_view(sw1[:], FF),
                           in_=w13a.rearrange("(c p) n -> p c n", p=128)
                           ).then_inc(ld.h, 16)
            sync.dma_start(out=chunk_view(sw3[:], FF),
                           in_=w13b.rearrange("(c p) n -> p c n", p=128)
                           ).then_inc(ld.h, 16)
            ld.n += 32
            return ld.n

        for q in range(NB * S // qt):
            b = q // (S // qt)
            t0 = (q % (S // qt)) * qt
            w13_mark = [None]
            with nc.Block() as block:
                @block.sync
                def _(sync, b=b, t0=t0):
                    sync.wait_ge(mm.h, mm.n)
                    sync.dma_start(
                        out=chunk_view(sxn[:], qt),
                        in_=ctx.xnt_scr[b, :, t0:t0 + qt]
                        .rearrange("(c p) t -> p c t", p=128)).then_inc(ld.h, 16)
                    sync.dma_start(
                        out=sdwT[:NEXP, :],
                        in_=dw_scr[:, b * S + t0:b * S + t0 + qt]
                        ).then_inc(ld.h, 16)
                    ld.n += 32
                    w13_mark[0] = load_w13(sync, 0)

            with (
                nc.sbuf_tensor(nm("q_ctx"), [128, DC * qt], BF16) as sctx,
                nc.sbuf_tensor(nm("q_woA"), [128, DC * 128], BF16) as swoA,
                nc.sbuf_tensor(nm("q_woB"), [128, DC * 128], BF16) as swoB,
                nc.Block() as block,
            ):
                swo = [swoA, swoB]
                wo_loaded = []
                wo_marks = []
                ctx_mark = [None]
                acc_marks = [0, 0, 0, 0]

                @block.sync
                def _(sync, b=b, t0=t0):
                    sync.dma_start(
                        out=chunk_view(sctx[:], qt),
                        in_=att_scr[b, :, t0:t0 + qt]
                        .rearrange("(c p) t -> p c t", p=128)).then_inc(ld.h, 16)
                    ld.n += 16
                    ctx_mark[0] = ld.n

                for dc in range(DC):
                    @block.sync
                    def _(sync, dc=dc):
                        if dc >= 2:
                            sync.wait_ge(mm.h, wo_loaded[dc - 2])
                        sync.dma_start(
                            out=chunk_view(swo[dc % 2][:], 128),
                            in_=wo[:, dc * 128:(dc + 1) * 128]
                            .rearrange("(c p) j -> p c j", p=128)
                            ).then_inc(ld.h, 16)
                        ld.n += 16
                        wo_marks.append(ld.n)

                    @block.tensor
                    def _(t, dc=dc):
                        t.wait_ge(ld.h, max(wo_marks[dc], ctx_mark[0]))
                        t.wait_ge(ve.h, acc_marks[dc % 4])
                        for tt in range(NTT):
                            p = pout_rot[dc % 4]
                            ins = None
                            if L == 0:
                                for hc in range(DC):
                                    ins = t.matmul(
                                        p[:],
                                        swo[dc % 2][:, hc * 128:(hc + 1) * 128],
                                        sctx[:, hc * qt + tt * 512:
                                             hc * qt + (tt + 1) * 512],
                                        start=(hc == 0), stop=(hc == DC - 1))
                                mm.inc(ins)
                            else:
                                sv = sctx[:].rearrange(
                                    "p (c m s) -> p c m s", m=32, s=16)
                                for sl in range(16):
                                    ins = t.matmul(
                                        p[:],
                                        swo[dc % 2][:, sl * 128:(sl + 1) * 128],
                                        sv[:, :, :, sl],
                                        start=(sl == 0), stop=(sl == 15))
                                mm.inc(ins)
                        wo_loaded.append(mm.n)

                    @block.vector
                    def _(v, dc=dc):
                        for tt in range(NTT):
                            p = pout_rot[dc % 4]
                            v.wait_ge(mm.h, mm.n - (NTT - 1) + tt)
                            dst = acc[:, dc * qt + tt * 512:
                                      dc * qt + (tt + 1) * 512]
                            if L == 0:
                                vi = v.tensor_scalar(dst, p[:],
                                                     sbo[:, dc:dc + 1], None,
                                                     op0=ALU.add)
                            else:
                                vi = v.tensor_copy(dst, p[:])
                            ve.inc(vi)
                        acc_marks[dc % 4] = ve.n

            if getattr(ctx, "dbg_acc", None) is not None and q == 0:
                with nc.Block() as block:
                    @block.sync
                    def _(sync):
                        sync.wait_ge(ve.h, ve.n)
                        sync.dma_start(out=ctx.dbg_acc[:, :qt * DC],
                                       in_=acc[:]).then_inc(st.h, 16)
                        st.n += 16
                        st.wait(sync)

            for ei, (w13a, w13b, w2, has_dw) in enumerate(
                    [] if DBG_SKIP_EXPERTS else experts):
                w2_mark = [None]
                with nc.Block() as block:
                    @block.sync
                    def _(sync, ei=ei):
                        sync.wait_ge(mm.h, mm.n)
                        sync.dma_start(
                            out=chunk_view(sw2[:], D),
                            in_=w2.rearrange("(c p) n -> p c n", p=128)
                            ).then_inc(ld.h, 16)
                        ld.n += 16
                        w2_mark[0] = ld.n

                    if has_dw:
                        @block.tensor
                        def _(t, ei=ei):
                            ve.wait(t)
                            se.wait(t)
                            for tt in range(NTT):
                                mm.inc(t.matmul(
                                    (pc if tt == 0 else pd)[:],
                                    ctx.sel[:NEXP, ei * 128:(ei + 1) * 128],
                                    sdwT[:NEXP, tt * 512:(tt + 1) * 512],
                                    start=True, stop=True))

                        @block.scalar
                        def _(s, ei=ei):
                            base = mm.n - NTT
                            for tt in range(NTT):
                                s.wait_ge(mm.h, base + tt + 1)
                                se.inc(s.copy(
                                    sdwb[:, tt * 512:(tt + 1) * 512],
                                    (pc if tt == 0 else pd)[:]))

                    gu_done = {}
                    for tt in range(NTT):
                        for fc in range(FC):
                            @block.tensor
                            def _(t, tt=tt, fc=fc):
                                t.wait_ge(ld.h, w13_mark[0])
                                if has_dw:
                                    se.wait(t)
                                pu = pu_rot[fc % 3]
                                pg = pg_rot[fc % 3]
                                gi = tt * FC + fc
                                if gi >= 3:
                                    pi = gi - 3
                                    prev = (pi // FC, pi % FC)
                                    t.wait_ge(ve.h, gu_done[prev][0])
                                    t.wait_ge(se.h, gu_done[prev][1])
                                ins = None
                                for kc in range(DC):
                                    ins = t.matmul(
                                        pu[:],
                                        sw3[:, kc * FF + fc * 128:
                                            kc * FF + (fc + 1) * 128],
                                        sxn[:, kc * qt + tt * 512:
                                            kc * qt + (tt + 1) * 512],
                                        start=(kc == 0), stop=(kc == DC - 1))
                                mm.inc(ins)
                                ins = None
                                for kc in range(DC):
                                    ins = t.matmul(
                                        pg[:],
                                        sw1[:, kc * FF + fc * 128:
                                            kc * FF + (fc + 1) * 128],
                                        sxn[:, kc * qt + tt * 512:
                                            kc * qt + (tt + 1) * 512],
                                        start=(kc == 0), stop=(kc == DC - 1))
                                mm.inc(ins)

                            @block.vector
                            def _(v, tt=tt, fc=fc):
                                pu = pu_rot[fc % 3]
                                v.wait_ge(mm.h, mm.n - 1)
                                if has_dw:
                                    vi = v.tensor_mul(
                                        sut[:], pu[:],
                                        sdwb[:, tt * 512:(tt + 1) * 512])
                                else:
                                    vi = v.tensor_copy(sut[:], pu[:])
                                ve.flush(v, vi)

                            @block.scalar
                            def _(s, tt=tt, fc=fc):
                                pg = pg_rot[fc % 3]
                                s.wait_ge(mm.h, mm.n)
                                prev1 = ((tt, fc - 1) if fc >= 1
                                         else (tt - 1, FC - 1))
                                if prev1 in gu_done:
                                    s.wait_ge(ve.h, gu_done[prev1][0])
                                se.inc(s.activation(sgt[:], pg[:], AF.Silu))

                            @block.vector
                            def _(v, tt=tt, fc=fc):
                                se.wait(v)
                                ve.inc(v.tensor_mul(
                                    sh[:, fc * qt + tt * 512:
                                       fc * qt + (tt + 1) * 512],
                                    sgt[:], sut[:]))
                                gu_done[(tt, fc)] = (ve.n, se.n)

                    @block.sync
                    def _(sync, ei=ei):
                        if ei + 1 < len(experts):
                            sync.wait_ge(mm.h, mm.n)
                            w13_mark[0] = load_w13(sync, ei + 1)

                    down_evac = {}
                    gu_ve_done = ve.n
                    for dc in range(DC):
                        for tt in range(NTT):
                            @block.tensor
                            def _(t, dc=dc, tt=tt):
                                t.wait_ge(ld.h, w2_mark[0])
                                idx = dc * NTT + tt
                                p = pdown_rot[idx % 4]
                                if idx >= 4:
                                    t.wait_ge(ve.h, down_evac[idx - 4])
                                else:
                                    t.wait_ge(ve.h, gu_ve_done)
                                ins = None
                                for fc in range(FC):
                                    ins = t.matmul(
                                        p[:],
                                        sw2[:, fc * D + dc * 128:
                                            fc * D + (dc + 1) * 128],
                                        sh[:, fc * qt + tt * 512:
                                           fc * qt + (tt + 1) * 512],
                                        start=(fc == 0), stop=(fc == FC - 1))
                                mm.inc(ins)

                            @block.vector
                            def _(v, dc=dc, tt=tt):
                                idx = dc * NTT + tt
                                p = pdown_rot[idx % 4]
                                mm.wait(v)
                                sl = acc[:, dc * qt + tt * 512:
                                         dc * qt + (tt + 1) * 512]
                                ve.inc(v.tensor_add(sl, sl, p[:]))
                                down_evac[idx] = ve.n

            with (
                nc.sbuf_tensor(nm("q_x0"), [128, D], F32) as sx0,
                nc.sbuf_tensor(nm("q_x1"), [128, D], F32) as sx1,
                nc.sbuf_tensor(nm("q_o0"), [128, D], F32) as so0,
                nc.sbuf_tensor(nm("q_o1"), [128, D], F32) as so1,
                nc.Block() as block,
            ):
                sx = [sx0, sx1]
                so = [so0, so1]
                ep_done = [0, 0]
                for ts in range(qt // 128):
                    @block.sync
                    def _(sync, ts=ts, b=b, t0=t0):
                        sync.wait_ge(st.h, ep_done[ts % 2])
                        sync.dma_start(out=sx[ts % 2][:],
                                       in_=x_src[b, t0 + ts * 128:
                                                 t0 + (ts + 1) * 128, :]
                                       ).then_inc(ld.h, 16)
                        ld.n += 16

                    @block.tensor
                    def _(t, ts=ts):
                        ve.wait(t)
                        se.wait(t)
                        ins = None
                        for dc in range(DC):
                            ins = t.transpose(
                                pout_rot[dc // 4][:, (dc % 4) * 128:
                                                  (dc % 4) * 128 + 128],
                                acc[:, dc * qt + ts * 128:
                                    dc * qt + (ts + 1) * 128],
                                ctx.ident_f[:])
                        mm.inc(ins)

                    @block.vector
                    def _(v, ts=ts):
                        mm.wait(v)
                        ld.wait(v)
                        for c4 in range(4):
                            ve.inc(v.tensor_add(
                                so[ts % 2][:, c4 * 512:(c4 + 1) * 512],
                                sx[ts % 2][:, c4 * 512:(c4 + 1) * 512],
                                pout_rot[c4][:]))

                    @block.sync
                    def _(sync, ts=ts, b=b, t0=t0):
                        ve.wait(sync)
                        sync.dma_start(out=x_dst[b, t0 + ts * 128:
                                                 t0 + (ts + 1) * 128, :],
                                       in_=so[ts % 2][:]).then_inc(st.h, 16)
                        st.n += 16
                        ep_done[ts % 2] = st.n

                @block.sync
                def _(sync):
                    st.wait(sync)


def build_program(layers, dbg=()):
    nc = bass.Bass("TRN2", target_bir_lowering=False)
    es = ExitStack()
    with es:
        inp = {}
        if 0 in layers:
            inp["L0_attn_x"] = nc.dram_tensor("L0_attn_x", [NB, S, D], F32,
                                              kind="ExternalInput")
            names = list(L0_WNAMES)
        else:
            inp["x1_in"] = nc.dram_tensor("x1_in", [NB, S, D], F32,
                                          kind="ExternalInput")
            names = []
        if 1 in layers:
            names += L1_WNAMES
        for n in names:
            inp[n] = nc.dram_tensor(n, INPUT_SHAPES[n], F32, kind="ExternalInput")
        out = nc.dram_tensor("out", [NB, S, D], F32, kind="ExternalOutput")

        ctx = _mk_ctx(nc, es, layers)
        ctx.inp = inp
        ctx.xnt_scr = nc.dram_tensor("xnt_scr", [NB, D, S], BF16, kind="Internal")
        if 0 in layers:
            ctx.ctxt_scr = nc.dram_tensor("ctxt_scr", [NB, D, S], BF16,
                                          kind="Internal")
            ctx.cmb_scr = nc.dram_tensor("cmb_scr", [NE0, NB * S], F32,
                                         kind="Internal")
        if 1 in layers:
            ctx.attt_scr = nc.dram_tensor("attt_scr", [NB, D, S], BF16,
                                          kind="Internal")
            ctx.dw1_scr = nc.dram_tensor("dw1_scr", [NE1, NB * S], F32,
                                         kind="Internal")
        if layers == [0, 1]:
            ctx.x1_scr = nc.dram_tensor("x1_scr", [NB, S, D], F32, kind="Internal")
        elif layers == [0]:
            ctx.x1_scr = out
        else:
            ctx.x1_scr = inp["x1_in"]

        _consts(ctx)
        _prep_weights(ctx, inp)

        for L in layers:
            x_src = inp["L0_attn_x"] if L == 0 else ctx.x1_scr
            x_dst = ctx.x1_scr if L == 0 else out
            if layers == [0]:
                x_dst = out
            with ExitStack() as _les:
                ctx.xnT = _les.enter_context(
                    nc.sbuf_tensor(nm("xnT"), [128, DC * S], BF16))
                if L == 1:
                    ctx.beta = _les.enter_context(
                        nc.sbuf_tensor(nm("beta"), [128, NH * 128], F32))
                for b in range(NB):
                    _ln(ctx, b, x_src)
                    if L == 0:
                        _router_l0(ctx, b)
                        _attn_l0(ctx, b, inp)
                    else:
                        _gate_l1(ctx, b, inp)
                        _attn_l1(ctx, b, inp)
            _quarters(ctx, L, x_src, x_dst, inp)

        if dbg:
            dbg_outs = {}
            for name in dbg:
                src = getattr(ctx, name)
                dbg_outs[name] = nc.dram_tensor(
                    "dbg_" + name, list(src.shape), src.dtype,
                    kind="ExternalOutput")
            with nc.Block() as block:
                @block.sync
                def _(sync):
                    for name in dbg:
                        src = getattr(ctx, name)
                        idx = tuple(slice(None) for _ in src.shape)
                        sync.dma_start(out=dbg_outs[name][idx],
                                       in_=src[idx]).then_inc(ctx.st.h, 16)
                        ctx.st.n += 16
                    ctx.st.wait(sync)
    return nc


FUSED = True


def kernel(**inputs):
    xs = np.split(np.ascontiguousarray(inputs["L0_attn_x"]), NCORES, axis=0)
    w = {k: np.ascontiguousarray(v) for k, v in inputs.items()
         if k != "L0_attn_x"}

    if FUSED:
        nc = build_program([0, 1])
        in_maps = [{**w, "L0_attn_x": xs[i]} for i in range(NCORES)]
        res = run_bass_kernel_spmd(nc, in_maps, core_ids=list(range(NCORES)))
        return np.concatenate([res.results[i]["out"] for i in range(NCORES)],
                              axis=0)

    ncA = build_program([0])
    in_maps = [{**{k: w[k] for k in L0_WNAMES}, "L0_attn_x": xs[i]}
               for i in range(NCORES)]
    resA = run_bass_kernel_spmd(ncA, in_maps, core_ids=list(range(NCORES)))
    x1s = [resA.results[i]["out"] for i in range(NCORES)]

    ncB = build_program([1])
    in_maps = [{**{k: w[k] for k in L1_WNAMES}, "x1_in": x1s[i]}
               for i in range(NCORES)]
    resB = run_bass_kernel_spmd(ncB, in_maps, core_ids=list(range(NCORES)))
    return np.concatenate([resB.results[i]["out"] for i in range(NCORES)],
                          axis=0)


# version 07
# speedup vs baseline: 1.0345x; 1.0345x over previous
import numpy as np
from contextlib import ExitStack
import concourse.bass as bass
import concourse.mybir as mybir
from concourse.bass_utils import run_bass_kernel_spmd

F32 = mybir.dt.float32
BF16 = mybir.dt.bfloat16
AF = mybir.ActivationFunctionType
ALU = mybir.AluOpType
AX = mybir.AxisListType.X

NCORES = 8
NB = 2
S = 2048
D = 2048
NH, DKH = 16, 128
NE0, F0 = 16, 768
NE1, I1 = 8, 1024
NSEG, SEG = 4, 512
QT = 1024
ISQ = 0.08838834764831845
NEGBIG = -1e30
EPS = 1e-5
DBG_SKIP_EXPERTS = False
DBG_SKIP_ATTNOUT = False
DC = D // 128
TC = S // 128
FC0 = F0 // 128
IC1 = I1 // 128

WNAMES = [
    "L0_attn_Wq", "L0_attn_bq", "L0_attn_Wk", "L0_attn_bk", "L0_attn_Wv",
    "L0_attn_bv", "L0_attn_Wo", "L0_attn_bo", "L0_ffn_router_w",
    "L0_ffn_gate_w", "L0_ffn_up_w", "L0_ffn_down_w",
    "L1_attn_Wq", "L1_attn_Wk", "L1_attn_Wv", "L1_attn_Wo", "L1_attn_betas",
    "L1_ffn_gate_w", "L1_ffn_gate_b", "L1_ffn_W1", "L1_ffn_W2", "L1_ffn_W3",
    "L1_ffn_Ws1", "L1_ffn_Ws2", "L1_ffn_Ws3",
]
L0_WNAMES = [n for n in WNAMES if n.startswith("L0")]
L1_WNAMES = [n for n in WNAMES if n.startswith("L1")]

INPUT_SHAPES = {
    "L0_attn_Wq": [D, D], "L0_attn_bq": [D], "L0_attn_Wk": [D, D],
    "L0_attn_bk": [D], "L0_attn_Wv": [D, D], "L0_attn_bv": [D],
    "L0_attn_Wo": [D, D], "L0_attn_bo": [D],
    "L0_ffn_router_w": [D, NE0],
    "L0_ffn_gate_w": [NE0, D, F0], "L0_ffn_up_w": [NE0, D, F0],
    "L0_ffn_down_w": [NE0, F0, D],
    "L1_attn_Wq": [D, D], "L1_attn_Wk": [D, D], "L1_attn_Wv": [D, D],
    "L1_attn_Wo": [D, D], "L1_attn_betas": [1, NH, 1, DKH],
    "L1_ffn_gate_w": [NE1, D], "L1_ffn_gate_b": [NE1],
    "L1_ffn_W1": [NE1, I1, D], "L1_ffn_W2": [NE1, D, I1],
    "L1_ffn_W3": [NE1, I1, D],
    "L1_ffn_Ws1": [I1, D], "L1_ffn_Ws2": [D, I1], "L1_ffn_Ws3": [I1, D],
}



_NAMEC = [0]


def nm(base):
    _NAMEC[0] += 1
    return f"{base}_{_NAMEC[0]}"

class Sem:
    __slots__ = ("h", "n")

    def __init__(self, es, nc, name):
        self.h = es.enter_context(nc.semaphore(name))
        self.n = 0

    def inc(self, ins, k=1):
        ins.then_inc(self.h, k)
        self.n += k

    def bump(self, eng, k=1):
        eng.sem_inc(self.h, k)
        self.n += k

    def flush(self, eng, ins):
        self.inc(ins)
        eng.wait_ge(self.h, self.n)
        return ins

    def wait(self, eng, val=None):
        eng.wait_ge(self.h, self.n if val is None else val)


class Ctx:
    pass


def chunk_view(ap, width):
    return ap.rearrange("p (c j) -> p c j", j=width)


def _mk_ctx(nc, es, layers):
    ctx = Ctx()
    ctx.nc = nc
    ctx.es = es
    ctx.layers = layers
    for s in ("ld", "st", "mm", "ve", "se", "gp"):
        setattr(ctx, s, Sem(es, nc, "sem_" + s))
    return ctx


def _consts(ctx):
    nc, es = ctx.nc, ctx.es
    ctx.ident_b = es.enter_context(nc.sbuf_tensor(nm("ident_b"), [128, 128], BF16))
    ctx.ident_f = es.enter_context(nc.sbuf_tensor(nm("ident_f"), [128, 128], F32))
    ctx.mask = es.enter_context(nc.sbuf_tensor(nm("maskstrip"), [128, 1280], F32))
    ctx.sel = es.enter_context(nc.sbuf_tensor(nm("sel"), [16, 2048], F32))
    ctx.ones_col_b = es.enter_context(nc.sbuf_tensor(nm("ones_col_b"), [128, 1], BF16))
    ctx.ones_row_b = es.enter_context(nc.sbuf_tensor(nm("ones_row_b"), [1, 128], BF16))
    ctx.ones_row_f = es.enter_context(nc.sbuf_tensor(nm("ones_row_f"), [1, 128], F32))
    ctx.one11_f = es.enter_context(nc.sbuf_tensor(nm("one11_f"), [1, 1], F32))
    ctx.eps_col = es.enter_context(nc.sbuf_tensor(nm("eps_col"), [128, 1], F32))
    with ctx.nc.Block() as block:
        @block.gpsimd
        def _(g):
            for t in (ctx.ident_b, ctx.ident_f):
                g.memset(t[:], 0.0)
                g.affine_select(out=t[:], in_=t[:],
                                compare_op=ALU.not_equal, fill=1.0, base=0,
                                pattern=[[-1, 128]], channel_multiplier=1)
            g.memset(ctx.mask[:], 0.0)
            g.affine_select(out=ctx.mask[:], in_=ctx.mask[:],
                            compare_op=ALU.is_ge, fill=NEGBIG, base=-384,
                            pattern=[[1, 1280]], channel_multiplier=-1)
            g.memset(ctx.sel[:], 0.0)
            g.affine_select(
                out=chunk_view(ctx.sel[:], 128),
                in_=chunk_view(ctx.sel[:], 128),
                compare_op=ALU.not_equal, fill=1.0, base=0,
                pattern=[[-1, 16], [0, 128]], channel_multiplier=1)
            g.memset(ctx.ones_col_b[:], 1.0)
            g.memset(ctx.ones_row_b[:], 1.0)
            g.memset(ctx.ones_row_f[:], 1.0)
            g.memset(ctx.one11_f[:], 1.0)
            ctx.gp.inc(g.memset(ctx.eps_col[:], EPS))



def _prep_cast(ctx, src2d, dst, R, C):
    nc = ctx.nc
    ld, ve, st = ctx.ld, ctx.ve, ctx.st
    ph = min(128, R)
    nch = max(1, R // 128)
    kmax = max(1, min(nch, 16384 // C))
    groups = [(g, min(kmax, nch - g)) for g in range(0, nch, kmax)]
    src = src2d.rearrange("(c p) n -> p c n", p=ph)
    dstv = dst[:, :].rearrange("(c p) n -> p c n", p=ph)
    with (
        nc.sbuf_tensor(nm("pc_f0"), [ph, kmax * C], F32) as sf0,
        nc.sbuf_tensor(nm("pc_f1"), [ph, kmax * C], F32) as sf1,
        nc.sbuf_tensor(nm("pc_b0"), [ph, kmax * C], BF16) as sb0,
        nc.sbuf_tensor(nm("pc_b1"), [ph, kmax * C], BF16) as sb1,
        nc.Block() as block,
    ):
        sfs, sbs = [sf0, sf1], [sb0, sb1]
        st_marks = {}
        ld_marks = {}
        ve_marks = {}
        for gi, (g0, k) in enumerate(groups):
            @block.sync
            def _(sync, gi=gi, g0=g0, k=k):
                if gi >= 2:
                    sync.wait_ge(st.h, st_marks[gi - 2])
                sync.dma_start(out=chunk_view(sfs[gi % 2][:, :k * C], C),
                               in_=src[:, g0:g0 + k, :]).then_inc(ld.h, 16)
                ld.n += 16
                ld_marks[gi] = ld.n

            @block.vector
            def _(v, gi=gi, g0=g0, k=k):
                v.wait_ge(ld.h, ld_marks[gi])
                ve.inc(v.tensor_copy(sbs[gi % 2][:, :k * C],
                                     sfs[gi % 2][:, :k * C]))
                ve_marks[gi] = ve.n

            @block.scalar
            def _(s, gi=gi, g0=g0, k=k):
                s.wait_ge(ve.h, ve_marks[gi])
                s.dma_start(out=dstv[:, g0:g0 + k, :],
                            in_=chunk_view(sbs[gi % 2][:, :k * C], C)
                            ).then_inc(st.h, 16)
                st.n += 16
                st_marks[gi] = st.n

        @block.sync
        def _(sync):
            st.wait(sync)


def _prep_transpose(ctx, src2d, dst, R, C):
    nc = ctx.nc
    ld, ve, se, mm, st = ctx.ld, ctx.ve, ctx.se, ctx.mm, ctx.st
    ph = min(128, R)
    nch = max(1, R // 128)
    ncc = C // 128
    kmax = max(1, min(nch, 16384 // C))
    groups = [(g, min(kmax, nch - g)) for g in range(0, nch, kmax)]
    src = src2d.rearrange("(c p) n -> p c n", p=ph)
    with (
        nc.sbuf_tensor(nm("pt_f"), [ph, kmax * C], F32) as sf,
        nc.sbuf_tensor(nm("pt_b"), [ph, kmax * C], BF16) as sb,
        nc.sbuf_tensor(nm("pt_o"), [128, ncc * R], BF16) as so,
        nc.psum_tensor(nm("pt_p0"), [128, 1024], BF16) as p0,
        nc.psum_tensor(nm("pt_p1"), [128, 1024], BF16) as p1,
        nc.Block() as block,
    ):
        pb = [p0, p1]
        fill = [0]
        evac_done = [0, 0]
        ov = so[:].rearrange("p (c r) -> p c r", r=R)
        for gi, (g0, k) in enumerate(groups):
            @block.sync
            def _(sync, g0=g0, k=k):
                sync.dma_start(out=chunk_view(sf[:, :k * C], C),
                               in_=src[:, g0:g0 + k, :]).then_inc(ld.h, 16)
                ld.n += 16

            @block.vector
            def _(v, g0=g0, k=k):
                ld.wait(v)
                if gi > 0:
                    v.wait_ge(mm.h, mm.n)
                ve.inc(v.tensor_copy(sb[:, :k * C], sf[:, :k * C]))

            @block.sync
            def _(sync):
                ve.wait(sync)

            for rr in range(g0, g0 + k):
                for cc0 in range(0, ncc, 8):
                    w = min(8, ncc - cc0)
                    fi = fill[0] % 2

                    @block.tensor
                    def _(t, rr=rr, cc0=cc0, w=w, fi=fi, g0=g0):
                        ve.wait(t)
                        t.wait_ge(se.h, evac_done[fi])
                        ins = None
                        for j in range(w):
                            cc = cc0 + j
                            ins = t.transpose(
                                pb[fi][:, j * 128:(j + 1) * 128],
                                sb[:, (rr - g0) * C + cc * 128:
                                       (rr - g0) * C + (cc + 1) * 128],
                                ctx.ident_b[:ph, :])
                        mm.inc(ins)

                    @block.scalar
                    def _(s, rr=rr, cc0=cc0, w=w, fi=fi):
                        mm.wait(s)
                        se.inc(s.copy(
                            ov[:, cc0:cc0 + w, rr * ph:(rr + 1) * ph],
                            chunk_view(pb[fi][:, :w * 128], 128)[:, :, :ph]))
                        evac_done[fi] = se.n
                    fill[0] += 1

        @block.sync
        def _(sync):
            se.wait(sync)
            sync.dma_start(out=dst[:, :].rearrange("(c p) r -> p c r", p=128),
                           in_=ov).then_inc(st.h, 16)
            st.n += 16
            st.wait(sync)


def _prep_weights(ctx, inp):
    nc = ctx.nc
    w = {}

    def scr(name, shape):
        w[name] = nc.dram_tensor("scr_" + name, shape, BF16, kind="Internal")
        return w[name]

    if 0 in ctx.layers:
        _prep_transpose(ctx, inp["L0_attn_Wq"][:, :], scr("wqT0", [D, D]), D, D)
        _prep_transpose(ctx, inp["L0_attn_Wk"][:, :], scr("wkT0", [D, D]), D, D)
        _prep_transpose(ctx, inp["L0_attn_Wv"][:, :], scr("wvT0", [D, D]), D, D)
        _prep_transpose(ctx, inp["L0_attn_Wo"][:, :], scr("woT0", [D, D]), D, D)
        _prep_cast(ctx, inp["L0_ffn_router_w"][:, :], scr("rt0", [D, NE0]), D, NE0)
        scr("gw0", [NE0, D, F0]); scr("uw0", [NE0, D, F0]); scr("dw0", [NE0, F0, D])
        for e in range(NE0):
            _prep_cast(ctx, inp["L0_ffn_gate_w"][e, :, :], w["gw0"][e, :, :], D, F0)
            _prep_cast(ctx, inp["L0_ffn_up_w"][e, :, :], w["uw0"][e, :, :], D, F0)
            _prep_cast(ctx, inp["L0_ffn_down_w"][e, :, :], w["dw0"][e, :, :], F0, D)
    if 1 in ctx.layers:
        _prep_cast(ctx, inp["L1_attn_Wq"][:, :], scr("wq1", [D, D]), D, D)
        _prep_cast(ctx, inp["L1_attn_Wk"][:, :], scr("wk1", [D, D]), D, D)
        _prep_cast(ctx, inp["L1_attn_Wv"][:, :], scr("wv1", [D, D]), D, D)
        _prep_cast(ctx, inp["L1_attn_Wo"][:, :], scr("wo1", [D, D]), D, D)
        _prep_transpose(ctx, inp["L1_ffn_gate_w"][:, :], scr("gT1", [D, NE1]), NE1, D)
        scr("w1T", [NE1, D, I1]); scr("w3T", [NE1, D, I1]); scr("w2T", [NE1, I1, D])
        for e in range(NE1):
            _prep_transpose(ctx, inp["L1_ffn_W1"][e, :, :], w["w1T"][e, :, :], I1, D)
            _prep_transpose(ctx, inp["L1_ffn_W3"][e, :, :], w["w3T"][e, :, :], I1, D)
            _prep_transpose(ctx, inp["L1_ffn_W2"][e, :, :], w["w2T"][e, :, :], D, I1)
        _prep_transpose(ctx, inp["L1_ffn_Ws1"][:, :], scr("ws1T", [D, I1]), I1, D)
        _prep_transpose(ctx, inp["L1_ffn_Ws3"][:, :], scr("ws3T", [D, I1]), I1, D)
        _prep_transpose(ctx, inp["L1_ffn_Ws2"][:, :], scr("ws2T", [I1, D]), D, I1)
    ctx.w = w



def _ln(ctx, b, x_src):
    nc = ctx.nc
    ld, ve, se, mm, st = ctx.ld, ctx.ve, ctx.se, ctx.mm, ctx.st
    with (
        nc.sbuf_tensor(nm("ln_x0"), [128, D], F32) as sx0,
        nc.sbuf_tensor(nm("ln_x1"), [128, D], F32) as sx1,
        nc.sbuf_tensor(nm("ln_sq"), [128, D], F32) as ssq,
        nc.sbuf_tensor(nm("ln_st"), [128, 16], F32) as sst,
        nc.sbuf_tensor(nm("ln_xn"), [128, D], BF16) as sxn,
        nc.psum_tensor(nm("ln_p0"), [128, 1024], BF16) as p0,
        nc.psum_tensor(nm("ln_p1"), [128, 1024], BF16) as p1,
        nc.Block() as block,
    ):
        sx = [sx0, sx1]
        pb = [p0, p1]
        xn_done = [0, 0]
        bank_free = [0, 0]
        xnv = chunk_view(ctx.xnT[:], S)
        for tc in range(TC):
            @block.sync
            def _(sync, tc=tc):
                sync.wait_ge(ve.h, xn_done[tc % 2])
                sync.dma_start(out=sx[tc % 2][:],
                               in_=x_src[b, tc * 128:(tc + 1) * 128, :]
                               ).then_inc(ld.h, 16)
                ld.n += 16

            o = 8 * (tc % 2)

            @block.scalar
            def _(s, tc=tc, o=o):
                ld.wait(s)
                se.inc(s.activation(ssq[:], sx[tc % 2][:], AF.Square,
                                    accum_out=sst[:, o + 1:o + 2]))

            @block.vector
            def _(v, tc=tc, o=o):
                ld.wait(v)
                ve.flush(v, v.reduce_sum(sst[:, o:o + 1], sx[tc % 2][:], axis=AX))
                se.wait(v)
                ve.flush(v, v.tensor_scalar_mul(sst[:, o + 2:o + 3],
                                                sst[:, o:o + 1], 1.0 / D))
                v.tensor_scalar_mul(sst[:, o + 3:o + 4], sst[:, o + 1:o + 2],
                                    1.0 / D)
                ve.flush(v, v.tensor_mul(sst[:, o + 4:o + 5], sst[:, o + 2:o + 3],
                                         sst[:, o + 2:o + 3]))
                ve.inc(v.tensor_sub(sst[:, o + 5:o + 6], sst[:, o + 3:o + 4],
                                    sst[:, o + 4:o + 5]))

            @block.scalar
            def _(s, tc=tc, o=o):
                ve.wait(s)
                se.inc(s.activation(sst[:, o + 6:o + 7], sst[:, o + 5:o + 6],
                                    AF.Sqrt, bias=ctx.eps_col[:]))

            @block.vector
            def _(v, tc=tc, o=o):
                se.wait(v)
                ve.flush(v, v.reciprocal(sst[:, o + 7:o + 8], sst[:, o + 6:o + 7]))
                ve.inc(v.tensor_scalar(sxn[:], sx[tc % 2][:], sst[:, o + 2:o + 3],
                                       sst[:, o + 7:o + 8], op0=ALU.subtract,
                                       op1=ALU.mult))
                xn_done[tc % 2] = ve.n

            for fi in range(2):
                @block.tensor
                def _(t, tc=tc, fi=fi):
                    ve.wait(t)
                    t.wait_ge(ve.h, bank_free[fi])
                    ins = None
                    for j in range(8):
                        cc = fi * 8 + j
                        ins = t.transpose(pb[fi][:, j * 128:(j + 1) * 128],
                                          sxn[:, cc * 128:(cc + 1) * 128],
                                          ctx.ident_b[:])
                    mm.inc(ins)

                @block.vector
                def _(v, tc=tc, fi=fi):
                    mm.wait(v)
                    ve.inc(v.tensor_copy(
                        xnv[:, fi * 8:fi * 8 + 8, tc * 128:(tc + 1) * 128],
                        chunk_view(pb[fi][:], 128)))
                    bank_free[fi] = ve.n

        @block.sync
        def _(sync):
            ve.wait(sync)
            sync.dma_start(
                out=ctx.xnt_scr[b, :, :].rearrange("(c p) t -> p c t", p=128),
                in_=xnv).then_inc(st.h, 16)
            st.n += 16
            st.wait(sync)



def _router_l0(ctx, b):
    nc = ctx.nc
    ld, ve, se, mm, st = ctx.ld, ctx.ve, ctx.se, ctx.mm, ctx.st
    E = NE0
    with (
        nc.sbuf_tensor(nm("rt_w"), [128, DC * E], BF16) as srt,
        nc.sbuf_tensor(nm("rt_lg"), [128, E], F32) as slg,
        nc.sbuf_tensor(nm("rt_t"), [128, 5 * E], F32) as stp,
        nc.sbuf_tensor(nm("rt_s"), [128, 4], F32) as sst,
        nc.sbuf_tensor(nm("rt_cT"), [16, S], F32) as scT,
        nc.psum_tensor(nm("rt_pl"), [128, E], F32) as plg,
        nc.psum_tensor(nm("rt_pt"), [16, 128], F32) as ptr,
        nc.Block() as block,
    ):
        @block.sync
        def _(sync):
            sync.dma_start(out=chunk_view(srt[:], E),
                           in_=ctx.w["rt0"][:, :].rearrange("(c p) n -> p c n", p=128)
                           ).then_inc(ld.h, 16)
            ld.n += 16

        plg_free = [0]
        ptr_free = [0]
        for tc in range(TC):
            @block.tensor
            def _(t, tc=tc):
                ld.wait(t)
                t.wait_ge(ve.h, plg_free[0])
                ins = None
                for kc in range(DC):
                    ins = t.matmul(plg[:],
                                   ctx.xnT[:, kc * S + tc * 128:
                                           kc * S + (tc + 1) * 128],
                                   srt[:, kc * E:(kc + 1) * E],
                                   start=(kc == 0), stop=(kc == DC - 1))
                mm.inc(ins)

            @block.vector
            def _(v, tc=tc):
                mm.wait(v)
                ve.flush(v, v.tensor_copy(slg[:], plg[:]))
                plg_free[0] = ve.n
                ve.flush(v, v.reduce_max(sst[:, 0:1], slg[:], axis=AX,
                                         negate=True))
                ve.flush(v, v.tensor_scalar(stp[:, 3 * E:4 * E], slg[:],
                                            sst[:, 0:1], 0.0,
                                            op0=ALU.add, op1=ALU.is_ge))
                ve.flush(v, v.scalar_tensor_tensor(stp[:, 4 * E:5 * E],
                                                   stp[:, 3 * E:4 * E], NEGBIG,
                                                   slg[:],
                                                   op0=ALU.mult, op1=ALU.add))
                ve.inc(v.reduce_max(sst[:, 1:2], stp[:, 4 * E:5 * E], axis=AX))

            @block.scalar
            def _(s, tc=tc):
                ve.wait(s)
                se.inc(s.activation(stp[:, 0:E], slg[:], AF.Exp, bias=sst[:, 0:1]))

            @block.vector
            def _(v, tc=tc):
                ve.wait(v)
                se.wait(v)
                ve.flush(v, v.tensor_scalar(stp[:, E:2 * E], slg[:], sst[:, 1:2],
                                            None, op0=ALU.is_ge))
                ve.flush(v, v.tensor_mul(stp[:, 2 * E:3 * E], stp[:, 0:E],
                                         stp[:, E:2 * E]))
                ve.flush(v, v.reduce_sum(sst[:, 2:3], stp[:, 2 * E:3 * E],
                                         axis=AX))
                ve.flush(v, v.reciprocal(sst[:, 3:4], sst[:, 2:3]))
                ve.inc(v.tensor_scalar_mul(stp[:, 2 * E:3 * E],
                                           stp[:, 2 * E:3 * E], sst[:, 3:4]))

            @block.tensor
            def _(t, tc=tc):
                ve.wait(t)
                t.wait_ge(ve.h, ptr_free[0])
                mm.inc(t.transpose(ptr[:], stp[:, 2 * E:3 * E], ctx.ident_f[:]))

            @block.vector
            def _(v, tc=tc):
                mm.wait(v)
                ve.inc(v.tensor_copy(scT[:, tc * 128:(tc + 1) * 128], ptr[:]))
                ptr_free[0] = ve.n

        @block.sync
        def _(sync):
            ve.wait(sync)
            sync.dma_start(out=ctx.cmb_scr[:, b * S:(b + 1) * S], in_=scT[:]
                           ).then_inc(st.h, 16)
            st.n += 16
            st.wait(sync)


def _gate_l1(ctx, b, inp):
    nc = ctx.nc
    ld, ve, se, mm, st = ctx.ld, ctx.ve, ctx.se, ctx.mm, ctx.st
    E = NE1
    with (
        nc.sbuf_tensor(nm("gt_w"), [128, DC * E], BF16) as sgw,
        nc.sbuf_tensor(nm("gt_br"), [1, E], F32) as sgbr,
        nc.sbuf_tensor(nm("gt_b"), [128, E], F32) as sgb,
        nc.sbuf_tensor(nm("gt_lg"), [128, E], F32) as slg,
        nc.sbuf_tensor(nm("gt_t"), [128, 6 * E], F32) as stp,
        nc.sbuf_tensor(nm("gt_s"), [128, 4], F32) as sst,
        nc.sbuf_tensor(nm("gt_cT"), [E, S], F32) as scT,
        nc.psum_tensor(nm("gt_pl"), [128, E], F32) as plg,
        nc.psum_tensor(nm("gt_pt"), [16, 128], F32) as ptr,
        nc.Block() as block,
    ):
        @block.sync
        def _(sync):
            sync.dma_start(out=chunk_view(sgw[:], E),
                           in_=ctx.w["gT1"][:, :].rearrange("(c p) n -> p c n", p=128)
                           ).then_inc(ld.h, 16)
            sync.dma_start(out=sgbr[:], in_=inp["L1_ffn_gate_b"][None, :]
                           ).then_inc(ld.h, 16)
            ld.n += 32

        @block.tensor
        def _(t):
            ld.wait(t)
            mm.inc(t.matmul(plg[:], ctx.ones_row_f[:], sgbr[:],
                            start=True, stop=True))

        @block.vector
        def _(v):
            mm.wait(v)
            ve.inc(v.tensor_copy(sgb[:], plg[:]))

        plg_free = [ve.n + 0]
        ptr_free = [0]
        plg_free[0] = ve.n
        for tc in range(TC):
            @block.tensor
            def _(t, tc=tc):
                t.wait_ge(ve.h, plg_free[0])
                ins = None
                for kc in range(DC):
                    ins = t.matmul(plg[:],
                                   ctx.xnT[:, kc * S + tc * 128:
                                           kc * S + (tc + 1) * 128],
                                   sgw[:, kc * E:(kc + 1) * E],
                                   start=(kc == 0), stop=(kc == DC - 1))
                mm.inc(ins)

            @block.vector
            def _(v, tc=tc):
                mm.wait(v)
                ve.flush(v, v.tensor_copy(slg[:], plg[:]))
                plg_free[0] = ve.n
                ve.inc(v.reduce_max(sst[:, 0:1], slg[:], axis=AX, negate=True))

            @block.scalar
            def _(s, tc=tc):
                ve.wait(s)
                se.inc(s.activation(stp[:, 0:E], slg[:], AF.Exp, bias=sst[:, 0:1],
                                    accum_out=sst[:, 1:2]))

            @block.vector
            def _(v, tc=tc):
                se.wait(v)
                ve.flush(v, v.reciprocal(sst[:, 2:3], sst[:, 1:2]))
                ve.flush(v, v.tensor_scalar_mul(stp[:, 0:E], stp[:, 0:E],
                                                sst[:, 2:3]))
                ve.flush(v, v.tensor_add(stp[:, E:2 * E], stp[:, 0:E], sgb[:]))
                ve.flush(v, v.reduce_max(sst[:, 3:4], stp[:, E:2 * E], axis=AX))
                ve.flush(v, v.tensor_scalar(stp[:, 3 * E:4 * E],
                                            stp[:, E:2 * E],
                                            sst[:, 3:4], None, op0=ALU.is_ge))
                ve.flush(v, v.scalar_tensor_tensor(stp[:, 4 * E:5 * E],
                                                   stp[:, 3 * E:4 * E], NEGBIG,
                                                   stp[:, E:2 * E],
                                                   op0=ALU.mult, op1=ALU.add))
                ve.flush(v, v.reduce_max(sst[:, 3:4], stp[:, 4 * E:5 * E],
                                         axis=AX))
                ve.flush(v, v.tensor_scalar(stp[:, 5 * E:6 * E],
                                            stp[:, E:2 * E],
                                            sst[:, 3:4], None, op0=ALU.is_ge))
                ve.inc(v.tensor_mul(stp[:, 2 * E:3 * E], stp[:, 0:E],
                                    stp[:, 5 * E:6 * E]))

            @block.tensor
            def _(t, tc=tc):
                ve.wait(t)
                t.wait_ge(ve.h, ptr_free[0])
                mm.inc(t.transpose(ptr[:E, :], stp[:, 2 * E:3 * E],
                                   ctx.ident_f[:]))

            @block.vector
            def _(v, tc=tc):
                mm.wait(v)
                ve.inc(v.tensor_copy(scT[:, tc * 128:(tc + 1) * 128], ptr[:E, :]))
                ptr_free[0] = ve.n

        @block.sync
        def _(sync):
            ve.wait(sync)
            sync.dma_start(out=ctx.dw1_scr[:, b * S:(b + 1) * S], in_=scT[:]
                           ).then_inc(st.h, 16)
            st.n += 16
            st.wait(sync)



def _bias_cols(ctx, block, rows_sb, cols_sb, p_b, n):
    mm, ve = ctx.mm, ctx.ve

    @block.tensor
    def _(t):
        ctx.ld.wait(t)
        ins = None
        for k in range(n):
            for c in range(DC):
                ins = t.matmul(p_b[:, k * DC + c:k * DC + c + 1],
                               rows_sb[0:1, k * D + c * 128:k * D + (c + 1) * 128],
                               ctx.one11_f[:], start=True, stop=True)
        mm.inc(ins)

    @block.vector
    def _(v):
        mm.wait(v)
        ve.inc(v.tensor_copy(cols_sb[:], p_b[:, :n * DC]))


def _attn_l0(ctx, b, inp):
    nc = ctx.nc
    ld, ve, se, mm, st = ctx.ld, ctx.ve, ctx.se, ctx.mm, ctx.st
    _es = ExitStack()
    with _es:
        sbrow = _es.enter_context(nc.sbuf_tensor(nm("a0_brow"), [1, 3 * D], F32))
        sbvr = _es.enter_context(nc.sbuf_tensor(nm("a0_bvrow"), [1, D], BF16))
        sbc = _es.enter_context(nc.sbuf_tensor(nm("a0_bcols"), [128, 3 * DC], F32))
        swq = _es.enter_context(nc.sbuf_tensor(nm("a0_wq"), [128, 2 * DC * 128], BF16))
        swk = _es.enter_context(nc.sbuf_tensor(nm("a0_wk"), [128, 2 * DC * 128], BF16))
        swv = _es.enter_context(nc.sbuf_tensor(nm("a0_wv"), [128, 2 * DC * 128], BF16))
        sqT = _es.enter_context(nc.sbuf_tensor(nm("a0_qT"), [128, S], BF16))
        skT = _es.enter_context(nc.sbuf_tensor(nm("a0_kT"), [128, S], BF16))
        svn = _es.enter_context(nc.sbuf_tensor(nm("a0_vn"), [128, S], BF16))
        sexp = _es.enter_context(nc.sbuf_tensor(nm("a0_exp"), [128, 16 * 512], BF16))
        srcp = _es.enter_context(nc.sbuf_tensor(nm("a0_rcp"), [1, S], F32))
        sbcn = _es.enter_context(nc.sbuf_tensor(nm("a0_bcn"), [128, 512], F32))
        sctxT = _es.enter_context(nc.sbuf_tensor(nm("a0_ctxT"), [128, S], BF16))
        pbig = _es.enter_context(nc.psum_tensor(nm("a0_pbig"), [128, 2048], F32))
        ps0 = _es.enter_context(nc.psum_tensor(nm("a0_ps0"), [128, 512], F32))
        ps1 = _es.enter_context(nc.psum_tensor(nm("a0_ps1"), [128, 512], F32))
        pdn = _es.enter_context(nc.psum_tensor(nm("a0_pdn"), [1, 512], F32))
        pctx = _es.enter_context(nc.psum_tensor(nm("a0_pctx"), [128, 512], F32))
        ps = [ps0, ps1]
        HW = DC * 128

        def load_head(sync, h):
            for sb_, wname in ((swq, "wqT0"), (swk, "wkT0"), (swv, "wvT0")):
                sync.dma_start(
                    out=chunk_view(sb_[:, (h % 2) * HW:(h % 2 + 1) * HW], 128),
                    in_=ctx.w[wname][:, h * 128:(h + 1) * 128]
                    .rearrange("(c p) j -> p c j", p=128)).then_inc(ld.h, 16)
                ld.n += 16
            return ld.n

        head_mark = {}
        with nc.Block() as block:
            @block.sync
            def _(sync):
                for k, bn in enumerate(["L0_attn_bq", "L0_attn_bk", "L0_attn_bv"]):
                    sync.dma_start(out=sbrow[0:1, k * D:(k + 1) * D],
                                   in_=inp[bn][None, :]).then_inc(ld.h, 16)
                    ld.n += 16
                head_mark[0] = load_head(sync, 0)
            _bias_cols(ctx, block, sbrow, sbc, pbig, 3)

            @block.scalar
            def _(s):
                ld.wait(s)
                se.inc(s.copy(sbvr[:], sbrow[0:1, 2 * D:3 * D]))

        ctxT_stored = [st.n]
        for h in range(NH):
            with nc.Block() as block:
                po = (h % 2) * HW

                @block.sync
                def _(sync, h=h):
                    if h + 1 < NH:
                        head_mark[h + 1] = load_head(sync, h + 1)

                for rnd, (wsb, dst, bcol) in enumerate(
                        ((swq, sqT, 0), (swk, skT, 1))):
                    @block.tensor
                    def _(t, rnd=rnd, wsb=wsb, h=h, po=po):
                        t.wait_ge(ld.h, head_mark[h])
                        vebase = ve.n
                        for nt in range(4):
                            t.wait_ge(ve.h, vebase - 3 + nt)
                            ins = None
                            for kc in range(DC):
                                ins = t.matmul(
                                    pbig[:, nt * 512:(nt + 1) * 512],
                                    wsb[:, po + kc * 128:po + (kc + 1) * 128],
                                    ctx.xnT[:, kc * S + nt * 512:
                                            kc * S + (nt + 1) * 512],
                                    start=(kc == 0), stop=(kc == DC - 1))
                            mm.inc(ins)

                    @block.vector
                    def _(v, rnd=rnd, dst=dst, bcol=bcol, h=h):
                        for nt in range(4):
                            v.wait_ge(mm.h, mm.n - 3 + nt)
                            vi = v.tensor_scalar(
                                dst[:, nt * 512:(nt + 1) * 512],
                                pbig[:, nt * 512:(nt + 1) * 512],
                                sbc[:, bcol * DC + h:bcol * DC + h + 1], None,
                                op0=ALU.add)
                            ve.inc(vi)

                @block.tensor
                def _(t, h=h):
                    ve.wait(t)
                    se.wait(t)
                    for grp in range(4):
                        ins = None
                        for tc in range(grp * 4, grp * 4 + 4):
                            for kc in range(DC):
                                ins = t.matmul(
                                    pbig[:, tc * 128:(tc + 1) * 128],
                                    ctx.xnT[:, kc * S + tc * 128:
                                            kc * S + (tc + 1) * 128],
                                    swv[:, po + kc * 128:po + (kc + 1) * 128],
                                    start=(kc == 0), stop=False)
                            ins = t.matmul(
                                pbig[:, tc * 128:(tc + 1) * 128],
                                ctx.ones_row_b[:],
                                sbvr[0:1, h * 128:(h + 1) * 128],
                                start=False, stop=True)
                        mm.inc(ins)

                @block.vector
                def _(v, h=h):
                    for grp in range(4):
                        v.wait_ge(mm.h, mm.n - 3 + grp)
                        ve.inc(v.tensor_copy(svn[:, grp * 512:(grp + 1) * 512],
                                             pbig[:, grp * 512:(grp + 1) * 512]))

                exp_done = {}
                bank_free = [ve.n, ve.n]
                bank_free_sem = [None, None]
                pctx_free = [0]
                pdn_free = [0]
                for j in range(4):
                    imax = 4 * j + 4
                    se0 = se.n

                    @block.tensor
                    def _(t, j=j, imax=imax, se0=se0):
                        ve.wait(t)
                        for i in range(imax):
                            if i >= 2:
                                t.wait_ge(se.h, se0 + i - 1)
                            if i < 2 and j > 0:
                                t.wait_ge(se.h, exp_done[(j - 1, "bcn")])
                            ins = t.matmul(ps[i % 2][:],
                                           skT[:, i * 128:(i + 1) * 128],
                                           sqT[:, j * 512:(j + 1) * 512],
                                           start=True, stop=True)
                            mm.inc(ins)

                    @block.vector
                    def _(v, j=j, imax=imax):
                        for i in range(4 * j, imax):
                            v.wait_ge(mm.h, mm.n - (imax - 1 - i))
                            c = 512 * j - 128 * i
                            ve.inc(v.tensor_add(
                                ps[i % 2][:], ps[i % 2][:],
                                ctx.mask[:, c + 384:c + 384 + 512]))

                    @block.scalar
                    def _(s, j=j, imax=imax):
                        mmbase = mm.n - imax
                        vebase = ve.n - 4
                        for i in range(imax):
                            if i < 4 * j:
                                s.wait_ge(mm.h, mmbase + i + 1)
                            else:
                                s.wait_ge(ve.h, vebase + (i - 4 * j) + 1)
                            se.inc(s.activation(sexp[:, i * 512:(i + 1) * 512],
                                                ps[i % 2][:], AF.Exp, scale=ISQ))
                            exp_done[(j, i)] = se.n

                    @block.tensor
                    def _(t, j=j, imax=imax):
                        t.wait_ge(ve.h, pdn_free[0])
                        t.wait_ge(ve.h, pctx_free[0])
                        ins = None
                        for i in range(imax):
                            t.wait_ge(se.h, exp_done[(j, i)])
                            t.matmul(pdn[:], ctx.ones_col_b[:],
                                     sexp[:, i * 512:(i + 1) * 512],
                                     start=(i == 0), stop=(i == imax - 1))
                            ins = t.matmul(pctx[:],
                                           svn[:, i * 128:(i + 1) * 128],
                                           sexp[:, i * 512:(i + 1) * 512],
                                           start=(i == 0), stop=(i == imax - 1))
                        mm.inc(ins)

                    @block.vector
                    def _(v, j=j):
                        mm.wait(v)
                        ve.inc(v.reciprocal(srcp[0:1, j * 512:(j + 1) * 512],
                                            pdn[:]))
                        pdn_free[0] = ve.n

                    @block.tensor
                    def _(t, j=j):
                        ve.wait(t)
                        mm.inc(t.matmul(ps0[:], ctx.ones_row_f[:],
                                        srcp[0:1, j * 512:(j + 1) * 512],
                                        start=True, stop=True))

                    @block.scalar
                    def _(s, j=j):
                        mm.wait(s)
                        se.inc(s.copy(sbcn[:], ps0[:]))
                        exp_done[(j, "bcn")] = se.n

                    @block.vector
                    def _(v, j=j, h=h):
                        se.wait(v)
                        if j == 0:
                            v.wait_ge(st.h, ctxT_stored[0])
                        ve.inc(v.tensor_mul(sctxT[:, j * 512:(j + 1) * 512],
                                            pctx[:], sbcn[:]))
                        pctx_free[0] = ve.n

                @block.sync
                def _(sync, h=h):
                    ve.wait(sync)
                    sync.dma_start(out=ctx.ctxt_scr[b, h * 128:(h + 1) * 128, :],
                                   in_=sctxT[:]).then_inc(st.h, 16)
                    st.n += 16
                    ctxT_stored[0] = st.n



def _attn_l1(ctx, b, inp):
    nc = ctx.nc
    ld, ve, se, mm, st = ctx.ld, ctx.ve, ctx.se, ctx.mm, ctx.st
    if b == 0:
        with (
            nc.sbuf_tensor(nm("b_raw"), [16, 128], F32) as sraw,
            nc.sbuf_tensor(nm("b_sig"), [16, 128], F32) as ssig,
            nc.psum_tensor(nm("b_p"), [128, 128], F32) as pbt,
            nc.Block() as block,
        ):
            @block.sync
            def _(sync):
                sync.dma_start(out=sraw[:], in_=inp["L1_attn_betas"][0, :, 0, :]
                               ).then_inc(ld.h, 16)
                ld.n += 16

            @block.scalar
            def _(s):
                ld.wait(s)
                se.inc(s.activation(ssig[:], sraw[:], AF.Sigmoid))

            pbt_free = [0]
            for h in range(NH):
                @block.tensor
                def _(t, h=h):
                    se.wait(t)
                    t.wait_ge(ve.h, pbt_free[0])
                    mm.inc(t.matmul(pbt[:], ctx.sel[:, h * 128:(h + 1) * 128],
                                    ssig[:], start=True, stop=True))

                @block.vector
                def _(v, h=h):
                    mm.wait(v)
                    ve.inc(v.tensor_copy(ctx.beta[:, h * 128:(h + 1) * 128],
                                         pbt[:]))
                    pbt_free[0] = ve.n

    _es = ExitStack()
    with _es:
        swq = _es.enter_context(nc.sbuf_tensor(nm("a1_wq"), [128, 2 * DC * 128], BF16))
        swk = _es.enter_context(nc.sbuf_tensor(nm("a1_wk"), [128, 2 * DC * 128], BF16))
        swv = _es.enter_context(nc.sbuf_tensor(nm("a1_wv"), [128, 2 * DC * 128], BF16))
        sqT = _es.enter_context(nc.sbuf_tensor(nm("a1_qT"), [128, S], BF16))
        skT = _es.enter_context(nc.sbuf_tensor(nm("a1_kT"), [128, S], BF16))
        svT = _es.enter_context(nc.sbuf_tensor(nm("a1_vT"), [128, S], BF16))
        svn = _es.enter_context(nc.sbuf_tensor(nm("a1_vn"), [128, S], BF16))
        ssqT = _es.enter_context(nc.sbuf_tensor(nm("a1_sqT"), [128, S], BF16))
        sskT = _es.enter_context(nc.sbuf_tensor(nm("a1_skT"), [128, S], BF16))
        sskn = _es.enter_context(nc.sbuf_tensor(nm("a1_skn"), [128, S], BF16))
        st1 = _es.enter_context(nc.sbuf_tensor(nm("a1_t1"), [128, S], F32))
        st2 = _es.enter_context(nc.sbuf_tensor(nm("a1_t2"), [128, S], F32))
        sexp = _es.enter_context(nc.sbuf_tensor(nm("a1_exp"), [128, 4 * 512], BF16))
        smem = _es.enter_context(nc.sbuf_tensor(nm("a1_mem"), [128, 128], F32))
        smembf = _es.enter_context(nc.sbuf_tensor(nm("a1_membf"), [128, 128], BF16))
        sz = _es.enter_context(nc.sbuf_tensor(nm("a1_z"), [128, 1], F32))
        szbf = _es.enter_context(nc.sbuf_tensor(nm("a1_zbf"), [128, 1], BF16))
        srn = _es.enter_context(nc.sbuf_tensor(nm("a1_rn"), [128, 12], F32))
        sdlt = _es.enter_context(nc.sbuf_tensor(nm("a1_dlt"), [128, 512], BF16))
        sam = _es.enter_context(nc.sbuf_tensor(nm("a1_am"), [128, 128], F32))
        sad = _es.enter_context(nc.sbuf_tensor(nm("a1_ad"), [128, 128], F32))
        sdf = _es.enter_context(nc.sbuf_tensor(nm("a1_df"), [128, 128], F32))
        satt = _es.enter_context(nc.sbuf_tensor(nm("a1_att"), [128, 128], F32))
        sattT = _es.enter_context(nc.sbuf_tensor(nm("a1_attT"), [128, S], BF16))
        pbig = _es.enter_context(nc.psum_tensor(nm("a1_pbig"), [128, 2048], F32))
        ps0 = _es.enter_context(nc.psum_tensor(nm("a1_ps0"), [128, 512], F32))
        ps1 = _es.enter_context(nc.psum_tensor(nm("a1_ps1"), [128, 512], F32))
        ptb = _es.enter_context(nc.psum_tensor(nm("a1_ptb"), [128, 1024], BF16))
        attT_stored = [st.n]
        HW1 = DC * 128

        def load_head1(sync, h):
            for sb_, wname in ((swq, "wq1"), (swk, "wk1"), (swv, "wv1")):
                sync.dma_start(
                    out=chunk_view(sb_[:, (h % 2) * HW1:(h % 2 + 1) * HW1], 128),
                    in_=ctx.w[wname][:, h * 128:(h + 1) * 128]
                    .rearrange("(c p) j -> p c j", p=128)).then_inc(ld.h, 16)
                ld.n += 16
            return ld.n

        head_mark1 = {}
        with nc.Block() as block:
            @block.sync
            def _(sync):
                head_mark1[0] = load_head1(sync, 0)

        for h in range(NH):
            with nc.Block() as block:
                po = (h % 2) * HW1

                @block.sync
                def _(sync, h=h):
                    if h + 1 < NH:
                        head_mark1[h + 1] = load_head1(sync, h + 1)

                for wsb, dst in ((swq, sqT), (swk, skT), (swv, svT)):
                    @block.tensor
                    def _(t, wsb=wsb, h=h, po=po):
                        t.wait_ge(ld.h, head_mark1[h])
                        ve.wait(t)
                        for nt in range(4):
                            ins = None
                            for kc in range(DC):
                                ins = t.matmul(
                                    pbig[:, nt * 512:(nt + 1) * 512],
                                    wsb[:, po + kc * 128:po + (kc + 1) * 128],
                                    ctx.xnT[:, kc * S + nt * 512:
                                            kc * S + (nt + 1) * 512],
                                    start=(kc == 0), stop=(kc == DC - 1))
                            mm.inc(ins)

                    @block.vector
                    def _(v, dst=dst):
                        for nt in range(4):
                            v.wait_ge(mm.h, mm.n - 3 + nt)
                            ve.inc(v.tensor_copy(dst[:, nt * 512:(nt + 1) * 512],
                                                 pbig[:, nt * 512:(nt + 1) * 512]))

                for src, dst in ((sqT, ssqT), (skT, sskT)):
                    @block.vector
                    def _(v, src=src):
                        ve.inc(v.tensor_scalar_min(st1[:], src[:], 0.0))

                    @block.scalar
                    def _(s):
                        ve.wait(s)
                        se.inc(s.activation(st1[:], st1[:], AF.Exp))

                    @block.vector
                    def _(v, src=src, dst=dst):
                        se.wait(v)
                        ve.flush(v, v.tensor_relu(st2[:], src[:]))
                        ve.inc(v.tensor_add(dst[:], st1[:], st2[:]))

                for src, dst in ((svT, svn), (sskT, sskn)):
                    for half in range(2):
                        @block.tensor
                        def _(t, src=src, half=half):
                            ve.wait(t)
                            se.wait(t)
                            ins = None
                            for j in range(8):
                                cc = half * 8 + j
                                ins = t.transpose(ptb[:, j * 128:(j + 1) * 128],
                                                  src[:, cc * 128:(cc + 1) * 128],
                                                  ctx.ident_b[:])
                            mm.inc(ins)

                        @block.scalar
                        def _(s, dst=dst, half=half):
                            mm.wait(s)
                            se.inc(s.copy(
                                chunk_view(dst[:], 128)[:, half * 8:half * 8 + 8, :],
                                chunk_view(ptb[:], 128)))

                @block.vector
                def _(v):
                    v.memset(smem[:], 0.0)
                    v.memset(smembf[:], 0.0)
                    v.memset(sz[:], 1.0 / DKH)
                    ve.inc(v.memset(szbf[:], 1.0 / DKH))

                for sg in range(NSEG):
                    exp_done = {}
                    se0 = se.n

                    @block.tensor
                    def _(t, sg=sg, se0=se0):
                        ve.wait(t)
                        se.wait(t)
                        for i in range(4):
                            if i >= 2:
                                t.wait_ge(se.h, se0 + i - 1)
                            ins = t.matmul((ps0 if i % 2 == 0 else ps1)[:],
                                           skT[:, sg * 512 + i * 128:
                                               sg * 512 + (i + 1) * 128],
                                           sqT[:, sg * 512:(sg + 1) * 512],
                                           start=True, stop=True)
                            mm.inc(ins)

                    @block.vector
                    def _(v, sg=sg):
                        base = mm.n - 4
                        for i in range(4):
                            v.wait_ge(mm.h, base + i + 1)
                            c = -128 * i
                            ve.inc(v.tensor_add(
                                (ps0 if i % 2 == 0 else ps1)[:],
                                (ps0 if i % 2 == 0 else ps1)[:],
                                ctx.mask[:, c + 384:c + 384 + 512]))

                    @block.scalar
                    def _(s, sg=sg):
                        base = ve.n - 4
                        for i in range(4):
                            s.wait_ge(ve.h, base + i + 1)
                            se.inc(s.activation(sexp[:, i * 512:(i + 1) * 512],
                                                (ps0 if i % 2 == 0 else ps1)[:],
                                                AF.Exp, scale=ISQ))
                            exp_done[i] = se.n

                    @block.tensor
                    def _(t, sg=sg):
                        se.wait(t)
                        ve.wait(t)
                        ins = None
                        for s1c in range(4):
                            t.matmul(pbig[:, s1c * 128:(s1c + 1) * 128],
                                     ssqT[:, sg * 512 + s1c * 128:
                                          sg * 512 + (s1c + 1) * 128],
                                     smembf[:], start=True, stop=True)
                            t.matmul(pbig[:, 1536 + s1c * 128:
                                          1536 + (s1c + 1) * 128],
                                     sskT[:, sg * 512 + s1c * 128:
                                          sg * 512 + (s1c + 1) * 128],
                                     smembf[:], start=True, stop=True)
                            for i in range(s1c + 1):
                                t.matmul(pbig[:, 1024 + s1c:1024 + s1c + 1],
                                         sexp[:, i * 512 + s1c * 128:
                                              i * 512 + (s1c + 1) * 128],
                                         ctx.ones_col_b[:],
                                         start=(i == 0), stop=(i == s1c))
                            for i in range(s1c + 1):
                                t.matmul(pbig[:, 512 + s1c * 128:
                                              512 + (s1c + 1) * 128],
                                         sexp[:, i * 512 + s1c * 128:
                                              i * 512 + (s1c + 1) * 128],
                                         svn[:, (sg * 4 + i) * 128:
                                             (sg * 4 + i + 1) * 128],
                                         start=(i == 0), stop=(i == s1c))
                            t.matmul(pbig[:, 1028 + s1c:1028 + s1c + 1],
                                     ssqT[:, sg * 512 + s1c * 128:
                                          sg * 512 + (s1c + 1) * 128],
                                     szbf[:], start=True, stop=True)
                            ins = t.matmul(pbig[:, 1032 + s1c:1032 + s1c + 1],
                                           sskT[:, sg * 512 + s1c * 128:
                                                sg * 512 + (s1c + 1) * 128],
                                           szbf[:], start=True, stop=True)
                        mm.inc(ins)

                    @block.vector
                    def _(v, sg=sg):
                        mm.wait(v)
                        ve.flush(v, v.reciprocal(srn[:], pbig[:, 1024:1036]))
                        ve.n -= 0

                    @block.vector
                    def _(v, sg=sg):
                        for s1c in range(4):
                            ve.inc(v.scalar_tensor_tensor(
                                sdlt[:, s1c * 128:(s1c + 1) * 128],
                                pbig[:, 1536 + s1c * 128:1536 + (s1c + 1) * 128],
                                srn[:, 8 + s1c:9 + s1c],
                                svn[:, (sg * 4 + s1c) * 128:
                                    (sg * 4 + s1c + 1) * 128],
                                op0=ALU.mult, op1=ALU.subtract))

                    @block.tensor
                    def _(t, sg=sg):
                        ve.wait(t)
                        for s1c in range(4):
                            t.matmul(ps0[:, 0:128],
                                     sskn[:, (sg * 4 + s1c) * 128:
                                          (sg * 4 + s1c + 1) * 128],
                                     sdlt[:, s1c * 128:(s1c + 1) * 128],
                                     start=(s1c == 0), stop=(s1c == 3))
                        ins = None
                        for s1c in range(4):
                            ins = t.matmul(ps0[:, 128:129],
                                           sskn[:, (sg * 4 + s1c) * 128:
                                                (sg * 4 + s1c + 1) * 128],
                                           ctx.ones_col_b[:],
                                           start=(s1c == 0), stop=(s1c == 3))
                        mm.inc(ins)

                    @block.vector
                    def _(v, sg=sg):
                        mm.wait(v)
                        v.tensor_sub(smem[:], smem[:], ps0[:, 0:128])
                        ve.inc(v.tensor_add(sz[:], sz[:], ps0[:, 128:129]))

                    @block.scalar
                    def _(s, sg=sg):
                        ve.wait(s)
                        s.copy(smembf[:], smem[:])
                        se.inc(s.copy(szbf[:], sz[:]))

                    for s1c in range(4):
                        @block.vector
                        def _(v, sg=sg, s1c=s1c, h=h):
                            v.tensor_scalar_mul(sam[:],
                                                pbig[:, s1c * 128:(s1c + 1) * 128],
                                                srn[:, 4 + s1c:5 + s1c])
                            ve.flush(v, v.tensor_scalar_mul(
                                sad[:],
                                pbig[:, 512 + s1c * 128:512 + (s1c + 1) * 128],
                                srn[:, s1c:s1c + 1]))
                            ve.flush(v, v.tensor_sub(sdf[:], sam[:], sad[:]))
                            ve.flush(v, v.tensor_mul(
                                sdf[:], sdf[:],
                                ctx.beta[:, h * 128:(h + 1) * 128]))
                            ve.inc(v.tensor_add(satt[:], sad[:], sdf[:]))

                        @block.tensor
                        def _(t, sg=sg, s1c=s1c):
                            ve.wait(t)
                            if s1c >= 1:
                                se.wait(t)
                            mm.inc(t.transpose(
                                ps1[:, 256 + (s1c % 2) * 128:
                                    256 + (s1c % 2) * 128 + 128],
                                satt[:], ctx.ident_f[:]))

                        @block.scalar
                        def _(s, sg=sg, s1c=s1c, h=h):
                            mm.wait(s)
                            if sg == 0 and s1c == 0:
                                s.wait_ge(st.h, attT_stored[0])
                            se.inc(s.copy(
                                sattT[:, sg * 512 + s1c * 128:
                                      sg * 512 + (s1c + 1) * 128],
                                ps1[:, 256 + (s1c % 2) * 128:
                                    256 + (s1c % 2) * 128 + 128]))

                @block.sync
                def _(sync, h=h):
                    se.wait(sync)
                    sync.dma_start(out=ctx.attt_scr[b, h * 128:(h + 1) * 128, :],
                                   in_=sattT[:]).then_inc(st.h, 16)
                    st.n += 16
                    attT_stored[0] = st.n


def _quarters(ctx, L, x_src, x_dst, inp):
    nc = ctx.nc
    ld, ve, se, mm, st = ctx.ld, ctx.ve, ctx.se, ctx.mm, ctx.st
    if L == 0:
        qt = 512
        FC, FF = FC0, F0
        NEXP = NE0
        dw_scr = ctx.cmb_scr
        att_scr = ctx.ctxt_scr
        wo = ctx.w["woT0"]
        experts = [(ctx.w["gw0"][e, :, :], ctx.w["uw0"][e, :, :],
                    ctx.w["dw0"][e, :, :], True) for e in range(NE0)]
    else:
        qt = 512
        FC, FF = IC1, I1
        NEXP = NE1
        dw_scr = ctx.dw1_scr
        att_scr = ctx.attt_scr
        wo = ctx.w["wo1"]
        experts = [(ctx.w["w1T"][e, :, :], ctx.w["w3T"][e, :, :],
                    ctx.w["w2T"][e, :, :], True) for e in range(NE1)]
        experts.append((ctx.w["ws1T"][:, :], ctx.w["ws3T"][:, :],
                        ctx.w["ws2T"][:, :], False))
    NTT = qt // 512

    _es = ExitStack()
    with _es:
        acc = _es.enter_context(nc.sbuf_tensor(nm("q_acc"), [128, DC * qt], F32))
        sxn = _es.enter_context(nc.sbuf_tensor(nm("q_xn"), [128, DC * qt], BF16))
        sdwT = _es.enter_context(nc.sbuf_tensor(nm("q_dwT"), [16, qt], F32))
        sdwb = _es.enter_context(nc.sbuf_tensor(nm("q_dwb"), [128, qt], F32))
        sh = _es.enter_context(nc.sbuf_tensor(nm("q_h"), [128, FC * qt], BF16))
        sgt = _es.enter_context(nc.sbuf_tensor(nm("q_gt"), [128, 512], BF16))
        sut = _es.enter_context(nc.sbuf_tensor(nm("q_ut"), [128, 512], BF16))
        sbo = _es.enter_context(nc.sbuf_tensor(nm("q_bo"), [128, DC], F32))
        sw1 = _es.enter_context(nc.sbuf_tensor(nm("q_w1"), [128, DC * FF], BF16))
        sw3 = _es.enter_context(nc.sbuf_tensor(nm("q_w3"), [128, DC * FF], BF16))
        sw2 = _es.enter_context(nc.sbuf_tensor(nm("q_w2"), [128, FC * D], BF16))
        pa = _es.enter_context(nc.psum_tensor(nm("q_pa"), [128, 512], F32))
        pb_ = _es.enter_context(nc.psum_tensor(nm("q_pb"), [128, 512], F32))
        pc = _es.enter_context(nc.psum_tensor(nm("q_pc"), [128, 512], F32))
        pd = _es.enter_context(nc.psum_tensor(nm("q_pd"), [128, 512], F32))
        pe5 = _es.enter_context(nc.psum_tensor(nm("q_pe5"), [128, 512], F32))
        pe6 = _es.enter_context(nc.psum_tensor(nm("q_pe6"), [128, 512], F32))
        pu_rot = [pa, pb_, pe5]
        pg_rot = [pc, pd, pe6]
        pdown_rot = [pa, pb_, pe5, pe6]
        pout_rot = [pa, pb_, pc, pd]
        if L == 0:
            with (
                nc.sbuf_tensor(nm("q_borow"), [1, D], F32) as sborow,
                nc.Block() as block,
            ):
                @block.sync
                def _(sync):
                    sync.dma_start(out=sborow[:], in_=inp["L0_attn_bo"][None, :]
                                   ).then_inc(ld.h, 16)
                    ld.n += 16
                _bias_cols(ctx, block, sborow, sbo, pa, 1)

        def load_w13(sync, e):
            w13a, w13b, _, _ = experts[e]
            sync.dma_start(out=chunk_view(sw1[:], FF),
                           in_=w13a.rearrange("(c p) n -> p c n", p=128)
                           ).then_inc(ld.h, 16)
            sync.dma_start(out=chunk_view(sw3[:], FF),
                           in_=w13b.rearrange("(c p) n -> p c n", p=128)
                           ).then_inc(ld.h, 16)
            ld.n += 32
            return ld.n

        for q in range(NB * S // qt):
            b = q // (S // qt)
            t0 = (q % (S // qt)) * qt
            w13_mark = [None]
            with nc.Block() as block:
                @block.sync
                def _(sync, b=b, t0=t0):
                    sync.wait_ge(mm.h, mm.n)
                    sync.dma_start(
                        out=chunk_view(sxn[:], qt),
                        in_=ctx.xnt_scr[b, :, t0:t0 + qt]
                        .rearrange("(c p) t -> p c t", p=128)).then_inc(ld.h, 16)
                    sync.dma_start(
                        out=sdwT[:NEXP, :],
                        in_=dw_scr[:, b * S + t0:b * S + t0 + qt]
                        ).then_inc(ld.h, 16)
                    ld.n += 32
                    w13_mark[0] = load_w13(sync, 0)

            with (
                nc.sbuf_tensor(nm("q_ctx"), [128, DC * qt], BF16) as sctx,
                nc.sbuf_tensor(nm("q_woA"), [128, DC * 128], BF16) as swoA,
                nc.sbuf_tensor(nm("q_woB"), [128, DC * 128], BF16) as swoB,
                nc.Block() as block,
            ):
                swo = [swoA, swoB]
                wo_loaded = []
                wo_marks = []
                ctx_mark = [None]
                acc_marks = [0, 0, 0, 0]

                @block.sync
                def _(sync, b=b, t0=t0):
                    sync.dma_start(
                        out=chunk_view(sctx[:], qt),
                        in_=att_scr[b, :, t0:t0 + qt]
                        .rearrange("(c p) t -> p c t", p=128)).then_inc(ld.h, 16)
                    ld.n += 16
                    ctx_mark[0] = ld.n

                for dc in range(DC):
                    @block.sync
                    def _(sync, dc=dc):
                        if dc >= 2:
                            sync.wait_ge(mm.h, wo_loaded[dc - 2])
                        sync.dma_start(
                            out=chunk_view(swo[dc % 2][:], 128),
                            in_=wo[:, dc * 128:(dc + 1) * 128]
                            .rearrange("(c p) j -> p c j", p=128)
                            ).then_inc(ld.h, 16)
                        ld.n += 16
                        wo_marks.append(ld.n)

                    @block.tensor
                    def _(t, dc=dc):
                        t.wait_ge(ld.h, max(wo_marks[dc], ctx_mark[0]))
                        t.wait_ge(ve.h, acc_marks[dc % 4])
                        for tt in range(NTT):
                            p = pout_rot[dc % 4]
                            ins = None
                            if L == 0:
                                for hc in range(DC):
                                    ins = t.matmul(
                                        p[:],
                                        swo[dc % 2][:, hc * 128:(hc + 1) * 128],
                                        sctx[:, hc * qt + tt * 512:
                                             hc * qt + (tt + 1) * 512],
                                        start=(hc == 0), stop=(hc == DC - 1))
                                mm.inc(ins)
                            else:
                                sv = sctx[:].rearrange(
                                    "p (c m s) -> p c m s", m=32, s=16)
                                for sl in range(16):
                                    ins = t.matmul(
                                        p[:],
                                        swo[dc % 2][:, sl * 128:(sl + 1) * 128],
                                        sv[:, :, :, sl],
                                        start=(sl == 0), stop=(sl == 15))
                                mm.inc(ins)
                        wo_loaded.append(mm.n)

                    @block.vector
                    def _(v, dc=dc):
                        for tt in range(NTT):
                            p = pout_rot[dc % 4]
                            v.wait_ge(mm.h, mm.n - (NTT - 1) + tt)
                            dst = acc[:, dc * qt + tt * 512:
                                      dc * qt + (tt + 1) * 512]
                            if L == 0:
                                vi = v.tensor_scalar(dst, p[:],
                                                     sbo[:, dc:dc + 1], None,
                                                     op0=ALU.add)
                            else:
                                vi = v.tensor_copy(dst, p[:])
                            ve.inc(vi)
                        acc_marks[dc % 4] = ve.n

            if getattr(ctx, "dbg_acc", None) is not None and q == 0:
                with nc.Block() as block:
                    @block.sync
                    def _(sync):
                        sync.wait_ge(ve.h, ve.n)
                        sync.dma_start(out=ctx.dbg_acc[:, :qt * DC],
                                       in_=acc[:]).then_inc(st.h, 16)
                        st.n += 16
                        st.wait(sync)

            for ei, (w13a, w13b, w2, has_dw) in enumerate(
                    [] if DBG_SKIP_EXPERTS else experts):
                w2_mark = [None]
                with nc.Block() as block:
                    @block.sync
                    def _(sync, ei=ei):
                        sync.wait_ge(mm.h, mm.n)
                        sync.dma_start(
                            out=chunk_view(sw2[:], D),
                            in_=w2.rearrange("(c p) n -> p c n", p=128)
                            ).then_inc(ld.h, 16)
                        ld.n += 16
                        w2_mark[0] = ld.n

                    if has_dw:
                        @block.tensor
                        def _(t, ei=ei):
                            ve.wait(t)
                            se.wait(t)
                            for tt in range(NTT):
                                mm.inc(t.matmul(
                                    (pc if tt == 0 else pd)[:],
                                    ctx.sel[:NEXP, ei * 128:(ei + 1) * 128],
                                    sdwT[:NEXP, tt * 512:(tt + 1) * 512],
                                    start=True, stop=True))

                        @block.scalar
                        def _(s, ei=ei):
                            base = mm.n - NTT
                            for tt in range(NTT):
                                s.wait_ge(mm.h, base + tt + 1)
                                se.inc(s.copy(
                                    sdwb[:, tt * 512:(tt + 1) * 512],
                                    (pc if tt == 0 else pd)[:]))

                    gu_done = {}
                    for tt in range(NTT):
                        for fc in range(FC):
                            @block.tensor
                            def _(t, tt=tt, fc=fc):
                                t.wait_ge(ld.h, w13_mark[0])
                                if has_dw:
                                    se.wait(t)
                                pu = pu_rot[fc % 3]
                                pg = pg_rot[fc % 3]
                                gi = tt * FC + fc
                                if gi >= 3:
                                    pi = gi - 3
                                    prev = (pi // FC, pi % FC)
                                    t.wait_ge(ve.h, gu_done[prev][0])
                                    t.wait_ge(se.h, gu_done[prev][1])
                                ins = None
                                for kc in range(DC):
                                    ins = t.matmul(
                                        pu[:],
                                        sw3[:, kc * FF + fc * 128:
                                            kc * FF + (fc + 1) * 128],
                                        sxn[:, kc * qt + tt * 512:
                                            kc * qt + (tt + 1) * 512],
                                        start=(kc == 0), stop=(kc == DC - 1))
                                mm.inc(ins)
                                ins = None
                                for kc in range(DC):
                                    ins = t.matmul(
                                        pg[:],
                                        sw1[:, kc * FF + fc * 128:
                                            kc * FF + (fc + 1) * 128],
                                        sxn[:, kc * qt + tt * 512:
                                            kc * qt + (tt + 1) * 512],
                                        start=(kc == 0), stop=(kc == DC - 1))
                                mm.inc(ins)

                            @block.vector
                            def _(v, tt=tt, fc=fc):
                                pu = pu_rot[fc % 3]
                                v.wait_ge(mm.h, mm.n - 1)
                                if has_dw:
                                    vi = v.tensor_mul(
                                        sut[:], pu[:],
                                        sdwb[:, tt * 512:(tt + 1) * 512])
                                else:
                                    vi = v.tensor_copy(sut[:], pu[:])
                                ve.flush(v, vi)

                            @block.scalar
                            def _(s, tt=tt, fc=fc):
                                pg = pg_rot[fc % 3]
                                s.wait_ge(mm.h, mm.n)
                                prev1 = ((tt, fc - 1) if fc >= 1
                                         else (tt - 1, FC - 1))
                                if prev1 in gu_done:
                                    s.wait_ge(ve.h, gu_done[prev1][0])
                                se.inc(s.activation(sgt[:], pg[:], AF.Silu))

                            @block.vector
                            def _(v, tt=tt, fc=fc):
                                se.wait(v)
                                ve.inc(v.tensor_mul(
                                    sh[:, fc * qt + tt * 512:
                                       fc * qt + (tt + 1) * 512],
                                    sgt[:], sut[:]))
                                gu_done[(tt, fc)] = (ve.n, se.n)

                    @block.sync
                    def _(sync, ei=ei):
                        if ei + 1 < len(experts):
                            sync.wait_ge(mm.h, mm.n)
                            w13_mark[0] = load_w13(sync, ei + 1)

                    down_evac = {}
                    gu_ve_done = ve.n
                    for dc in range(DC):
                        for tt in range(NTT):
                            @block.tensor
                            def _(t, dc=dc, tt=tt):
                                t.wait_ge(ld.h, w2_mark[0])
                                idx = dc * NTT + tt
                                p = pdown_rot[idx % 4]
                                if idx >= 4:
                                    t.wait_ge(ve.h, down_evac[idx - 4])
                                else:
                                    t.wait_ge(ve.h, gu_ve_done)
                                ins = None
                                for fc in range(FC):
                                    ins = t.matmul(
                                        p[:],
                                        sw2[:, fc * D + dc * 128:
                                            fc * D + (dc + 1) * 128],
                                        sh[:, fc * qt + tt * 512:
                                           fc * qt + (tt + 1) * 512],
                                        start=(fc == 0), stop=(fc == FC - 1))
                                mm.inc(ins)

                            @block.vector
                            def _(v, dc=dc, tt=tt):
                                idx = dc * NTT + tt
                                p = pdown_rot[idx % 4]
                                mm.wait(v)
                                sl = acc[:, dc * qt + tt * 512:
                                         dc * qt + (tt + 1) * 512]
                                ve.inc(v.tensor_add(sl, sl, p[:]))
                                down_evac[idx] = ve.n

            with (
                nc.sbuf_tensor(nm("q_x0"), [128, D], F32) as sx0,
                nc.sbuf_tensor(nm("q_x1"), [128, D], F32) as sx1,
                nc.sbuf_tensor(nm("q_o0"), [128, D], F32) as so0,
                nc.sbuf_tensor(nm("q_o1"), [128, D], F32) as so1,
                nc.Block() as block,
            ):
                sx = [sx0, sx1]
                so = [so0, so1]
                ep_done = [0, 0]
                for ts in range(qt // 128):
                    @block.sync
                    def _(sync, ts=ts, b=b, t0=t0):
                        sync.wait_ge(st.h, ep_done[ts % 2])
                        sync.dma_start(out=sx[ts % 2][:],
                                       in_=x_src[b, t0 + ts * 128:
                                                 t0 + (ts + 1) * 128, :]
                                       ).then_inc(ld.h, 16)
                        ld.n += 16

                    @block.tensor
                    def _(t, ts=ts):
                        ve.wait(t)
                        se.wait(t)
                        ins = None
                        for dc in range(DC):
                            ins = t.transpose(
                                pout_rot[dc // 4][:, (dc % 4) * 128:
                                                  (dc % 4) * 128 + 128],
                                acc[:, dc * qt + ts * 128:
                                    dc * qt + (ts + 1) * 128],
                                ctx.ident_f[:])
                        mm.inc(ins)

                    @block.vector
                    def _(v, ts=ts):
                        mm.wait(v)
                        ld.wait(v)
                        for c4 in range(4):
                            ve.inc(v.tensor_add(
                                so[ts % 2][:, c4 * 512:(c4 + 1) * 512],
                                sx[ts % 2][:, c4 * 512:(c4 + 1) * 512],
                                pout_rot[c4][:]))

                    @block.sync
                    def _(sync, ts=ts, b=b, t0=t0):
                        ve.wait(sync)
                        sync.dma_start(out=x_dst[b, t0 + ts * 128:
                                                 t0 + (ts + 1) * 128, :],
                                       in_=so[ts % 2][:]).then_inc(st.h, 16)
                        st.n += 16
                        ep_done[ts % 2] = st.n

                @block.sync
                def _(sync):
                    st.wait(sync)


def build_program(layers, dbg=()):
    nc = bass.Bass("TRN2", target_bir_lowering=False)
    es = ExitStack()
    with es:
        inp = {}
        if 0 in layers:
            inp["L0_attn_x"] = nc.dram_tensor("L0_attn_x", [NB, S, D], F32,
                                              kind="ExternalInput")
            names = list(L0_WNAMES)
        else:
            inp["x1_in"] = nc.dram_tensor("x1_in", [NB, S, D], F32,
                                          kind="ExternalInput")
            names = []
        if 1 in layers:
            names += L1_WNAMES
        for n in names:
            inp[n] = nc.dram_tensor(n, INPUT_SHAPES[n], F32, kind="ExternalInput")
        out = nc.dram_tensor("out", [NB, S, D], F32, kind="ExternalOutput")

        ctx = _mk_ctx(nc, es, layers)
        ctx.inp = inp
        ctx.xnt_scr = nc.dram_tensor("xnt_scr", [NB, D, S], BF16, kind="Internal")
        if 0 in layers:
            ctx.ctxt_scr = nc.dram_tensor("ctxt_scr", [NB, D, S], BF16,
                                          kind="Internal")
            ctx.cmb_scr = nc.dram_tensor("cmb_scr", [NE0, NB * S], F32,
                                         kind="Internal")
        if 1 in layers:
            ctx.attt_scr = nc.dram_tensor("attt_scr", [NB, D, S], BF16,
                                          kind="Internal")
            ctx.dw1_scr = nc.dram_tensor("dw1_scr", [NE1, NB * S], F32,
                                         kind="Internal")
        if layers == [0, 1]:
            ctx.x1_scr = nc.dram_tensor("x1_scr", [NB, S, D], F32, kind="Internal")
        elif layers == [0]:
            ctx.x1_scr = out
        else:
            ctx.x1_scr = inp["x1_in"]

        _consts(ctx)
        _prep_weights(ctx, inp)

        for L in layers:
            x_src = inp["L0_attn_x"] if L == 0 else ctx.x1_scr
            x_dst = ctx.x1_scr if L == 0 else out
            if layers == [0]:
                x_dst = out
            with ExitStack() as _les:
                ctx.xnT = _les.enter_context(
                    nc.sbuf_tensor(nm("xnT"), [128, DC * S], BF16))
                if L == 1:
                    ctx.beta = _les.enter_context(
                        nc.sbuf_tensor(nm("beta"), [128, NH * 128], F32))
                for b in range(NB):
                    _ln(ctx, b, x_src)
                    if L == 0:
                        _router_l0(ctx, b)
                        _attn_l0(ctx, b, inp)
                    else:
                        _gate_l1(ctx, b, inp)
                        _attn_l1(ctx, b, inp)
            _quarters(ctx, L, x_src, x_dst, inp)

        if dbg:
            dbg_outs = {}
            for name in dbg:
                src = getattr(ctx, name)
                dbg_outs[name] = nc.dram_tensor(
                    "dbg_" + name, list(src.shape), src.dtype,
                    kind="ExternalOutput")
            with nc.Block() as block:
                @block.sync
                def _(sync):
                    for name in dbg:
                        src = getattr(ctx, name)
                        idx = tuple(slice(None) for _ in src.shape)
                        sync.dma_start(out=dbg_outs[name][idx],
                                       in_=src[idx]).then_inc(ctx.st.h, 16)
                        ctx.st.n += 16
                    ctx.st.wait(sync)
    return nc


FUSED = True


def kernel(**inputs):
    xs = np.split(np.ascontiguousarray(inputs["L0_attn_x"]), NCORES, axis=0)
    w = {k: np.ascontiguousarray(v) for k, v in inputs.items()
         if k != "L0_attn_x"}

    if FUSED:
        nc = build_program([0, 1])
        in_maps = [{**w, "L0_attn_x": xs[i]} for i in range(NCORES)]
        res = run_bass_kernel_spmd(nc, in_maps, core_ids=list(range(NCORES)))
        return np.concatenate([res.results[i]["out"] for i in range(NCORES)],
                              axis=0)

    ncA = build_program([0])
    in_maps = [{**{k: w[k] for k in L0_WNAMES}, "L0_attn_x": xs[i]}
               for i in range(NCORES)]
    resA = run_bass_kernel_spmd(ncA, in_maps, core_ids=list(range(NCORES)))
    x1s = [resA.results[i]["out"] for i in range(NCORES)]

    ncB = build_program([1])
    in_maps = [{**{k: w[k] for k in L1_WNAMES}, "x1_in": x1s[i]}
               for i in range(NCORES)]
    resB = run_bass_kernel_spmd(ncB, in_maps, core_ids=list(range(NCORES)))
    return np.concatenate([resB.results[i]["out"] for i in range(NCORES)],
                          axis=0)


# version 08 (final)
# speedup vs baseline: 1.0492x; 1.0142x over previous
import numpy as np
from contextlib import ExitStack
import concourse.bass as bass
import concourse.mybir as mybir
from concourse.bass_utils import run_bass_kernel_spmd

F32 = mybir.dt.float32
BF16 = mybir.dt.bfloat16
AF = mybir.ActivationFunctionType
ALU = mybir.AluOpType
AX = mybir.AxisListType.X

NCORES = 8
NB = 2
S = 2048
D = 2048
NH, DKH = 16, 128
NE0, F0 = 16, 768
NE1, I1 = 8, 1024
NSEG, SEG = 4, 512
QT = 1024
ISQ = 0.08838834764831845
NEGBIG = -1e30
EPS = 1e-5
DBG_SKIP_EXPERTS = False
DBG_SKIP_ATTNOUT = False
DC = D // 128
TC = S // 128
FC0 = F0 // 128
IC1 = I1 // 128

WNAMES = [
    "L0_attn_Wq", "L0_attn_bq", "L0_attn_Wk", "L0_attn_bk", "L0_attn_Wv",
    "L0_attn_bv", "L0_attn_Wo", "L0_attn_bo", "L0_ffn_router_w",
    "L0_ffn_gate_w", "L0_ffn_up_w", "L0_ffn_down_w",
    "L1_attn_Wq", "L1_attn_Wk", "L1_attn_Wv", "L1_attn_Wo", "L1_attn_betas",
    "L1_ffn_gate_w", "L1_ffn_gate_b", "L1_ffn_W1", "L1_ffn_W2", "L1_ffn_W3",
    "L1_ffn_Ws1", "L1_ffn_Ws2", "L1_ffn_Ws3",
]
L0_WNAMES = [n for n in WNAMES if n.startswith("L0")]
L1_WNAMES = [n for n in WNAMES if n.startswith("L1")]

INPUT_SHAPES = {
    "L0_attn_Wq": [D, D], "L0_attn_bq": [D], "L0_attn_Wk": [D, D],
    "L0_attn_bk": [D], "L0_attn_Wv": [D, D], "L0_attn_bv": [D],
    "L0_attn_Wo": [D, D], "L0_attn_bo": [D],
    "L0_ffn_router_w": [D, NE0],
    "L0_ffn_gate_w": [NE0, D, F0], "L0_ffn_up_w": [NE0, D, F0],
    "L0_ffn_down_w": [NE0, F0, D],
    "L1_attn_Wq": [D, D], "L1_attn_Wk": [D, D], "L1_attn_Wv": [D, D],
    "L1_attn_Wo": [D, D], "L1_attn_betas": [1, NH, 1, DKH],
    "L1_ffn_gate_w": [NE1, D], "L1_ffn_gate_b": [NE1],
    "L1_ffn_W1": [NE1, I1, D], "L1_ffn_W2": [NE1, D, I1],
    "L1_ffn_W3": [NE1, I1, D],
    "L1_ffn_Ws1": [I1, D], "L1_ffn_Ws2": [D, I1], "L1_ffn_Ws3": [I1, D],
}



_NAMEC = [0]


def nm(base):
    _NAMEC[0] += 1
    return f"{base}_{_NAMEC[0]}"

class Sem:
    __slots__ = ("h", "n")

    def __init__(self, es, nc, name):
        self.h = es.enter_context(nc.semaphore(name))
        self.n = 0

    def inc(self, ins, k=1):
        ins.then_inc(self.h, k)
        self.n += k

    def bump(self, eng, k=1):
        eng.sem_inc(self.h, k)
        self.n += k

    def flush(self, eng, ins):
        self.inc(ins)
        eng.wait_ge(self.h, self.n)
        return ins

    def wait(self, eng, val=None):
        eng.wait_ge(self.h, self.n if val is None else val)


class Ctx:
    pass


def chunk_view(ap, width):
    return ap.rearrange("p (c j) -> p c j", j=width)


def _mk_ctx(nc, es, layers):
    ctx = Ctx()
    ctx.nc = nc
    ctx.es = es
    ctx.layers = layers
    for s in ("ld", "st", "mm", "ve", "se", "gp"):
        setattr(ctx, s, Sem(es, nc, "sem_" + s))
    return ctx


def _consts(ctx):
    nc, es = ctx.nc, ctx.es
    ctx.ident_b = es.enter_context(nc.sbuf_tensor(nm("ident_b"), [128, 128], BF16))
    ctx.ident_f = es.enter_context(nc.sbuf_tensor(nm("ident_f"), [128, 128], F32))
    ctx.mask = es.enter_context(nc.sbuf_tensor(nm("maskstrip"), [128, 1280], F32))
    ctx.sel = es.enter_context(nc.sbuf_tensor(nm("sel"), [16, 2048], F32))
    ctx.ones_col_b = es.enter_context(nc.sbuf_tensor(nm("ones_col_b"), [128, 1], BF16))
    ctx.ones_row_b = es.enter_context(nc.sbuf_tensor(nm("ones_row_b"), [1, 128], BF16))
    ctx.ones_row_f = es.enter_context(nc.sbuf_tensor(nm("ones_row_f"), [1, 128], F32))
    ctx.one11_f = es.enter_context(nc.sbuf_tensor(nm("one11_f"), [1, 1], F32))
    ctx.eps_col = es.enter_context(nc.sbuf_tensor(nm("eps_col"), [128, 1], F32))
    with ctx.nc.Block() as block:
        @block.gpsimd
        def _(g):
            for t in (ctx.ident_b, ctx.ident_f):
                g.memset(t[:], 0.0)
                g.affine_select(out=t[:], in_=t[:],
                                compare_op=ALU.not_equal, fill=1.0, base=0,
                                pattern=[[-1, 128]], channel_multiplier=1)
            g.memset(ctx.mask[:], 0.0)
            g.affine_select(out=ctx.mask[:], in_=ctx.mask[:],
                            compare_op=ALU.is_ge, fill=NEGBIG, base=-384,
                            pattern=[[1, 1280]], channel_multiplier=-1)
            g.memset(ctx.sel[:], 0.0)
            g.affine_select(
                out=chunk_view(ctx.sel[:], 128),
                in_=chunk_view(ctx.sel[:], 128),
                compare_op=ALU.not_equal, fill=1.0, base=0,
                pattern=[[-1, 16], [0, 128]], channel_multiplier=1)
            g.memset(ctx.ones_col_b[:], 1.0)
            g.memset(ctx.ones_row_b[:], 1.0)
            g.memset(ctx.ones_row_f[:], 1.0)
            g.memset(ctx.one11_f[:], 1.0)
            ctx.gp.inc(g.memset(ctx.eps_col[:], EPS))



def _prep_cast(ctx, src2d, dst, R, C):
    nc = ctx.nc
    ld, ve, st = ctx.ld, ctx.ve, ctx.st
    ph = min(128, R)
    nch = max(1, R // 128)
    kmax = max(1, min(nch, 16384 // C))
    groups = [(g, min(kmax, nch - g)) for g in range(0, nch, kmax)]
    src = src2d.rearrange("(c p) n -> p c n", p=ph)
    dstv = dst[:, :].rearrange("(c p) n -> p c n", p=ph)
    with (
        nc.sbuf_tensor(nm("pc_f0"), [ph, kmax * C], F32) as sf0,
        nc.sbuf_tensor(nm("pc_f1"), [ph, kmax * C], F32) as sf1,
        nc.sbuf_tensor(nm("pc_b0"), [ph, kmax * C], BF16) as sb0,
        nc.sbuf_tensor(nm("pc_b1"), [ph, kmax * C], BF16) as sb1,
        nc.Block() as block,
    ):
        sfs, sbs = [sf0, sf1], [sb0, sb1]
        st_marks = {}
        ld_marks = {}
        ve_marks = {}
        for gi, (g0, k) in enumerate(groups):
            @block.sync
            def _(sync, gi=gi, g0=g0, k=k):
                if gi >= 2:
                    sync.wait_ge(st.h, st_marks[gi - 2])
                sync.dma_start(out=chunk_view(sfs[gi % 2][:, :k * C], C),
                               in_=src[:, g0:g0 + k, :]).then_inc(ld.h, 16)
                ld.n += 16
                ld_marks[gi] = ld.n

            @block.vector
            def _(v, gi=gi, g0=g0, k=k):
                v.wait_ge(ld.h, ld_marks[gi])
                ve.inc(v.tensor_copy(sbs[gi % 2][:, :k * C],
                                     sfs[gi % 2][:, :k * C]))
                ve_marks[gi] = ve.n

            @block.scalar
            def _(s, gi=gi, g0=g0, k=k):
                s.wait_ge(ve.h, ve_marks[gi])
                s.dma_start(out=dstv[:, g0:g0 + k, :],
                            in_=chunk_view(sbs[gi % 2][:, :k * C], C)
                            ).then_inc(st.h, 16)
                st.n += 16
                st_marks[gi] = st.n

        @block.sync
        def _(sync):
            st.wait(sync)


def _prep_transpose(ctx, src2d, dst, R, C):
    nc = ctx.nc
    ld, ve, se, mm, st = ctx.ld, ctx.ve, ctx.se, ctx.mm, ctx.st
    ph = min(128, R)
    nch = max(1, R // 128)
    ncc = C // 128
    kmax = max(1, min(nch, 8192 // C))
    groups = [(g, min(kmax, nch - g)) for g in range(0, nch, kmax)]
    src = src2d.rearrange("(c p) n -> p c n", p=ph)
    with (
        nc.sbuf_tensor(nm("pt_f0"), [ph, kmax * C], F32) as sf0,
        nc.sbuf_tensor(nm("pt_f1"), [ph, kmax * C], F32) as sf1,
        nc.sbuf_tensor(nm("pt_b0"), [ph, kmax * C], BF16) as sb0,
        nc.sbuf_tensor(nm("pt_b1"), [ph, kmax * C], BF16) as sb1,
        nc.sbuf_tensor(nm("pt_o"), [128, ncc * R], BF16) as so,
        nc.psum_tensor(nm("pt_p0"), [128, 1024], BF16) as p0,
        nc.psum_tensor(nm("pt_p1"), [128, 1024], BF16) as p1,
        nc.Block() as block,
    ):
        sfs, sbs = [sf0, sf1], [sb0, sb1]
        pb = [p0, p1]
        fill = [0]
        evac_done = [0, 0]
        cast_mark = {}
        tr_mark = {}
        ld_mark = {}
        ov = so[:].rearrange("p (c r) -> p c r", r=R)
        for gi, (g0, k) in enumerate(groups):
            @block.sync
            def _(sync, gi=gi, g0=g0, k=k):
                if gi >= 2:
                    sync.wait_ge(ve.h, cast_mark[gi - 2])
                sync.dma_start(out=chunk_view(sfs[gi % 2][:, :k * C], C),
                               in_=src[:, g0:g0 + k, :]).then_inc(ld.h, 16)
                ld.n += 16
                ld_mark[gi] = ld.n

            @block.vector
            def _(v, gi=gi, g0=g0, k=k):
                v.wait_ge(ld.h, ld_mark[gi])
                if gi >= 2:
                    v.wait_ge(mm.h, tr_mark[gi - 2])
                ve.inc(v.tensor_copy(sbs[gi % 2][:, :k * C],
                                     sfs[gi % 2][:, :k * C]))
                cast_mark[gi] = ve.n

            for rr in range(g0, g0 + k):
                for cc0 in range(0, ncc, 8):
                    w = min(8, ncc - cc0)
                    fi = fill[0] % 2

                    @block.tensor
                    def _(t, gi=gi, rr=rr, cc0=cc0, w=w, fi=fi, g0=g0):
                        t.wait_ge(ve.h, cast_mark[gi])
                        t.wait_ge(se.h, evac_done[fi])
                        ins = None
                        for j in range(w):
                            cc = cc0 + j
                            ins = t.transpose(
                                pb[fi][:, j * 128:(j + 1) * 128],
                                sbs[gi % 2][:, (rr - g0) * C + cc * 128:
                                            (rr - g0) * C + (cc + 1) * 128],
                                ctx.ident_b[:ph, :])
                        mm.inc(ins)
                        tr_mark[gi] = mm.n

                    @block.scalar
                    def _(s, rr=rr, cc0=cc0, w=w, fi=fi):
                        mm.wait(s)
                        se.inc(s.copy(
                            ov[:, cc0:cc0 + w, rr * ph:(rr + 1) * ph],
                            chunk_view(pb[fi][:, :w * 128], 128)[:, :, :ph]))
                        evac_done[fi] = se.n
                    fill[0] += 1

        @block.sync
        def _(sync):
            se.wait(sync)
            sync.dma_start(out=dst[:, :].rearrange("(c p) r -> p c r", p=128),
                           in_=ov).then_inc(st.h, 16)
            st.n += 16
            st.wait(sync)


def _prep_weights(ctx, inp):
    nc = ctx.nc
    w = {}

    def scr(name, shape):
        w[name] = nc.dram_tensor("scr_" + name, shape, BF16, kind="Internal")
        return w[name]

    if 0 in ctx.layers:
        _prep_transpose(ctx, inp["L0_attn_Wq"][:, :], scr("wqT0", [D, D]), D, D)
        _prep_transpose(ctx, inp["L0_attn_Wk"][:, :], scr("wkT0", [D, D]), D, D)
        _prep_transpose(ctx, inp["L0_attn_Wv"][:, :], scr("wvT0", [D, D]), D, D)
        _prep_transpose(ctx, inp["L0_attn_Wo"][:, :], scr("woT0", [D, D]), D, D)
        _prep_cast(ctx, inp["L0_ffn_router_w"][:, :], scr("rt0", [D, NE0]), D, NE0)
        scr("gw0", [NE0, D, F0]); scr("uw0", [NE0, D, F0]); scr("dw0", [NE0, F0, D])
        for e in range(NE0):
            _prep_cast(ctx, inp["L0_ffn_gate_w"][e, :, :], w["gw0"][e, :, :], D, F0)
            _prep_cast(ctx, inp["L0_ffn_up_w"][e, :, :], w["uw0"][e, :, :], D, F0)
            _prep_cast(ctx, inp["L0_ffn_down_w"][e, :, :], w["dw0"][e, :, :], F0, D)
    if 1 in ctx.layers:
        _prep_cast(ctx, inp["L1_attn_Wq"][:, :], scr("wq1", [D, D]), D, D)
        _prep_cast(ctx, inp["L1_attn_Wk"][:, :], scr("wk1", [D, D]), D, D)
        _prep_cast(ctx, inp["L1_attn_Wv"][:, :], scr("wv1", [D, D]), D, D)
        _prep_cast(ctx, inp["L1_attn_Wo"][:, :], scr("wo1", [D, D]), D, D)
        _prep_transpose(ctx, inp["L1_ffn_gate_w"][:, :], scr("gT1", [D, NE1]), NE1, D)
        scr("w1T", [NE1, D, I1]); scr("w3T", [NE1, D, I1]); scr("w2T", [NE1, I1, D])
        for e in range(NE1):
            _prep_transpose(ctx, inp["L1_ffn_W1"][e, :, :], w["w1T"][e, :, :], I1, D)
            _prep_transpose(ctx, inp["L1_ffn_W3"][e, :, :], w["w3T"][e, :, :], I1, D)
            _prep_transpose(ctx, inp["L1_ffn_W2"][e, :, :], w["w2T"][e, :, :], D, I1)
        _prep_transpose(ctx, inp["L1_ffn_Ws1"][:, :], scr("ws1T", [D, I1]), I1, D)
        _prep_transpose(ctx, inp["L1_ffn_Ws3"][:, :], scr("ws3T", [D, I1]), I1, D)
        _prep_transpose(ctx, inp["L1_ffn_Ws2"][:, :], scr("ws2T", [I1, D]), D, I1)
    ctx.w = w



def _ln(ctx, b, x_src):
    nc = ctx.nc
    ld, ve, se, mm, st = ctx.ld, ctx.ve, ctx.se, ctx.mm, ctx.st
    with (
        nc.sbuf_tensor(nm("ln_x0"), [128, D], F32) as sx0,
        nc.sbuf_tensor(nm("ln_x1"), [128, D], F32) as sx1,
        nc.sbuf_tensor(nm("ln_sq"), [128, D], F32) as ssq,
        nc.sbuf_tensor(nm("ln_st"), [128, 16], F32) as sst,
        nc.sbuf_tensor(nm("ln_xn"), [128, D], BF16) as sxn,
        nc.psum_tensor(nm("ln_p0"), [128, 1024], BF16) as p0,
        nc.psum_tensor(nm("ln_p1"), [128, 1024], BF16) as p1,
        nc.Block() as block,
    ):
        sx = [sx0, sx1]
        pb = [p0, p1]
        xn_done = [0, 0]
        bank_free = [0, 0]
        xnv = chunk_view(ctx.xnT[:], S)
        for tc in range(TC):
            @block.sync
            def _(sync, tc=tc):
                sync.wait_ge(ve.h, xn_done[tc % 2])
                sync.dma_start(out=sx[tc % 2][:],
                               in_=x_src[b, tc * 128:(tc + 1) * 128, :]
                               ).then_inc(ld.h, 16)
                ld.n += 16

            o = 8 * (tc % 2)

            @block.scalar
            def _(s, tc=tc, o=o):
                ld.wait(s)
                se.inc(s.activation(ssq[:], sx[tc % 2][:], AF.Square,
                                    accum_out=sst[:, o + 1:o + 2]))

            @block.vector
            def _(v, tc=tc, o=o):
                ld.wait(v)
                ve.flush(v, v.reduce_sum(sst[:, o:o + 1], sx[tc % 2][:], axis=AX))
                se.wait(v)
                ve.flush(v, v.tensor_scalar_mul(sst[:, o + 2:o + 3],
                                                sst[:, o:o + 1], 1.0 / D))
                v.tensor_scalar_mul(sst[:, o + 3:o + 4], sst[:, o + 1:o + 2],
                                    1.0 / D)
                ve.flush(v, v.tensor_mul(sst[:, o + 4:o + 5], sst[:, o + 2:o + 3],
                                         sst[:, o + 2:o + 3]))
                ve.inc(v.tensor_sub(sst[:, o + 5:o + 6], sst[:, o + 3:o + 4],
                                    sst[:, o + 4:o + 5]))

            @block.scalar
            def _(s, tc=tc, o=o):
                ve.wait(s)
                se.inc(s.activation(sst[:, o + 6:o + 7], sst[:, o + 5:o + 6],
                                    AF.Sqrt, bias=ctx.eps_col[:]))

            @block.vector
            def _(v, tc=tc, o=o):
                se.wait(v)
                ve.flush(v, v.reciprocal(sst[:, o + 7:o + 8], sst[:, o + 6:o + 7]))
                ve.inc(v.tensor_scalar(sxn[:], sx[tc % 2][:], sst[:, o + 2:o + 3],
                                       sst[:, o + 7:o + 8], op0=ALU.subtract,
                                       op1=ALU.mult))
                xn_done[tc % 2] = ve.n

            for fi in range(2):
                @block.tensor
                def _(t, tc=tc, fi=fi):
                    ve.wait(t)
                    t.wait_ge(ve.h, bank_free[fi])
                    ins = None
                    for j in range(8):
                        cc = fi * 8 + j
                        ins = t.transpose(pb[fi][:, j * 128:(j + 1) * 128],
                                          sxn[:, cc * 128:(cc + 1) * 128],
                                          ctx.ident_b[:])
                    mm.inc(ins)

                @block.vector
                def _(v, tc=tc, fi=fi):
                    mm.wait(v)
                    ve.inc(v.tensor_copy(
                        xnv[:, fi * 8:fi * 8 + 8, tc * 128:(tc + 1) * 128],
                        chunk_view(pb[fi][:], 128)))
                    bank_free[fi] = ve.n

        @block.sync
        def _(sync):
            ve.wait(sync)
            sync.dma_start(
                out=ctx.xnt_scr[b, :, :].rearrange("(c p) t -> p c t", p=128),
                in_=xnv).then_inc(st.h, 16)
            st.n += 16
            st.wait(sync)



def _router_l0(ctx, b):
    nc = ctx.nc
    ld, ve, se, mm, st = ctx.ld, ctx.ve, ctx.se, ctx.mm, ctx.st
    E = NE0
    with (
        nc.sbuf_tensor(nm("rt_w"), [128, DC * E], BF16) as srt,
        nc.sbuf_tensor(nm("rt_lg"), [128, E], F32) as slg,
        nc.sbuf_tensor(nm("rt_t"), [128, 5 * E], F32) as stp,
        nc.sbuf_tensor(nm("rt_s"), [128, 4], F32) as sst,
        nc.sbuf_tensor(nm("rt_cT"), [16, S], F32) as scT,
        nc.psum_tensor(nm("rt_pl"), [128, E], F32) as plg,
        nc.psum_tensor(nm("rt_pt"), [16, 128], F32) as ptr,
        nc.Block() as block,
    ):
        @block.sync
        def _(sync):
            sync.dma_start(out=chunk_view(srt[:], E),
                           in_=ctx.w["rt0"][:, :].rearrange("(c p) n -> p c n", p=128)
                           ).then_inc(ld.h, 16)
            ld.n += 16

        plg_free = [0]
        ptr_free = [0]
        for tc in range(TC):
            @block.tensor
            def _(t, tc=tc):
                ld.wait(t)
                t.wait_ge(ve.h, plg_free[0])
                ins = None
                for kc in range(DC):
                    ins = t.matmul(plg[:],
                                   ctx.xnT[:, kc * S + tc * 128:
                                           kc * S + (tc + 1) * 128],
                                   srt[:, kc * E:(kc + 1) * E],
                                   start=(kc == 0), stop=(kc == DC - 1))
                mm.inc(ins)

            @block.vector
            def _(v, tc=tc):
                mm.wait(v)
                ve.flush(v, v.tensor_copy(slg[:], plg[:]))
                plg_free[0] = ve.n
                ve.flush(v, v.reduce_max(sst[:, 0:1], slg[:], axis=AX,
                                         negate=True))
                ve.flush(v, v.tensor_scalar(stp[:, 3 * E:4 * E], slg[:],
                                            sst[:, 0:1], 0.0,
                                            op0=ALU.add, op1=ALU.is_ge))
                ve.flush(v, v.scalar_tensor_tensor(stp[:, 4 * E:5 * E],
                                                   stp[:, 3 * E:4 * E], NEGBIG,
                                                   slg[:],
                                                   op0=ALU.mult, op1=ALU.add))
                ve.inc(v.reduce_max(sst[:, 1:2], stp[:, 4 * E:5 * E], axis=AX))

            @block.scalar
            def _(s, tc=tc):
                ve.wait(s)
                se.inc(s.activation(stp[:, 0:E], slg[:], AF.Exp, bias=sst[:, 0:1]))

            @block.vector
            def _(v, tc=tc):
                ve.wait(v)
                se.wait(v)
                ve.flush(v, v.tensor_scalar(stp[:, E:2 * E], slg[:], sst[:, 1:2],
                                            None, op0=ALU.is_ge))
                ve.flush(v, v.tensor_mul(stp[:, 2 * E:3 * E], stp[:, 0:E],
                                         stp[:, E:2 * E]))
                ve.flush(v, v.reduce_sum(sst[:, 2:3], stp[:, 2 * E:3 * E],
                                         axis=AX))
                ve.flush(v, v.reciprocal(sst[:, 3:4], sst[:, 2:3]))
                ve.inc(v.tensor_scalar_mul(stp[:, 2 * E:3 * E],
                                           stp[:, 2 * E:3 * E], sst[:, 3:4]))

            @block.tensor
            def _(t, tc=tc):
                ve.wait(t)
                t.wait_ge(ve.h, ptr_free[0])
                mm.inc(t.transpose(ptr[:], stp[:, 2 * E:3 * E], ctx.ident_f[:]))

            @block.vector
            def _(v, tc=tc):
                mm.wait(v)
                ve.inc(v.tensor_copy(scT[:, tc * 128:(tc + 1) * 128], ptr[:]))
                ptr_free[0] = ve.n

        @block.sync
        def _(sync):
            ve.wait(sync)
            sync.dma_start(out=ctx.cmb_scr[:, b * S:(b + 1) * S], in_=scT[:]
                           ).then_inc(st.h, 16)
            st.n += 16
            st.wait(sync)


def _gate_l1(ctx, b, inp):
    nc = ctx.nc
    ld, ve, se, mm, st = ctx.ld, ctx.ve, ctx.se, ctx.mm, ctx.st
    E = NE1
    with (
        nc.sbuf_tensor(nm("gt_w"), [128, DC * E], BF16) as sgw,
        nc.sbuf_tensor(nm("gt_br"), [1, E], F32) as sgbr,
        nc.sbuf_tensor(nm("gt_b"), [128, E], F32) as sgb,
        nc.sbuf_tensor(nm("gt_lg"), [128, E], F32) as slg,
        nc.sbuf_tensor(nm("gt_t"), [128, 6 * E], F32) as stp,
        nc.sbuf_tensor(nm("gt_s"), [128, 4], F32) as sst,
        nc.sbuf_tensor(nm("gt_cT"), [E, S], F32) as scT,
        nc.psum_tensor(nm("gt_pl"), [128, E], F32) as plg,
        nc.psum_tensor(nm("gt_pt"), [16, 128], F32) as ptr,
        nc.Block() as block,
    ):
        @block.sync
        def _(sync):
            sync.dma_start(out=chunk_view(sgw[:], E),
                           in_=ctx.w["gT1"][:, :].rearrange("(c p) n -> p c n", p=128)
                           ).then_inc(ld.h, 16)
            sync.dma_start(out=sgbr[:], in_=inp["L1_ffn_gate_b"][None, :]
                           ).then_inc(ld.h, 16)
            ld.n += 32

        @block.tensor
        def _(t):
            ld.wait(t)
            mm.inc(t.matmul(plg[:], ctx.ones_row_f[:], sgbr[:],
                            start=True, stop=True))

        @block.vector
        def _(v):
            mm.wait(v)
            ve.inc(v.tensor_copy(sgb[:], plg[:]))

        plg_free = [ve.n + 0]
        ptr_free = [0]
        plg_free[0] = ve.n
        for tc in range(TC):
            @block.tensor
            def _(t, tc=tc):
                t.wait_ge(ve.h, plg_free[0])
                ins = None
                for kc in range(DC):
                    ins = t.matmul(plg[:],
                                   ctx.xnT[:, kc * S + tc * 128:
                                           kc * S + (tc + 1) * 128],
                                   sgw[:, kc * E:(kc + 1) * E],
                                   start=(kc == 0), stop=(kc == DC - 1))
                mm.inc(ins)

            @block.vector
            def _(v, tc=tc):
                mm.wait(v)
                ve.flush(v, v.tensor_copy(slg[:], plg[:]))
                plg_free[0] = ve.n
                ve.inc(v.reduce_max(sst[:, 0:1], slg[:], axis=AX, negate=True))

            @block.scalar
            def _(s, tc=tc):
                ve.wait(s)
                se.inc(s.activation(stp[:, 0:E], slg[:], AF.Exp, bias=sst[:, 0:1],
                                    accum_out=sst[:, 1:2]))

            @block.vector
            def _(v, tc=tc):
                se.wait(v)
                ve.flush(v, v.reciprocal(sst[:, 2:3], sst[:, 1:2]))
                ve.flush(v, v.tensor_scalar_mul(stp[:, 0:E], stp[:, 0:E],
                                                sst[:, 2:3]))
                ve.flush(v, v.tensor_add(stp[:, E:2 * E], stp[:, 0:E], sgb[:]))
                ve.flush(v, v.reduce_max(sst[:, 3:4], stp[:, E:2 * E], axis=AX))
                ve.flush(v, v.tensor_scalar(stp[:, 3 * E:4 * E],
                                            stp[:, E:2 * E],
                                            sst[:, 3:4], None, op0=ALU.is_ge))
                ve.flush(v, v.scalar_tensor_tensor(stp[:, 4 * E:5 * E],
                                                   stp[:, 3 * E:4 * E], NEGBIG,
                                                   stp[:, E:2 * E],
                                                   op0=ALU.mult, op1=ALU.add))
                ve.flush(v, v.reduce_max(sst[:, 3:4], stp[:, 4 * E:5 * E],
                                         axis=AX))
                ve.flush(v, v.tensor_scalar(stp[:, 5 * E:6 * E],
                                            stp[:, E:2 * E],
                                            sst[:, 3:4], None, op0=ALU.is_ge))
                ve.inc(v.tensor_mul(stp[:, 2 * E:3 * E], stp[:, 0:E],
                                    stp[:, 5 * E:6 * E]))

            @block.tensor
            def _(t, tc=tc):
                ve.wait(t)
                t.wait_ge(ve.h, ptr_free[0])
                mm.inc(t.transpose(ptr[:E, :], stp[:, 2 * E:3 * E],
                                   ctx.ident_f[:]))

            @block.vector
            def _(v, tc=tc):
                mm.wait(v)
                ve.inc(v.tensor_copy(scT[:, tc * 128:(tc + 1) * 128], ptr[:E, :]))
                ptr_free[0] = ve.n

        @block.sync
        def _(sync):
            ve.wait(sync)
            sync.dma_start(out=ctx.dw1_scr[:, b * S:(b + 1) * S], in_=scT[:]
                           ).then_inc(st.h, 16)
            st.n += 16
            st.wait(sync)



def _bias_cols(ctx, block, rows_sb, cols_sb, p_b, n):
    mm, ve = ctx.mm, ctx.ve

    @block.tensor
    def _(t):
        ctx.ld.wait(t)
        ins = None
        for k in range(n):
            for c in range(DC):
                ins = t.matmul(p_b[:, k * DC + c:k * DC + c + 1],
                               rows_sb[0:1, k * D + c * 128:k * D + (c + 1) * 128],
                               ctx.one11_f[:], start=True, stop=True)
        mm.inc(ins)

    @block.vector
    def _(v):
        mm.wait(v)
        ve.inc(v.tensor_copy(cols_sb[:], p_b[:, :n * DC]))


def _attn_l0(ctx, b, inp):
    nc = ctx.nc
    ld, ve, se, mm, st = ctx.ld, ctx.ve, ctx.se, ctx.mm, ctx.st
    _es = ExitStack()
    with _es:
        sbrow = _es.enter_context(nc.sbuf_tensor(nm("a0_brow"), [1, 3 * D], F32))
        sbvr = _es.enter_context(nc.sbuf_tensor(nm("a0_bvrow"), [1, D], BF16))
        sbc = _es.enter_context(nc.sbuf_tensor(nm("a0_bcols"), [128, 3 * DC], F32))
        swq = _es.enter_context(nc.sbuf_tensor(nm("a0_wq"), [128, 2 * DC * 128], BF16))
        swk = _es.enter_context(nc.sbuf_tensor(nm("a0_wk"), [128, 2 * DC * 128], BF16))
        swv = _es.enter_context(nc.sbuf_tensor(nm("a0_wv"), [128, 2 * DC * 128], BF16))
        sqT = _es.enter_context(nc.sbuf_tensor(nm("a0_qT"), [128, S], BF16))
        skT = _es.enter_context(nc.sbuf_tensor(nm("a0_kT"), [128, S], BF16))
        svn = _es.enter_context(nc.sbuf_tensor(nm("a0_vn"), [128, S], BF16))
        sexp = _es.enter_context(nc.sbuf_tensor(nm("a0_exp"), [128, 16 * 512], BF16))
        srcp = _es.enter_context(nc.sbuf_tensor(nm("a0_rcp"), [1, S], F32))
        sbcn = _es.enter_context(nc.sbuf_tensor(nm("a0_bcn"), [128, 512], F32))
        sctxT = _es.enter_context(nc.sbuf_tensor(nm("a0_ctxT"), [128, S], BF16))
        pbig = _es.enter_context(nc.psum_tensor(nm("a0_pbig"), [128, 2048], F32))
        ps0 = _es.enter_context(nc.psum_tensor(nm("a0_ps0"), [128, 512], F32))
        ps1 = _es.enter_context(nc.psum_tensor(nm("a0_ps1"), [128, 512], F32))
        pdn = _es.enter_context(nc.psum_tensor(nm("a0_pdn"), [1, 512], F32))
        pctx = _es.enter_context(nc.psum_tensor(nm("a0_pctx"), [128, 512], F32))
        ps = [ps0, ps1]
        HW = DC * 128

        def load_head(sync, h):
            for sb_, wname in ((swq, "wqT0"), (swk, "wkT0"), (swv, "wvT0")):
                sync.dma_start(
                    out=chunk_view(sb_[:, (h % 2) * HW:(h % 2 + 1) * HW], 128),
                    in_=ctx.w[wname][:, h * 128:(h + 1) * 128]
                    .rearrange("(c p) j -> p c j", p=128)).then_inc(ld.h, 16)
                ld.n += 16
            return ld.n

        head_mark = {}
        with nc.Block() as block:
            @block.sync
            def _(sync):
                for k, bn in enumerate(["L0_attn_bq", "L0_attn_bk", "L0_attn_bv"]):
                    sync.dma_start(out=sbrow[0:1, k * D:(k + 1) * D],
                                   in_=inp[bn][None, :]).then_inc(ld.h, 16)
                    ld.n += 16
                head_mark[0] = load_head(sync, 0)
            _bias_cols(ctx, block, sbrow, sbc, pbig, 3)

            @block.scalar
            def _(s):
                ld.wait(s)
                se.inc(s.copy(sbvr[:], sbrow[0:1, 2 * D:3 * D]))

        ctxT_stored = [st.n]
        for h in range(NH):
            with nc.Block() as block:
                po = (h % 2) * HW

                @block.sync
                def _(sync, h=h):
                    if h + 1 < NH:
                        head_mark[h + 1] = load_head(sync, h + 1)

                for rnd, (wsb, dst, bcol) in enumerate(
                        ((swq, sqT, 0), (swk, skT, 1))):
                    @block.tensor
                    def _(t, rnd=rnd, wsb=wsb, h=h, po=po):
                        t.wait_ge(ld.h, head_mark[h])
                        vebase = ve.n
                        for nt in range(4):
                            t.wait_ge(ve.h, vebase - 3 + nt)
                            ins = None
                            for kc in range(DC):
                                ins = t.matmul(
                                    pbig[:, nt * 512:(nt + 1) * 512],
                                    wsb[:, po + kc * 128:po + (kc + 1) * 128],
                                    ctx.xnT[:, kc * S + nt * 512:
                                            kc * S + (nt + 1) * 512],
                                    start=(kc == 0), stop=(kc == DC - 1))
                            mm.inc(ins)

                    @block.vector
                    def _(v, rnd=rnd, dst=dst, bcol=bcol, h=h):
                        for nt in range(4):
                            v.wait_ge(mm.h, mm.n - 3 + nt)
                            vi = v.tensor_scalar(
                                dst[:, nt * 512:(nt + 1) * 512],
                                pbig[:, nt * 512:(nt + 1) * 512],
                                sbc[:, bcol * DC + h:bcol * DC + h + 1], None,
                                op0=ALU.add)
                            ve.inc(vi)

                @block.tensor
                def _(t, h=h):
                    ve.wait(t)
                    se.wait(t)
                    for grp in range(4):
                        ins = None
                        for tc in range(grp * 4, grp * 4 + 4):
                            for kc in range(DC):
                                ins = t.matmul(
                                    pbig[:, tc * 128:(tc + 1) * 128],
                                    ctx.xnT[:, kc * S + tc * 128:
                                            kc * S + (tc + 1) * 128],
                                    swv[:, po + kc * 128:po + (kc + 1) * 128],
                                    start=(kc == 0), stop=False)
                            ins = t.matmul(
                                pbig[:, tc * 128:(tc + 1) * 128],
                                ctx.ones_row_b[:],
                                sbvr[0:1, h * 128:(h + 1) * 128],
                                start=False, stop=True)
                        mm.inc(ins)

                @block.vector
                def _(v, h=h):
                    for grp in range(4):
                        v.wait_ge(mm.h, mm.n - 3 + grp)
                        ve.inc(v.tensor_copy(svn[:, grp * 512:(grp + 1) * 512],
                                             pbig[:, grp * 512:(grp + 1) * 512]))

                exp_done = {}
                bank_free = [ve.n, ve.n]
                bank_free_sem = [None, None]
                pctx_free = [0]
                pdn_free = [0]
                for j in range(4):
                    imax = 4 * j + 4
                    se0 = se.n

                    @block.tensor
                    def _(t, j=j, imax=imax, se0=se0):
                        ve.wait(t)
                        for i in range(imax):
                            if i >= 2:
                                t.wait_ge(se.h, se0 + i - 1)
                            if i < 2 and j > 0:
                                t.wait_ge(se.h, exp_done[(j - 1, "bcn")])
                            ins = t.matmul(ps[i % 2][:],
                                           skT[:, i * 128:(i + 1) * 128],
                                           sqT[:, j * 512:(j + 1) * 512],
                                           start=True, stop=True)
                            mm.inc(ins)

                    @block.vector
                    def _(v, j=j, imax=imax):
                        for i in range(4 * j, imax):
                            v.wait_ge(mm.h, mm.n - (imax - 1 - i))
                            c = 512 * j - 128 * i
                            ve.inc(v.tensor_add(
                                ps[i % 2][:], ps[i % 2][:],
                                ctx.mask[:, c + 384:c + 384 + 512]))

                    @block.scalar
                    def _(s, j=j, imax=imax):
                        mmbase = mm.n - imax
                        vebase = ve.n - 4
                        for i in range(imax):
                            if i < 4 * j:
                                s.wait_ge(mm.h, mmbase + i + 1)
                            else:
                                s.wait_ge(ve.h, vebase + (i - 4 * j) + 1)
                            se.inc(s.activation(sexp[:, i * 512:(i + 1) * 512],
                                                ps[i % 2][:], AF.Exp, scale=ISQ))
                            exp_done[(j, i)] = se.n

                    @block.tensor
                    def _(t, j=j, imax=imax):
                        t.wait_ge(ve.h, pdn_free[0])
                        t.wait_ge(ve.h, pctx_free[0])
                        ins = None
                        for i in range(imax):
                            t.wait_ge(se.h, exp_done[(j, i)])
                            t.matmul(pdn[:], ctx.ones_col_b[:],
                                     sexp[:, i * 512:(i + 1) * 512],
                                     start=(i == 0), stop=(i == imax - 1))
                            ins = t.matmul(pctx[:],
                                           svn[:, i * 128:(i + 1) * 128],
                                           sexp[:, i * 512:(i + 1) * 512],
                                           start=(i == 0), stop=(i == imax - 1))
                        mm.inc(ins)

                    @block.vector
                    def _(v, j=j):
                        mm.wait(v)
                        ve.inc(v.reciprocal(srcp[0:1, j * 512:(j + 1) * 512],
                                            pdn[:]))
                        pdn_free[0] = ve.n

                    @block.tensor
                    def _(t, j=j):
                        ve.wait(t)
                        mm.inc(t.matmul(ps0[:], ctx.ones_row_f[:],
                                        srcp[0:1, j * 512:(j + 1) * 512],
                                        start=True, stop=True))

                    @block.scalar
                    def _(s, j=j):
                        mm.wait(s)
                        se.inc(s.copy(sbcn[:], ps0[:]))
                        exp_done[(j, "bcn")] = se.n

                    @block.vector
                    def _(v, j=j, h=h):
                        se.wait(v)
                        if j == 0:
                            v.wait_ge(st.h, ctxT_stored[0])
                        ve.inc(v.tensor_mul(sctxT[:, j * 512:(j + 1) * 512],
                                            pctx[:], sbcn[:]))
                        pctx_free[0] = ve.n

                @block.sync
                def _(sync, h=h):
                    ve.wait(sync)
                    sync.dma_start(out=ctx.ctxt_scr[b, h * 128:(h + 1) * 128, :],
                                   in_=sctxT[:]).then_inc(st.h, 16)
                    st.n += 16
                    ctxT_stored[0] = st.n



def _attn_l1(ctx, b, inp):
    nc = ctx.nc
    ld, ve, se, mm, st = ctx.ld, ctx.ve, ctx.se, ctx.mm, ctx.st
    if b == 0:
        with (
            nc.sbuf_tensor(nm("b_raw"), [16, 128], F32) as sraw,
            nc.sbuf_tensor(nm("b_sig"), [16, 128], F32) as ssig,
            nc.psum_tensor(nm("b_p"), [128, 128], F32) as pbt,
            nc.Block() as block,
        ):
            @block.sync
            def _(sync):
                sync.dma_start(out=sraw[:], in_=inp["L1_attn_betas"][0, :, 0, :]
                               ).then_inc(ld.h, 16)
                ld.n += 16

            @block.scalar
            def _(s):
                ld.wait(s)
                se.inc(s.activation(ssig[:], sraw[:], AF.Sigmoid))

            pbt_free = [0]
            for h in range(NH):
                @block.tensor
                def _(t, h=h):
                    se.wait(t)
                    t.wait_ge(ve.h, pbt_free[0])
                    mm.inc(t.matmul(pbt[:], ctx.sel[:, h * 128:(h + 1) * 128],
                                    ssig[:], start=True, stop=True))

                @block.vector
                def _(v, h=h):
                    mm.wait(v)
                    ve.inc(v.tensor_copy(ctx.beta[:, h * 128:(h + 1) * 128],
                                         pbt[:]))
                    pbt_free[0] = ve.n

    _es = ExitStack()
    with _es:
        swq = _es.enter_context(nc.sbuf_tensor(nm("a1_wq"), [128, 2 * DC * 128], BF16))
        swk = _es.enter_context(nc.sbuf_tensor(nm("a1_wk"), [128, 2 * DC * 128], BF16))
        swv = _es.enter_context(nc.sbuf_tensor(nm("a1_wv"), [128, 2 * DC * 128], BF16))
        sqT = _es.enter_context(nc.sbuf_tensor(nm("a1_qT"), [128, S], BF16))
        skT = _es.enter_context(nc.sbuf_tensor(nm("a1_kT"), [128, S], BF16))
        svT = _es.enter_context(nc.sbuf_tensor(nm("a1_vT"), [128, S], BF16))
        svn = _es.enter_context(nc.sbuf_tensor(nm("a1_vn"), [128, S], BF16))
        ssqT = _es.enter_context(nc.sbuf_tensor(nm("a1_sqT"), [128, S], BF16))
        sskT = _es.enter_context(nc.sbuf_tensor(nm("a1_skT"), [128, S], BF16))
        sskn = _es.enter_context(nc.sbuf_tensor(nm("a1_skn"), [128, S], BF16))
        st1 = _es.enter_context(nc.sbuf_tensor(nm("a1_t1"), [128, S], F32))
        st2 = _es.enter_context(nc.sbuf_tensor(nm("a1_t2"), [128, S], F32))
        sexp = _es.enter_context(nc.sbuf_tensor(nm("a1_exp"), [128, 4 * 512], BF16))
        smem = _es.enter_context(nc.sbuf_tensor(nm("a1_mem"), [128, 128], F32))
        smembf = _es.enter_context(nc.sbuf_tensor(nm("a1_membf"), [128, 128], BF16))
        sz = _es.enter_context(nc.sbuf_tensor(nm("a1_z"), [128, 1], F32))
        szbf = _es.enter_context(nc.sbuf_tensor(nm("a1_zbf"), [128, 1], BF16))
        srn = _es.enter_context(nc.sbuf_tensor(nm("a1_rn"), [128, 12], F32))
        sdlt = _es.enter_context(nc.sbuf_tensor(nm("a1_dlt"), [128, 512], BF16))
        sam = _es.enter_context(nc.sbuf_tensor(nm("a1_am"), [128, 128], F32))
        sad = _es.enter_context(nc.sbuf_tensor(nm("a1_ad"), [128, 128], F32))
        sdf = _es.enter_context(nc.sbuf_tensor(nm("a1_df"), [128, 128], F32))
        satt = _es.enter_context(nc.sbuf_tensor(nm("a1_att"), [128, 128], F32))
        sattT = _es.enter_context(nc.sbuf_tensor(nm("a1_attT"), [128, S], BF16))
        pbig = _es.enter_context(nc.psum_tensor(nm("a1_pbig"), [128, 2048], F32))
        ps0 = _es.enter_context(nc.psum_tensor(nm("a1_ps0"), [128, 512], F32))
        ps1 = _es.enter_context(nc.psum_tensor(nm("a1_ps1"), [128, 512], F32))
        ptb = _es.enter_context(nc.psum_tensor(nm("a1_ptb"), [128, 1024], BF16))
        attT_stored = [st.n]
        HW1 = DC * 128

        def load_head1(sync, h):
            for sb_, wname in ((swq, "wq1"), (swk, "wk1"), (swv, "wv1")):
                sync.dma_start(
                    out=chunk_view(sb_[:, (h % 2) * HW1:(h % 2 + 1) * HW1], 128),
                    in_=ctx.w[wname][:, h * 128:(h + 1) * 128]
                    .rearrange("(c p) j -> p c j", p=128)).then_inc(ld.h, 16)
                ld.n += 16
            return ld.n

        head_mark1 = {}
        with nc.Block() as block:
            @block.sync
            def _(sync):
                head_mark1[0] = load_head1(sync, 0)

        for h in range(NH):
            with nc.Block() as block:
                po = (h % 2) * HW1

                @block.sync
                def _(sync, h=h):
                    if h + 1 < NH:
                        head_mark1[h + 1] = load_head1(sync, h + 1)

                for wsb, dst in ((swq, sqT), (swk, skT), (swv, svT)):
                    @block.tensor
                    def _(t, wsb=wsb, h=h, po=po):
                        t.wait_ge(ld.h, head_mark1[h])
                        ve.wait(t)
                        for nt in range(4):
                            ins = None
                            for kc in range(DC):
                                ins = t.matmul(
                                    pbig[:, nt * 512:(nt + 1) * 512],
                                    wsb[:, po + kc * 128:po + (kc + 1) * 128],
                                    ctx.xnT[:, kc * S + nt * 512:
                                            kc * S + (nt + 1) * 512],
                                    start=(kc == 0), stop=(kc == DC - 1))
                            mm.inc(ins)

                    @block.vector
                    def _(v, dst=dst):
                        for nt in range(4):
                            v.wait_ge(mm.h, mm.n - 3 + nt)
                            ve.inc(v.tensor_copy(dst[:, nt * 512:(nt + 1) * 512],
                                                 pbig[:, nt * 512:(nt + 1) * 512]))

                for src, dst in ((sqT, ssqT), (skT, sskT)):
                    @block.vector
                    def _(v, src=src):
                        ve.inc(v.tensor_scalar_min(st1[:], src[:], 0.0))

                    @block.scalar
                    def _(s):
                        ve.wait(s)
                        se.inc(s.activation(st1[:], st1[:], AF.Exp))

                    @block.vector
                    def _(v, src=src, dst=dst):
                        se.wait(v)
                        ve.flush(v, v.tensor_relu(st2[:], src[:]))
                        ve.inc(v.tensor_add(dst[:], st1[:], st2[:]))

                for src, dst in ((svT, svn), (sskT, sskn)):
                    for half in range(2):
                        @block.tensor
                        def _(t, src=src, half=half):
                            ve.wait(t)
                            se.wait(t)
                            ins = None
                            for j in range(8):
                                cc = half * 8 + j
                                ins = t.transpose(ptb[:, j * 128:(j + 1) * 128],
                                                  src[:, cc * 128:(cc + 1) * 128],
                                                  ctx.ident_b[:])
                            mm.inc(ins)

                        @block.scalar
                        def _(s, dst=dst, half=half):
                            mm.wait(s)
                            se.inc(s.copy(
                                chunk_view(dst[:], 128)[:, half * 8:half * 8 + 8, :],
                                chunk_view(ptb[:], 128)))

                @block.vector
                def _(v):
                    v.memset(smem[:], 0.0)
                    v.memset(smembf[:], 0.0)
                    v.memset(sz[:], 1.0 / DKH)
                    ve.inc(v.memset(szbf[:], 1.0 / DKH))

                for sg in range(NSEG):
                    exp_done = {}
                    se0 = se.n

                    @block.tensor
                    def _(t, sg=sg, se0=se0):
                        ve.wait(t)
                        se.wait(t)
                        for i in range(4):
                            if i >= 2:
                                t.wait_ge(se.h, se0 + i - 1)
                            ins = t.matmul((ps0 if i % 2 == 0 else ps1)[:],
                                           skT[:, sg * 512 + i * 128:
                                               sg * 512 + (i + 1) * 128],
                                           sqT[:, sg * 512:(sg + 1) * 512],
                                           start=True, stop=True)
                            mm.inc(ins)

                    @block.vector
                    def _(v, sg=sg):
                        base = mm.n - 4
                        for i in range(4):
                            v.wait_ge(mm.h, base + i + 1)
                            c = -128 * i
                            ve.inc(v.tensor_add(
                                (ps0 if i % 2 == 0 else ps1)[:],
                                (ps0 if i % 2 == 0 else ps1)[:],
                                ctx.mask[:, c + 384:c + 384 + 512]))

                    @block.scalar
                    def _(s, sg=sg):
                        base = ve.n - 4
                        for i in range(4):
                            s.wait_ge(ve.h, base + i + 1)
                            se.inc(s.activation(sexp[:, i * 512:(i + 1) * 512],
                                                (ps0 if i % 2 == 0 else ps1)[:],
                                                AF.Exp, scale=ISQ))
                            exp_done[i] = se.n

                    @block.tensor
                    def _(t, sg=sg):
                        se.wait(t)
                        ve.wait(t)
                        ins = None
                        for s1c in range(4):
                            t.matmul(pbig[:, s1c * 128:(s1c + 1) * 128],
                                     ssqT[:, sg * 512 + s1c * 128:
                                          sg * 512 + (s1c + 1) * 128],
                                     smembf[:], start=True, stop=True)
                            t.matmul(pbig[:, 1536 + s1c * 128:
                                          1536 + (s1c + 1) * 128],
                                     sskT[:, sg * 512 + s1c * 128:
                                          sg * 512 + (s1c + 1) * 128],
                                     smembf[:], start=True, stop=True)
                            for i in range(s1c + 1):
                                t.matmul(pbig[:, 1024 + s1c:1024 + s1c + 1],
                                         sexp[:, i * 512 + s1c * 128:
                                              i * 512 + (s1c + 1) * 128],
                                         ctx.ones_col_b[:],
                                         start=(i == 0), stop=(i == s1c))
                            for i in range(s1c + 1):
                                t.matmul(pbig[:, 512 + s1c * 128:
                                              512 + (s1c + 1) * 128],
                                         sexp[:, i * 512 + s1c * 128:
                                              i * 512 + (s1c + 1) * 128],
                                         svn[:, (sg * 4 + i) * 128:
                                             (sg * 4 + i + 1) * 128],
                                         start=(i == 0), stop=(i == s1c))
                            t.matmul(pbig[:, 1028 + s1c:1028 + s1c + 1],
                                     ssqT[:, sg * 512 + s1c * 128:
                                          sg * 512 + (s1c + 1) * 128],
                                     szbf[:], start=True, stop=True)
                            ins = t.matmul(pbig[:, 1032 + s1c:1032 + s1c + 1],
                                           sskT[:, sg * 512 + s1c * 128:
                                                sg * 512 + (s1c + 1) * 128],
                                           szbf[:], start=True, stop=True)
                        mm.inc(ins)

                    @block.vector
                    def _(v, sg=sg):
                        mm.wait(v)
                        ve.flush(v, v.reciprocal(srn[:], pbig[:, 1024:1036]))
                        ve.n -= 0

                    @block.vector
                    def _(v, sg=sg):
                        for s1c in range(4):
                            ve.inc(v.scalar_tensor_tensor(
                                sdlt[:, s1c * 128:(s1c + 1) * 128],
                                pbig[:, 1536 + s1c * 128:1536 + (s1c + 1) * 128],
                                srn[:, 8 + s1c:9 + s1c],
                                svn[:, (sg * 4 + s1c) * 128:
                                    (sg * 4 + s1c + 1) * 128],
                                op0=ALU.mult, op1=ALU.subtract))

                    @block.tensor
                    def _(t, sg=sg):
                        ve.wait(t)
                        for s1c in range(4):
                            t.matmul(ps0[:, 0:128],
                                     sskn[:, (sg * 4 + s1c) * 128:
                                          (sg * 4 + s1c + 1) * 128],
                                     sdlt[:, s1c * 128:(s1c + 1) * 128],
                                     start=(s1c == 0), stop=(s1c == 3))
                        ins = None
                        for s1c in range(4):
                            ins = t.matmul(ps0[:, 128:129],
                                           sskn[:, (sg * 4 + s1c) * 128:
                                                (sg * 4 + s1c + 1) * 128],
                                           ctx.ones_col_b[:],
                                           start=(s1c == 0), stop=(s1c == 3))
                        mm.inc(ins)

                    @block.vector
                    def _(v, sg=sg):
                        mm.wait(v)
                        v.tensor_sub(smem[:], smem[:], ps0[:, 0:128])
                        ve.inc(v.tensor_add(sz[:], sz[:], ps0[:, 128:129]))

                    @block.scalar
                    def _(s, sg=sg):
                        ve.wait(s)
                        s.copy(smembf[:], smem[:])
                        se.inc(s.copy(szbf[:], sz[:]))

                    for s1c in range(4):
                        @block.vector
                        def _(v, sg=sg, s1c=s1c, h=h):
                            v.tensor_scalar_mul(sam[:],
                                                pbig[:, s1c * 128:(s1c + 1) * 128],
                                                srn[:, 4 + s1c:5 + s1c])
                            ve.flush(v, v.tensor_scalar_mul(
                                sad[:],
                                pbig[:, 512 + s1c * 128:512 + (s1c + 1) * 128],
                                srn[:, s1c:s1c + 1]))
                            ve.flush(v, v.tensor_sub(sdf[:], sam[:], sad[:]))
                            ve.flush(v, v.tensor_mul(
                                sdf[:], sdf[:],
                                ctx.beta[:, h * 128:(h + 1) * 128]))
                            ve.inc(v.tensor_add(satt[:], sad[:], sdf[:]))

                        @block.tensor
                        def _(t, sg=sg, s1c=s1c):
                            ve.wait(t)
                            if s1c >= 1:
                                se.wait(t)
                            mm.inc(t.transpose(
                                ps1[:, 256 + (s1c % 2) * 128:
                                    256 + (s1c % 2) * 128 + 128],
                                satt[:], ctx.ident_f[:]))

                        @block.scalar
                        def _(s, sg=sg, s1c=s1c, h=h):
                            mm.wait(s)
                            if sg == 0 and s1c == 0:
                                s.wait_ge(st.h, attT_stored[0])
                            se.inc(s.copy(
                                sattT[:, sg * 512 + s1c * 128:
                                      sg * 512 + (s1c + 1) * 128],
                                ps1[:, 256 + (s1c % 2) * 128:
                                    256 + (s1c % 2) * 128 + 128]))

                @block.sync
                def _(sync, h=h):
                    se.wait(sync)
                    sync.dma_start(out=ctx.attt_scr[b, h * 128:(h + 1) * 128, :],
                                   in_=sattT[:]).then_inc(st.h, 16)
                    st.n += 16
                    attT_stored[0] = st.n


def _quarters(ctx, L, x_src, x_dst, inp):
    nc = ctx.nc
    ld, ve, se, mm, st = ctx.ld, ctx.ve, ctx.se, ctx.mm, ctx.st
    if L == 0:
        qt = 512
        FC, FF = FC0, F0
        NEXP = NE0
        dw_scr = ctx.cmb_scr
        att_scr = ctx.ctxt_scr
        wo = ctx.w["woT0"]
        experts = [(ctx.w["gw0"][e, :, :], ctx.w["uw0"][e, :, :],
                    ctx.w["dw0"][e, :, :], True) for e in range(NE0)]
    else:
        qt = 512
        FC, FF = IC1, I1
        NEXP = NE1
        dw_scr = ctx.dw1_scr
        att_scr = ctx.attt_scr
        wo = ctx.w["wo1"]
        experts = [(ctx.w["w1T"][e, :, :], ctx.w["w3T"][e, :, :],
                    ctx.w["w2T"][e, :, :], True) for e in range(NE1)]
        experts.append((ctx.w["ws1T"][:, :], ctx.w["ws3T"][:, :],
                        ctx.w["ws2T"][:, :], False))
    NTT = qt // 512

    _es = ExitStack()
    with _es:
        acc = _es.enter_context(nc.sbuf_tensor(nm("q_acc"), [128, DC * qt], F32))
        sxn = _es.enter_context(nc.sbuf_tensor(nm("q_xn"), [128, DC * qt], BF16))
        sdwT = _es.enter_context(nc.sbuf_tensor(nm("q_dwT"), [16, qt], F32))
        sdwb = _es.enter_context(nc.sbuf_tensor(nm("q_dwb"), [128, qt], F32))
        sh = _es.enter_context(nc.sbuf_tensor(nm("q_h"), [128, FC * qt], BF16))
        sgt = _es.enter_context(nc.sbuf_tensor(nm("q_gt"), [128, 512], BF16))
        sut = _es.enter_context(nc.sbuf_tensor(nm("q_ut"), [128, 512], BF16))
        sbo = _es.enter_context(nc.sbuf_tensor(nm("q_bo"), [128, DC], F32))
        sw1 = _es.enter_context(nc.sbuf_tensor(nm("q_w1"), [128, DC * FF], BF16))
        sw3 = _es.enter_context(nc.sbuf_tensor(nm("q_w3"), [128, DC * FF], BF16))
        sw2 = _es.enter_context(nc.sbuf_tensor(nm("q_w2"), [128, FC * D], BF16))
        pa = _es.enter_context(nc.psum_tensor(nm("q_pa"), [128, 512], F32))
        pb_ = _es.enter_context(nc.psum_tensor(nm("q_pb"), [128, 512], F32))
        pc = _es.enter_context(nc.psum_tensor(nm("q_pc"), [128, 512], F32))
        pd = _es.enter_context(nc.psum_tensor(nm("q_pd"), [128, 512], F32))
        pe5 = _es.enter_context(nc.psum_tensor(nm("q_pe5"), [128, 512], F32))
        pe6 = _es.enter_context(nc.psum_tensor(nm("q_pe6"), [128, 512], F32))
        pu_rot = [pa, pb_, pe5]
        pg_rot = [pc, pd, pe6]
        pdown_rot = [pa, pb_, pe5, pe6]
        pout_rot = [pa, pb_, pc, pd]
        if L == 0:
            with (
                nc.sbuf_tensor(nm("q_borow"), [1, D], F32) as sborow,
                nc.Block() as block,
            ):
                @block.sync
                def _(sync):
                    sync.dma_start(out=sborow[:], in_=inp["L0_attn_bo"][None, :]
                                   ).then_inc(ld.h, 16)
                    ld.n += 16
                _bias_cols(ctx, block, sborow, sbo, pa, 1)

        def load_w13(sync, e):
            w13a, w13b, _, _ = experts[e]
            sync.dma_start(out=chunk_view(sw1[:], FF),
                           in_=w13a.rearrange("(c p) n -> p c n", p=128)
                           ).then_inc(ld.h, 16)
            sync.dma_start(out=chunk_view(sw3[:], FF),
                           in_=w13b.rearrange("(c p) n -> p c n", p=128)
                           ).then_inc(ld.h, 16)
            ld.n += 32
            return ld.n

        for q in range(NB * S // qt):
            b = q // (S // qt)
            t0 = (q % (S // qt)) * qt
            w13_mark = [None]
            with nc.Block() as block:
                @block.sync
                def _(sync, b=b, t0=t0):
                    sync.wait_ge(mm.h, mm.n)
                    sync.dma_start(
                        out=chunk_view(sxn[:], qt),
                        in_=ctx.xnt_scr[b, :, t0:t0 + qt]
                        .rearrange("(c p) t -> p c t", p=128)).then_inc(ld.h, 16)
                    sync.dma_start(
                        out=sdwT[:NEXP, :],
                        in_=dw_scr[:, b * S + t0:b * S + t0 + qt]
                        ).then_inc(ld.h, 16)
                    ld.n += 32
                    w13_mark[0] = load_w13(sync, 0)

            with (
                nc.sbuf_tensor(nm("q_ctx"), [128, DC * qt], BF16) as sctx,
                nc.sbuf_tensor(nm("q_woA"), [128, DC * 128], BF16) as swoA,
                nc.sbuf_tensor(nm("q_woB"), [128, DC * 128], BF16) as swoB,
                nc.Block() as block,
            ):
                swo = [swoA, swoB]
                wo_loaded = []
                wo_marks = []
                ctx_mark = [None]
                acc_marks = [0, 0, 0, 0]

                @block.sync
                def _(sync, b=b, t0=t0):
                    sync.dma_start(
                        out=chunk_view(sctx[:], qt),
                        in_=att_scr[b, :, t0:t0 + qt]
                        .rearrange("(c p) t -> p c t", p=128)).then_inc(ld.h, 16)
                    ld.n += 16
                    ctx_mark[0] = ld.n

                for dc in range(DC):
                    @block.sync
                    def _(sync, dc=dc):
                        if dc >= 2:
                            sync.wait_ge(mm.h, wo_loaded[dc - 2])
                        sync.dma_start(
                            out=chunk_view(swo[dc % 2][:], 128),
                            in_=wo[:, dc * 128:(dc + 1) * 128]
                            .rearrange("(c p) j -> p c j", p=128)
                            ).then_inc(ld.h, 16)
                        ld.n += 16
                        wo_marks.append(ld.n)

                    @block.tensor
                    def _(t, dc=dc):
                        t.wait_ge(ld.h, max(wo_marks[dc], ctx_mark[0]))
                        t.wait_ge(ve.h, acc_marks[dc % 4])
                        for tt in range(NTT):
                            p = pout_rot[dc % 4]
                            ins = None
                            if L == 0:
                                for hc in range(DC):
                                    ins = t.matmul(
                                        p[:],
                                        swo[dc % 2][:, hc * 128:(hc + 1) * 128],
                                        sctx[:, hc * qt + tt * 512:
                                             hc * qt + (tt + 1) * 512],
                                        start=(hc == 0), stop=(hc == DC - 1))
                                mm.inc(ins)
                            else:
                                sv = sctx[:].rearrange(
                                    "p (c m s) -> p c m s", m=32, s=16)
                                for sl in range(16):
                                    ins = t.matmul(
                                        p[:],
                                        swo[dc % 2][:, sl * 128:(sl + 1) * 128],
                                        sv[:, :, :, sl],
                                        start=(sl == 0), stop=(sl == 15))
                                mm.inc(ins)
                        wo_loaded.append(mm.n)

                    @block.vector
                    def _(v, dc=dc):
                        for tt in range(NTT):
                            p = pout_rot[dc % 4]
                            v.wait_ge(mm.h, mm.n - (NTT - 1) + tt)
                            dst = acc[:, dc * qt + tt * 512:
                                      dc * qt + (tt + 1) * 512]
                            if L == 0:
                                vi = v.tensor_scalar(dst, p[:],
                                                     sbo[:, dc:dc + 1], None,
                                                     op0=ALU.add)
                            else:
                                vi = v.tensor_copy(dst, p[:])
                            ve.inc(vi)
                        acc_marks[dc % 4] = ve.n

            if getattr(ctx, "dbg_acc", None) is not None and q == 0:
                with nc.Block() as block:
                    @block.sync
                    def _(sync):
                        sync.wait_ge(ve.h, ve.n)
                        sync.dma_start(out=ctx.dbg_acc[:, :qt * DC],
                                       in_=acc[:]).then_inc(st.h, 16)
                        st.n += 16
                        st.wait(sync)

            for ei, (w13a, w13b, w2, has_dw) in enumerate(
                    [] if DBG_SKIP_EXPERTS else experts):
                w2_mark = [None]
                with nc.Block() as block:
                    @block.sync
                    def _(sync, ei=ei):
                        sync.wait_ge(mm.h, mm.n)
                        sync.dma_start(
                            out=chunk_view(sw2[:], D),
                            in_=w2.rearrange("(c p) n -> p c n", p=128)
                            ).then_inc(ld.h, 16)
                        ld.n += 16
                        w2_mark[0] = ld.n

                    if has_dw:
                        @block.tensor
                        def _(t, ei=ei):
                            se.wait(t)
                            for tt in range(NTT):
                                mm.inc(t.matmul(
                                    (pc if tt == 0 else pd)[:],
                                    ctx.sel[:NEXP, ei * 128:(ei + 1) * 128],
                                    sdwT[:NEXP, tt * 512:(tt + 1) * 512],
                                    start=True, stop=True))

                        @block.scalar
                        def _(s, ei=ei):
                            base = mm.n - NTT
                            for tt in range(NTT):
                                s.wait_ge(mm.h, base + tt + 1)
                                se.inc(s.copy(
                                    sdwb[:, tt * 512:(tt + 1) * 512],
                                    (pc if tt == 0 else pd)[:]))

                    gu_done = {}
                    for tt in range(NTT):
                        for fc in range(FC):
                            @block.tensor
                            def _(t, tt=tt, fc=fc):
                                t.wait_ge(ld.h, w13_mark[0])
                                if has_dw:
                                    se.wait(t)
                                pu = pu_rot[fc % 3]
                                pg = pg_rot[fc % 3]
                                gi = tt * FC + fc
                                if gi >= 3:
                                    pi = gi - 3
                                    prev = (pi // FC, pi % FC)
                                    t.wait_ge(ve.h, gu_done[prev][0])
                                    t.wait_ge(se.h, gu_done[prev][1])
                                ins = None
                                for kc in range(DC):
                                    ins = t.matmul(
                                        pu[:],
                                        sw3[:, kc * FF + fc * 128:
                                            kc * FF + (fc + 1) * 128],
                                        sxn[:, kc * qt + tt * 512:
                                            kc * qt + (tt + 1) * 512],
                                        start=(kc == 0), stop=(kc == DC - 1))
                                mm.inc(ins)
                                ins = None
                                for kc in range(DC):
                                    ins = t.matmul(
                                        pg[:],
                                        sw1[:, kc * FF + fc * 128:
                                            kc * FF + (fc + 1) * 128],
                                        sxn[:, kc * qt + tt * 512:
                                            kc * qt + (tt + 1) * 512],
                                        start=(kc == 0), stop=(kc == DC - 1))
                                mm.inc(ins)

                            @block.vector
                            def _(v, tt=tt, fc=fc):
                                pu = pu_rot[fc % 3]
                                v.wait_ge(mm.h, mm.n - 1)
                                if has_dw:
                                    vi = v.tensor_mul(
                                        sut[:], pu[:],
                                        sdwb[:, tt * 512:(tt + 1) * 512])
                                else:
                                    vi = v.tensor_copy(sut[:], pu[:])
                                ve.flush(v, vi)

                            @block.scalar
                            def _(s, tt=tt, fc=fc):
                                pg = pg_rot[fc % 3]
                                s.wait_ge(mm.h, mm.n)
                                prev1 = ((tt, fc - 1) if fc >= 1
                                         else (tt - 1, FC - 1))
                                if prev1 in gu_done:
                                    s.wait_ge(ve.h, gu_done[prev1][0])
                                se.inc(s.activation(sgt[:], pg[:], AF.Silu))

                            @block.vector
                            def _(v, tt=tt, fc=fc):
                                se.wait(v)
                                ve.inc(v.tensor_mul(
                                    sh[:, fc * qt + tt * 512:
                                       fc * qt + (tt + 1) * 512],
                                    sgt[:], sut[:]))
                                gu_done[(tt, fc)] = (ve.n, se.n)

                    @block.sync
                    def _(sync, ei=ei):
                        if ei + 1 < len(experts):
                            sync.wait_ge(mm.h, mm.n)
                            w13_mark[0] = load_w13(sync, ei + 1)

                    down_evac = {}
                    gu_ve_done = ve.n
                    for dc in range(DC):
                        for tt in range(NTT):
                            @block.tensor
                            def _(t, dc=dc, tt=tt):
                                t.wait_ge(ld.h, w2_mark[0])
                                idx = dc * NTT + tt
                                p = pdown_rot[idx % 4]
                                if idx >= 4:
                                    t.wait_ge(ve.h, down_evac[idx - 4])
                                else:
                                    t.wait_ge(ve.h, gu_ve_done)
                                ins = None
                                for fc in range(FC):
                                    ins = t.matmul(
                                        p[:],
                                        sw2[:, fc * D + dc * 128:
                                            fc * D + (dc + 1) * 128],
                                        sh[:, fc * qt + tt * 512:
                                           fc * qt + (tt + 1) * 512],
                                        start=(fc == 0), stop=(fc == FC - 1))
                                mm.inc(ins)

                            @block.vector
                            def _(v, dc=dc, tt=tt):
                                idx = dc * NTT + tt
                                p = pdown_rot[idx % 4]
                                mm.wait(v)
                                sl = acc[:, dc * qt + tt * 512:
                                         dc * qt + (tt + 1) * 512]
                                ve.inc(v.tensor_add(sl, sl, p[:]))
                                down_evac[idx] = ve.n

            with (
                nc.sbuf_tensor(nm("q_x0"), [128, D], F32) as sx0,
                nc.sbuf_tensor(nm("q_x1"), [128, D], F32) as sx1,
                nc.sbuf_tensor(nm("q_o0"), [128, D], F32) as so0,
                nc.sbuf_tensor(nm("q_o1"), [128, D], F32) as so1,
                nc.Block() as block,
            ):
                sx = [sx0, sx1]
                so = [so0, so1]
                ep_done = [0, 0]
                for ts in range(qt // 128):
                    @block.sync
                    def _(sync, ts=ts, b=b, t0=t0):
                        sync.wait_ge(st.h, ep_done[ts % 2])
                        sync.dma_start(out=sx[ts % 2][:],
                                       in_=x_src[b, t0 + ts * 128:
                                                 t0 + (ts + 1) * 128, :]
                                       ).then_inc(ld.h, 16)
                        ld.n += 16

                    @block.tensor
                    def _(t, ts=ts):
                        ve.wait(t)
                        se.wait(t)
                        ins = None
                        for dc in range(DC):
                            ins = t.transpose(
                                pout_rot[dc // 4][:, (dc % 4) * 128:
                                                  (dc % 4) * 128 + 128],
                                acc[:, dc * qt + ts * 128:
                                    dc * qt + (ts + 1) * 128],
                                ctx.ident_f[:])
                        mm.inc(ins)

                    @block.vector
                    def _(v, ts=ts):
                        mm.wait(v)
                        ld.wait(v)
                        for c4 in range(4):
                            ve.inc(v.tensor_add(
                                so[ts % 2][:, c4 * 512:(c4 + 1) * 512],
                                sx[ts % 2][:, c4 * 512:(c4 + 1) * 512],
                                pout_rot[c4][:]))

                    @block.sync
                    def _(sync, ts=ts, b=b, t0=t0):
                        ve.wait(sync)
                        sync.dma_start(out=x_dst[b, t0 + ts * 128:
                                                 t0 + (ts + 1) * 128, :],
                                       in_=so[ts % 2][:]).then_inc(st.h, 16)
                        st.n += 16
                        ep_done[ts % 2] = st.n

                @block.sync
                def _(sync):
                    st.wait(sync)


def build_program(layers, dbg=()):
    nc = bass.Bass("TRN2", target_bir_lowering=False)
    es = ExitStack()
    with es:
        inp = {}
        if 0 in layers:
            inp["L0_attn_x"] = nc.dram_tensor("L0_attn_x", [NB, S, D], F32,
                                              kind="ExternalInput")
            names = list(L0_WNAMES)
        else:
            inp["x1_in"] = nc.dram_tensor("x1_in", [NB, S, D], F32,
                                          kind="ExternalInput")
            names = []
        if 1 in layers:
            names += L1_WNAMES
        for n in names:
            inp[n] = nc.dram_tensor(n, INPUT_SHAPES[n], F32, kind="ExternalInput")
        out = nc.dram_tensor("out", [NB, S, D], F32, kind="ExternalOutput")

        ctx = _mk_ctx(nc, es, layers)
        ctx.inp = inp
        ctx.xnt_scr = nc.dram_tensor("xnt_scr", [NB, D, S], BF16, kind="Internal")
        if 0 in layers:
            ctx.ctxt_scr = nc.dram_tensor("ctxt_scr", [NB, D, S], BF16,
                                          kind="Internal")
            ctx.cmb_scr = nc.dram_tensor("cmb_scr", [NE0, NB * S], F32,
                                         kind="Internal")
        if 1 in layers:
            ctx.attt_scr = nc.dram_tensor("attt_scr", [NB, D, S], BF16,
                                          kind="Internal")
            ctx.dw1_scr = nc.dram_tensor("dw1_scr", [NE1, NB * S], F32,
                                         kind="Internal")
        if layers == [0, 1]:
            ctx.x1_scr = nc.dram_tensor("x1_scr", [NB, S, D], F32, kind="Internal")
        elif layers == [0]:
            ctx.x1_scr = out
        else:
            ctx.x1_scr = inp["x1_in"]

        _consts(ctx)
        _prep_weights(ctx, inp)

        for L in layers:
            x_src = inp["L0_attn_x"] if L == 0 else ctx.x1_scr
            x_dst = ctx.x1_scr if L == 0 else out
            if layers == [0]:
                x_dst = out
            with ExitStack() as _les:
                ctx.xnT = _les.enter_context(
                    nc.sbuf_tensor(nm("xnT"), [128, DC * S], BF16))
                if L == 1:
                    ctx.beta = _les.enter_context(
                        nc.sbuf_tensor(nm("beta"), [128, NH * 128], F32))
                for b in range(NB):
                    _ln(ctx, b, x_src)
                    if L == 0:
                        _router_l0(ctx, b)
                        _attn_l0(ctx, b, inp)
                    else:
                        _gate_l1(ctx, b, inp)
                        _attn_l1(ctx, b, inp)
            _quarters(ctx, L, x_src, x_dst, inp)

        if dbg:
            dbg_outs = {}
            for name in dbg:
                src = getattr(ctx, name)
                dbg_outs[name] = nc.dram_tensor(
                    "dbg_" + name, list(src.shape), src.dtype,
                    kind="ExternalOutput")
            with nc.Block() as block:
                @block.sync
                def _(sync):
                    for name in dbg:
                        src = getattr(ctx, name)
                        idx = tuple(slice(None) for _ in src.shape)
                        sync.dma_start(out=dbg_outs[name][idx],
                                       in_=src[idx]).then_inc(ctx.st.h, 16)
                        ctx.st.n += 16
                    ctx.st.wait(sync)
    return nc


FUSED = True


def kernel(**inputs):
    xs = np.split(np.ascontiguousarray(inputs["L0_attn_x"]), NCORES, axis=0)
    w = {k: np.ascontiguousarray(v) for k, v in inputs.items()
         if k != "L0_attn_x"}

    if FUSED:
        nc = build_program([0, 1])
        in_maps = [{**w, "L0_attn_x": xs[i]} for i in range(NCORES)]
        res = run_bass_kernel_spmd(nc, in_maps, core_ids=list(range(NCORES)))
        return np.concatenate([res.results[i]["out"] for i in range(NCORES)],
                              axis=0)

    ncA = build_program([0])
    in_maps = [{**{k: w[k] for k in L0_WNAMES}, "L0_attn_x": xs[i]}
               for i in range(NCORES)]
    resA = run_bass_kernel_spmd(ncA, in_maps, core_ids=list(range(NCORES)))
    x1s = [resA.results[i]["out"] for i in range(NCORES)]

    ncB = build_program([1])
    in_maps = [{**{k: w[k] for k in L1_WNAMES}, "x1_in": x1s[i]}
               for i in range(NCORES)]
    resB = run_bass_kernel_spmd(ncB, in_maps, core_ids=list(range(NCORES)))
    return np.concatenate([resB.results[i]["out"] for i in range(NCORES)],
                          axis=0)
